# version 01
# baseline (speedup 1.0000x reference)
import numpy as np
import ml_dtypes
import concourse.bass as bass
import concourse.mybir as mybir
from concourse.bass_utils import run_bass_kernel_spmd

F32 = mybir.dt.float32
BF16 = mybir.dt.bfloat16
F16 = mybir.dt.float16
I32 = mybir.dt.int32
U32 = mybir.dt.uint32
U16 = mybir.dt.uint16
AF = mybir.ActivationFunctionType
ALU = mybir.AluOpType
AX = mybir.AxisListType

P = 128
DM = 2048
DIH = 2048
NST = 16
RK = 128
IH = 2816
GU = 2 * IH
DMH = 1024
LN_EPS = 1e-5
RMS_EPS = 1e-6
DBLK = DIH // P
MBLK = DM // P
OBLK = GU // P
IBLK2 = 2 * IH // P
WOBLK = 4096 // P

ENGS = ("sync", "vector", "scalar", "tensor", "gpsimd")


class Op:
    __slots__ = ("eng", "idx", "sem_val", "dma", "pend_inc", "label", "slot", "slot_val")

    def __init__(self, eng, idx, dma):
        self.eng = eng
        self.idx = idx
        self.dma = dma
        self.sem_val = None
        self.pend_inc = 0


class Buf:
    __slots__ = ("ap", "wop", "rops", "name")

    def __init__(self, ap, name=""):
        self.ap = ap
        self.wop = None
        self.rops = []
        self.name = name


class K:

    NSLOT = 8

    def __init__(self, nc):
        self.nc = nc
        self.ops = {e: [] for e in ENGS}
        self.clos = {e: [] for e in ENGS}
        self.milestones = {e: 0 for e in ENGS}
        self.dma_emitted = {e: 0 for e in ENGS}
        self.slot_cnt = {e: [0] * self.NSLOT for e in ENGS}
        self.waited = {e: {} for e in ENGS}
        self.sems = {}

    def _milestone(self, op):
        if op.dma:
            return (op.eng, "d", op.slot), op.slot_val
        if op.sem_val is None:
            last = self.ops[op.eng][-1]
            while last.dma:
                for o in reversed(self.ops[op.eng]):
                    if not o.dma and o.idx >= op.idx:
                        last = o
                        break
                else:
                    last = None
                break
            if last is None:
                lastd = self.ops[op.eng][-1]
                return (op.eng, "d", lastd.slot), lastd.slot_val
            if last.sem_val is None:
                self.milestones[op.eng] += 1
                last.sem_val = self.milestones[op.eng]
                last.pend_inc += 1
            return (op.eng, "c"), last.sem_val
        return (op.eng, "c"), op.sem_val

    def _wait(self, eng, key, val):
        if self.waited[eng].get(key, 0) >= val:
            return
        self.waited[eng][key] = val
        self.clos[eng].append(("wait", key, val))

    def _dep(self, eng, op):
        key, val = self._milestone(op)
        self._wait(eng, key, val)

    def emit(self, eng, fn, reads=(), writes=(), dma=False, label=""):
        for b in reads:
            if b.wop is not None:
                self._dep(eng, b.wop)
        for b in writes:
            if b.wop is not None and b.wop.eng != eng:
                self._dep(eng, b.wop)
            for r in b.rops:
                if r.eng != eng:
                    self._dep(eng, r)
        op = Op(eng, len(self.ops[eng]), dma)
        op.label = label
        if dma:
            i = self.dma_emitted[eng]
            slot = i % self.NSLOT
            if self.slot_cnt[eng][slot] > 0:
                self._wait(eng, (eng, "d", slot), self.slot_cnt[eng][slot])
            self.slot_cnt[eng][slot] += 16
            op.slot = slot
            op.slot_val = self.slot_cnt[eng][slot]
            self.dma_emitted[eng] += 1
        self.ops[eng].append(op)
        self.clos[eng].append(("op", op, fn))
        for b in reads:
            b.rops.append(op)
        for b in writes:
            b.wop = op
            b.rops = []
        return op

    def dma(self, eng, out, in_, rbufs=(), wbufs=(), **kw):
        reads = list(rbufs)
        writes = list(wbufs)
        if isinstance(in_, Buf):
            reads.append(in_)
            in_ap = in_.ap
        else:
            in_ap = in_
        if isinstance(out, Buf):
            writes.append(out)
            out_ap = out.ap
        else:
            out_ap = out
        return self.emit(eng, lambda e: e.dma_start(out=out_ap, in_=in_ap, **kw),
                         reads=reads, writes=writes, dma=True)

    def barrier(self):
        marks = []
        for en in ENGS:
            for s in range(self.NSLOT):
                if self.slot_cnt[en][s]:
                    marks.append(((en, "d", s), self.slot_cnt[en][s]))
            for o in reversed(self.ops[en]):
                if not o.dma:
                    key, val = self._milestone(o)
                    marks.append((key, val))
                    break
        for en in ENGS:
            for key, val in marks:
                if key[0] != en:
                    self._wait(en, key, val)
                elif len(key) == 3:
                    self._wait(en, key, val)

    def flush(self, block):
        makers = {"sync": block.sync, "vector": block.vector,
                  "scalar": block.scalar, "tensor": block.tensor,
                  "gpsimd": block.gpsimd}
        for en in ENGS:
            items = self.clos[en]
            if not items:
                continue

            def body(e, items=items, en=en):
                for it in items:
                    if it[0] == "wait":
                        _, key, val = it
                        e.wait_ge(self.sems[key], val)
                    else:
                        _, op, fn = it
                        inst = fn(e)
                        if op.dma:
                            inst.then_inc(self.sems[(en, "d", op.slot)], 16)
                        if op.pend_inc:
                            inst.then_inc(self.sems[(en, "c")], op.pend_inc)
            makers[en](body)


def build_program(t_tokens=8192, tc=512, taps=()):
    nch = t_tokens // tc
    nbig = 4
    cpb = nch // nbig
    tb = cpb * tc
    nc = bass.Bass("TRN2", target_bir_lowering=False)

    def din(name, shape, dt=F32):
        return nc.dram_tensor(name, shape, dt, kind="ExternalInput").ap()

    x_full = din("x_full", [t_tokens, DM])
    x_halfi = din("x_halfi", [t_tokens, DMH])
    w_in_x_T = din("w_in_x_T", [DM, DIH])
    w_in_z_T = din("w_in_z_T", [DM, DIH])
    conv_w = din("conv_w", [DIH, 4])
    conv_b = din("conv_b", [DIH, 1])
    w_xp_T = din("w_xp_T", [DIH, 160])
    w_dt_T = din("w_dt_T", [RK, DIH])
    dt_b = din("dt_b", [DIH, 1])
    a_log = din("a_log", [DIH, NST])
    d_vec = din("d_vec", [DIH, 1])
    w_out_T = din("w_out_T", [4096, DMH])
    gu_packed = din("gu_packed", [GU, DM // 8], I32)
    gu_scales_T = din("gu_scales_T", [GU, DM // 128], U16)
    dn_packed = din("dn_packed", [DMH, 2 * IH // 8], I32)
    dn_scales_T = din("dn_scales_T", [DMH, 2 * IH // 128], U16)
    j_in = din("j_in", [1, 1], U32)
    nonce_in = din("nonce_in", [1, 4], U32)
    ident_in = din("ident", [128, 128], BF16)
    constv_in = din("constv", [128, 3])

    out_half = nc.dram_tensor("out_half", [t_tokens, DMH], F32, kind="ExternalOutput").ap()

    def dloc(name, shape, dt):
        return Buf(nc.dram_tensor(name, shape, dt, kind="Internal").ap(), name)

    xn_dr = dloc("xn_dr", [DM, t_tokens], BF16)
    xt_dr = dloc("xt_dr", [DIH, t_tokens], F32)
    zs_dr = dloc("zs_dr", [DIH, t_tokens], F32)
    wout_dr = dloc("wout_dr", [WOBLK, 128, DMH], BF16)
    wgu_dr = dloc("wgu_dr", [MBLK, 128, GU], BF16)
    wd_dr = dloc("wd_dr", [IBLK2, 128, DMH], BF16)
    yffn_dr = dloc("yffn_dr", [t_tokens, DMH], F32)

    def dsh(name, shape, dt):
        return nc.dram_tensor(name, shape, dt, kind="Internal", addr_space="Shared").ap()

    projsh = [dsh(f"projsh{k}", [160, t_tokens], F32) for k in range(2)]
    ysh = [dsh(f"ysh{k}", [DIH, t_tokens], BF16) for k in range(2)]
    hsh = [dsh(f"hsh{k}", [IH, t_tokens], BF16) for k in range(2)]
    sqsh = [dsh(f"sqsh{k}", [t_tokens // P, P], F32) for k in range(2)]
    NFLAG = 2 * nch + 2 * nbig
    flagsh = [dsh(f"flagsh{k}", [NFLAG, 16], U32) for k in range(2)]
    barsh = [dsh(f"barsh{k}", [1, 4], U32) for k in range(2)]
    projsh_b = Buf(None, "projsh")
    ysh_b = Buf(None, "ysh")
    hsh_b = Buf(None, "hsh")
    sqsh_b = Buf(None, "sqsh")

    tap_out = {}
    tap_shapes = {
        "xn": ([DM, t_tokens], BF16), "xt": ([DIH, t_tokens], F32),
        "zs": ([DIH, t_tokens], F32), "proj": ([160, t_tokens], F32),
        "dt": ([DIH, t_tokens], F32), "y": ([DIH, t_tokens], F32),
        "wgu": ([MBLK, 128, GU], BF16), "wd": ([IBLK2, 128, DMH], BF16),
        "s": ([t_tokens, DMH], F32), "yffn": ([t_tokens, DMH], F32),
        "dqx": ([128, DM], BF16), "dqn": ([128, DM], U16),
        "hh": ([2, IH, t_tokens], BF16), "yy": ([2, DIH, t_tokens], BF16),
        "wo": ([WOBLK, 128, DMH], BF16),
    }
    for t in taps:
        sh, dt = tap_shapes[t]
        tap_out[t] = nc.dram_tensor("tap_" + t, sh, dt, kind="ExternalOutput").ap()

    def sbraw(name, shape, dt):
        return nc.alloc_sbuf_tensor(name, shape, dt).ap()

    def sb(name, shape, dt):
        return Buf(sbraw(name, shape, dt), name)

    ident_s = sb("ident_s", [128, 128], BF16)
    j_s = sb("j_s", [1, 1], U32)
    nonce_s = sb("nonce_s", [1, 4], U32)
    bar_s = sb("bar_s", [1, 4], U32)
    zero16 = sb("zero16", [1, 16], U32)
    one16 = sb("one16", [1, 16], U32)
    flag_s = sb("flag_s", [1, 1], U32)
    ones_bf = sb("ones_bf", [1, 128], BF16)
    ones_f = sb("ones_f", [128, 128], F32)

    a_sc = [sb(f"a_sc{i}", [P, NST], F32) for i in range(DBLK)]
    cw_s = [sb(f"cw_s{i}", [P, 4], F32) for i in range(DBLK)]
    cb_s = [sb(f"cb_s{i}", [P, 1], F32) for i in range(DBLK)]
    ncb_s = [sb(f"ncb_s{i}", [P, 1], F32) for i in range(DBLK)]
    dtb_s = [sb(f"dtb_s{i}", [P, 1], F32) for i in range(DBLK)]
    dvec_s = [sb(f"dvec_s{i}", [P, 1], F32) for i in range(DBLK)]
    hstate = [sb(f"hstate{i}", [P, NST], F32) for i in range(DBLK)]
    halo = [sb(f"halo{i}", [P, 3], F32) for i in range(DBLK)]
    wdt_bf = sb("wdt_bf", [RK, DIH], BF16)
    wxp_stg = sb("wxp_stg", [P, 160], F32)
    wxp_bf = [sb(f"wxp_bf{i}", [P, 160], BF16) for i in range(DBLK)]
    wstg = sb("wstg", [P, DIH], F32)
    ln_m = sb("ln_m", [P, 1], F32)
    ln_ssq = sb("ln_ssq", [P, 1], F32)
    ln_rs = sb("ln_rs", [P, 1], F32)
    sq_p = sb("sq_p", [P, 1], F32)
    sq0 = sb("sq0", [P, 1], F32)
    sq1 = sb("sq1", [P, 1], F32)
    rstd3 = sb("rstd3", [P, 1], F32)
    constv = sb("constv_s", [P, 3], F32)

    ARENA_HW = 94000
    arena_ap = sbraw("arena", [P, ARENA_HW], BF16)
    _dt_bytes = {F32: 4, BF16: 2, F16: 2, I32: 4, U16: 2, U32: 4}

    class Arena:
        def __init__(self):
            self.cur = 0

        def reset(self):
            self.cur = 0

        def al(self, name, cols, dt, parts=P):
            b = _dt_bytes[dt]
            off = (self.cur + 3) & ~3
            w_hw = (cols * b + 1) // 2
            assert off + w_hw * 2 <= ARENA_HW * 2, f"arena overflow at {name}"
            view = arena_ap[:parts, off // 2: off // 2 + w_hw]
            self.cur = off + w_hw * 2
            if dt != BF16:
                view = view.bitcast(dt)
                if view.shape[1] != cols:
                    view = view[:, :cols]
            else:
                view = view[:, :cols]
            return Buf(view, name)

    AR = Arena()

    wstg2 = AR.al("wstg2", DMH, F32)
    wo_bf = AR.al("wo_bf", DMH, BF16)
    DQCH_GU = 1024
    DQCH_DN = 1408
    DQCH = 1408
    dq_pk = AR.al("dq_pk", 704, I32)
    dq_nib = AR.al("dq_nib", DQCH, U16)
    dq_t1 = AR.al("dq_t1", DQCH, U16)
    dq_t2 = AR.al("dq_t2", DQCH, U16)
    dq_t3 = AR.al("dq_t3", DQCH, U16)
    dq_sc16 = AR.al("dq_sc16", 44, U16)
    dq_sc = AR.al("dq_sc", 44, F32)
    dq_srep = AR.al("dq_srep", DQCH, F32)
    dq_out = AR.al("dq_out", DQCH, BF16)
    dq_tpo = [AR.al(f"dq_tpo{i}", 128, BF16) for i in range(2)]

    AR.reset()
    win_sb = [AR.al(f"win{i}", DIH, BF16) for i in range(MBLK)]
    xtok = [AR.al(f"xtok{i}", DM, F32) for i in range(2)]
    xc_t = [AR.al(f"xc_t{i}", DM, F32) for i in range(2)]
    xn_tok = [AR.al(f"xn_tok{i}", DM, BF16) for i in range(2)]
    ln_sq = AR.al("ln_sq", DM, F32)
    xn_fm = [AR.al(f"xn_fm{i}", tc, BF16) for i in range(MBLK)]
    xl_t = [AR.al(f"xl_t{i}", tc + 3, F32) for i in range(2)]
    cv_t = [AR.al(f"cv_t{i}", tc, F32) for i in range(2)]
    sil_e = [AR.al(f"sil_e{i}", tc, F32) for i in range(2)]
    sil_r = [AR.al(f"sil_r{i}", tc, F32) for i in range(2)]
    xt_t = [AR.al(f"xt_t{i}", tc, F32) for i in range(2)]
    xt_bf = [AR.al(f"xt_bf{i}", tc, BF16) for i in range(2)]
    z_t = [AR.al(f"z_t{i}", tc, F32) for i in range(2)]
    zs_t = [AR.al(f"zs_t{i}", tc, F32) for i in range(2)]
    pj1_s = AR.al("pj1_s", tc, F32)
    pj2_s = AR.al("pj2_s", tc, F32, parts=32)

    AR.reset()
    pj1a = AR.al("pj1a", tc, F32)
    pj1b = AR.al("pj1b", tc, F32)
    pj1bf = AR.al("pj1bf", tc, BF16)
    pj2a = AR.al("pj2a", tc, F32, parts=32)
    pj2b = AR.al("pj2b", tc, F32, parts=32)
    pjbc_bf = AR.al("pjbc_bf", tc, BF16, parts=32)
    rowstg = [AR.al(f"rowstg{i}", tc, BF16, parts=1) for i in range(2)]
    bb_t = [AR.al(f"bb_t{i}", tc, F32) for i in range(NST)]
    cb_t = [AR.al(f"cb_t{i}", tc, F32) for i in range(NST)]
    dt_s = AR.al("dt_s", tc, F32)
    sp_r = AR.al("sp_r", tc, F32)
    sp_a = AR.al("sp_a", tc, F32)
    w_s = AR.al("w_s", tc, F32)
    da_t = [AR.al(f"da_t{i}", tc, F32) for i in range(2)]
    wb_t = [AR.al(f"wb_t{i}", tc, F32) for i in range(2)]
    h_slab = AR.al("h_slab", 8 * tc, F32)
    hc_slab = AR.al("hc_slab", 8 * tc, F32)
    ypart = [AR.al(f"ypart{i}", tc, F32) for i in range(2)]
    y_s = AR.al("y_s", tc, F32)
    yg_slab = AR.al("yg_slab", DBLK * tc, BF16)
    xt_rd = AR.al("xt_rd", tc, F32)
    zs_rd = AR.al("zs_rd", tc, F32)

    AR.reset()
    xn3 = [AR.al(f"xn3_{i}", tb, BF16) for i in range(MBLK)]
    ph3_lo_hw = AR.cur
    AR.reset()
    wd_sb = [AR.al(f"wd{i}", DMH, BF16) for i in range(IBLK2)]
    AR.reset()
    wout_sb = [AR.al(f"wout{i}", DMH, BF16) for i in range(WOBLK)]
    AR.cur = max(ph3_lo_hw, IBLK2 * 2048 + 64, WOBLK * 2048 + 64, AR.cur)
    g_e = AR.al("g_e", 512, F32)
    g_cp = AR.al("g_cp", 512, F32)
    g_r = AR.al("g_r", 512, F32)
    g_sg = AR.al("g_sg", 512, F32)
    u_cp = AR.al("u_cp", 512, F32)
    h_bf = [AR.al(f"h_bf{i}", 512, BF16) for i in range(2)]
    wgu_t = [AR.al(f"wgu_t{i}", 128, BF16) for i in range(2 * MBLK)]
    h_ld = [AR.al(f"h_ld{i}", 128, BF16) for i in range(4)]
    yf_t = [AR.al(f"yf_t{i}", DMH, F32) for i in range(4)]
    sqsc = AR.al("sqsc", DMH, F32)
    y_ld = [AR.al(f"y_ld{i}", 128, BF16) for i in range(4)]
    s_ev = [AR.al(f"s_ev{i}", DMH, F32) for i in range(2)]
    yfr = AR.al("yfr", DMH, F32)
    g3e = AR.al("g3e", DMH, F32)
    g3r = AR.al("g3r", DMH, F32)
    xh_t = AR.al("xh_t", DMH, F32)
    o_t = [AR.al(f"o_t{i}", DMH, F32) for i in range(2)]

    import contextlib
    with contextlib.ExitStack() as _stack:
        ent = _stack.enter_context
        cs_sync = ent(nc.semaphore("cs_sync")); cs_vec = ent(nc.semaphore("cs_vec"))
        cs_sca = ent(nc.semaphore("cs_sca")); cs_ten = ent(nc.semaphore("cs_ten"))
        cs_gps = ent(nc.semaphore("cs_gps"))
        sem_poll = ent(nc.semaphore("s_poll")); sem_gq = ent(nc.semaphore("s_gq"))
        dpool = {}
        for qe in ("sync", "gpsimd"):
            for s in range(K.NSLOT):
                dpool[(qe, s)] = ent(nc.semaphore(f"dq_{qe}_{s}"))
        ps0_h = ent(nc.psum_tensor("ps0", [P, 512], F32))
        ps1_h = ent(nc.psum_tensor("ps1", [P, 512], F32))
        ps2_h = ent(nc.psum_tensor("ps2", [P, 512], F32))
        ps3_h = ent(nc.psum_tensor("ps3", [P, 512], F32))
        ps4_h = ent(nc.psum_tensor("ps4", [P, 512], F32))
        ps5_h = ent(nc.psum_tensor("ps5", [P, 512], F32))
        pst0_h = ent(nc.psum_tensor("pst0", [P, 128], BF16))
        pst1_h = ent(nc.psum_tensor("pst1", [P, 128], BF16))
        block = ent(nc.Block())
        kk = K(nc)
        for en, ch in (("sync", cs_sync), ("vector", cs_vec),
                       ("scalar", cs_sca), ("tensor", cs_ten),
                       ("gpsimd", cs_gps)):
            kk.sems[(en, "c")] = ch
        for (qe, s), h in dpool.items():
            kk.sems[(qe, "d", s)] = h
        psb = [Buf(h.ap(), f"ps{i}") for i, h in enumerate(
            [ps0_h, ps1_h, ps2_h, ps3_h, ps4_h, ps5_h])]
        pstb = [Buf(h.ap(), f"pst{i}") for i, h in enumerate([pst0_h, pst1_h])]

        jr = nc.gpsimd.alloc_register("jr")
        r_fl = nc.gpsimd.alloc_register("r_fl")
        r_nd = nc.gpsimd.alloc_register("r_nd")
        r_a = nc.gpsimd.alloc_register("r_a")
        r_b = nc.gpsimd.alloc_register("r_b")
        r_pc = nc.gpsimd.alloc_register("r_pc")
        gqn = [0]

        def g_branch_write(dst0_ap, dst1_ap, src_ap):
            def fn(g):
                with g.If_eq(jr, 0):
                    g.dma_start(out=dst0_ap, in_=src_ap).then_inc(sem_gq, 16)
                with g.Else():
                    g.dma_start(out=dst1_ap, in_=src_ap).then_inc(sem_gq, 16)
                gqn[0] += 16
                g.wait_ge(sem_gq, gqn[0])
                return g.engine_nop()
            return fn

        def g_flag(fid):
            return g_branch_write(flagsh[0][fid:fid + 1, :],
                                  flagsh[1][fid:fid + 1, :], one16.ap)

        def g_poll(fid):
            def fn(g):
                for k in range(2):
                    g.reg_mov(r_nd, 1)
                    with g.While(r_nd):
                        g.reg_add(r_pc, r_pc, 16)
                        g.dma_start(out=flag_s.ap, in_=flagsh[k][fid:fid + 1, 0:1]).then_inc(sem_poll, 16)
                        g.wait_ge(sem_poll, r_pc)
                        g.reg_load(r_fl, flag_s.ap[:1, :1])
                        g.reg_alu(r_nd, r_fl, 0, ALU.is_equal)
                return g.engine_nop()
            return fn

        nc.const_aps.register(F32, LN_EPS, constv.ap[:, 0:1])
        nc.const_aps.register(F32, RMS_EPS, constv.ap[:, 1:2])
        nc.const_aps.register(F32, 1.0, constv.ap[:, 2:3])
        kk.dma("sync", constv, constv_in)
        kk.dma("sync", ident_s, ident_in)
        kk.dma("sync", j_s, j_in)
        kk.dma("sync", nonce_s, nonce_in)
        for i in range(DBLK):
            kk.dma("sync", a_sc[i], a_log[i * P:(i + 1) * P, :])
            kk.dma("sync", cw_s[i], conv_w[i * P:(i + 1) * P, :])
            kk.dma("sync", cb_s[i], conv_b[i * P:(i + 1) * P, :])
            kk.dma("sync", dtb_s[i], dt_b[i * P:(i + 1) * P, :])
            kk.dma("sync", dvec_s[i], d_vec[i * P:(i + 1) * P, :])
        wdt_f32 = Buf(wstg.ap[:RK, :], "wdt_f32")
        kk.dma("sync", wdt_f32, w_dt_T)

        kk.emit("vector", lambda e: e.memset(zero16.ap, 0), writes=[zero16])
        kk.emit("vector", lambda e: e.memset(one16.ap, 1), writes=[one16])
        kk.emit("vector", lambda e: e.memset(ones_bf.ap, 1.0), writes=[ones_bf])
        kk.emit("vector", lambda e: e.memset(ones_f.ap, 1.0), writes=[ones_f])
        for i in range(DBLK):
            kk.emit("scalar", lambda e, i=i: e.activation(out=a_sc[i].ap, in_=a_sc[i].ap, func=AF.Exp),
                    reads=[a_sc[i]], writes=[a_sc[i]])
            kk.emit("vector", lambda e, i=i: e.tensor_scalar(out=a_sc[i].ap, in0=a_sc[i].ap,
                    scalar1=-1.0, scalar2=None, op0=ALU.mult), reads=[a_sc[i]], writes=[a_sc[i]])
            kk.emit("vector", lambda e, i=i: e.tensor_scalar(out=ncb_s[i].ap, in0=cb_s[i].ap,
                    scalar1=-1.0, scalar2=None, op0=ALU.mult), reads=[cb_s[i]], writes=[ncb_s[i]])
            kk.emit("vector", lambda e, i=i: e.memset(halo[i].ap, 0.0), writes=[halo[i]])
            kk.emit("vector", lambda e, i=i: e.memset(hstate[i].ap, 0.0), writes=[hstate[i]])
        kk.emit("scalar", lambda e: e.copy(out=wdt_bf.ap, in_=wdt_f32.ap),
                reads=[wdt_f32], writes=[wdt_bf])
        for i in range(DBLK):
            kk.dma("sync", wxp_stg, w_xp_T[i * P:(i + 1) * P, :])
            kk.emit("scalar", lambda e, i=i: e.copy(out=wxp_bf[i].ap, in_=wxp_stg.ap),
                    reads=[wxp_stg], writes=[wxp_bf[i]])

        def init_fn(g):
            g.reg_load(jr, j_s.ap[:1, :1])
            g.reg_mov(r_pc, 0)
            with g.If_eq(jr, 0):
                for f in range(NFLAG):
                    g.dma_start(out=flagsh[0][f:f + 1, :], in_=zero16.ap).then_inc(sem_gq, 16)
                g.dma_start(out=barsh[0], in_=nonce_s.ap).then_inc(sem_gq, 16)
            with g.Else():
                for f in range(NFLAG):
                    g.dma_start(out=flagsh[1][f:f + 1, :], in_=zero16.ap).then_inc(sem_gq, 16)
                g.dma_start(out=barsh[1], in_=nonce_s.ap).then_inc(sem_gq, 16)
            gqn[0] += 16 * (NFLAG + 1)
            g.wait_ge(sem_gq, gqn[0])
            for barx in (barsh[0], barsh[1]):
                g.reg_mov(r_nd, 1)
                with g.While(r_nd):
                    g.reg_add(r_pc, r_pc, 16)
                    g.dma_start(out=bar_s.ap, in_=barx).then_inc(sem_poll, 16)
                    g.wait_ge(sem_poll, r_pc)
                    g.reg_mov(r_nd, 0)
                    for kq in range(4):
                        g.reg_load(r_a, bar_s.ap[:1, kq:kq + 1])
                        g.reg_load(r_b, nonce_s.ap[:1, kq:kq + 1])
                        g.reg_alu(r_a, r_a, r_b, ALU.not_equal)
                        g.reg_alu(r_nd, r_nd, r_a, ALU.bitwise_or)
            return g.engine_nop()

        kk.emit("gpsimd", init_fn, reads=[j_s, nonce_s, zero16, one16], writes=[bar_s])

        kk.dma("gpsimd", wout_dr.ap.rearrange("b p c -> (b p) c"), w_out_T,
               wbufs=[wout_dr])

        def emit_dequant(src_packed, src_scales_T, n_rows_blk, n_in, dst_slab, ch_w):
            n_ch = n_in // ch_w
            pkv = Buf(dq_pk.ap[:, :n_in // 8], "pkv")
            scv16 = Buf(dq_sc16.ap[:, :n_in // 128], "scv16")
            scv = Buf(dq_sc.ap[:, :n_in // 128], "scv")
            nibv = Buf(dq_nib.ap[:, :ch_w], "nibv")
            t1 = Buf(dq_t1.ap[:, :ch_w], "t1")
            t2 = Buf(dq_t2.ap[:, :ch_w], "t2")
            t3 = Buf(dq_t3.ap[:, :ch_w], "t3")
            srep = Buf(dq_srep.ap[:, :ch_w], "srep")
            dqo = Buf(dq_out.ap[:, :ch_w], "dqo")
            for r in range(n_rows_blk):
                kk.barrier()
                kk.dma("sync", pkv, src_packed[r * P:(r + 1) * P, :])
                kk.dma("sync", scv16, src_scales_T[r * P:(r + 1) * P, :])
                kk.emit("vector", lambda e, scv=scv, scv16=scv16: e.tensor_copy(
                    out=scv.ap, in_=scv16.ap.bitcast(F16)), reads=[scv16], writes=[scv])
                for chi in range(n_ch):
                    c0 = chi * ch_w
                    pk16 = pkv.ap.bitcast(U16)[:, c0 // 4:(c0 + ch_w) // 4]
                    for kq in range(4):
                        outv = nibv.ap.rearrange("p (j four) -> p j four", four=4)[:, :, kq]
                        kk.emit("vector", lambda e, kq=kq, outv=outv, pk16=pk16: e.tensor_scalar(
                            out=outv, in0=pk16, scalar1=4 * kq, scalar2=15,
                            op0=ALU.logical_shift_right, op1=ALU.bitwise_and),
                            reads=[pkv], writes=[nibv])
                    kk.emit("vector", lambda e, t1=t1, nibv=nibv: e.tensor_scalar(
                        out=t1.ap, in0=nibv.ap, scalar1=7, scalar2=6,
                        op0=ALU.bitwise_and, op1=ALU.logical_shift_left),
                        reads=[nibv], writes=[t1])
                    kk.emit("vector", lambda e, t2=t2, nibv=nibv: e.tensor_scalar(
                        out=t2.ap, in0=nibv.ap, scalar1=8, scalar2=12,
                        op0=ALU.bitwise_and, op1=ALU.logical_shift_left),
                        reads=[nibv], writes=[t2])
                    kk.emit("vector", lambda e, t1=t1, t2=t2: e.scalar_tensor_tensor(
                        out=t1.ap, in0=t1.ap, scalar=16128, in1=t2.ap,
                        op0=ALU.add, op1=ALU.add), reads=[t1, t2], writes=[t1])
                    kk.emit("vector", lambda e, t3=t3, nibv=nibv: e.tensor_scalar(
                        out=t3.ap, in0=nibv.ap, scalar1=1, scalar2=None,
                        op0=ALU.bitwise_and), reads=[nibv], writes=[t3])
                    kk.emit("vector", lambda e, t3=t3: e.tensor_scalar(
                        out=t3.ap, in0=t3.ap, scalar1=16128, scalar2=None,
                        op0=ALU.mult), reads=[t3], writes=[t3])
                    kk.emit("vector", lambda e, t3=t3, t2=t2: e.tensor_tensor(
                        out=t3.ap, in0=t3.ap, in1=t2.ap, op=ALU.add),
                        reads=[t3, t2], writes=[t3])
                    kk.emit("vector", lambda e, nibv=nibv: e.tensor_scalar(
                        out=nibv.ap, in0=nibv.ap, scalar1=6, scalar2=None,
                        op0=ALU.bitwise_and), reads=[nibv], writes=[nibv])
                    kk.emit("vector", lambda e, t2=t2, nibv=nibv, t1=t1, t3=t3: e.select(
                        out=t2.ap, mask=nibv.ap, on_true=t1.ap, on_false=t3.ap),
                        reads=[nibv, t1, t3], writes=[t2])
                    for gi in range(ch_w // 128):
                        gg = c0 // 128 + gi
                        kk.emit("vector", lambda e, gi=gi, gg=gg, srep=srep, scv=scv:
                                e.tensor_scalar(
                            out=srep.ap[:, gi * 128:(gi + 1) * 128], in0=ones_f.ap,
                            scalar1=scv.ap[:, gg:gg + 1], scalar2=None, op0=ALU.mult),
                            reads=[ones_f, scv], writes=[srep])
                    kk.emit("vector", lambda e, dqo=dqo, t2=t2, srep=srep: e.tensor_tensor(
                        out=dqo.ap, in0=t2.ap.bitcast(BF16), in1=srep.ap, op=ALU.mult),
                        reads=[t2, srep], writes=[dqo])
                    if "dqx" in tap_out and dst_slab is wgu_dr and r == 0:
                        kk.dma("gpsimd", tap_out["dqx"][:, c0:c0 + ch_w], dqo)
                        kk.dma("gpsimd", tap_out["dqn"][:, c0:c0 + ch_w], nibv)
                    for ib in range(ch_w // 128):
                        gib = c0 // 128 + ib
                        pt = pstb[ib % 2]
                        po = dq_tpo[ib % 2]
                        kk.emit("tensor", lambda e, ib=ib, pt=pt, dqo=dqo: e.transpose(
                            pt.ap, dqo.ap[:, ib * 128:(ib + 1) * 128], ident_s.ap),
                            reads=[dqo, ident_s], writes=[pt])
                        kk.emit("scalar", lambda e, pt=pt, po=po: e.copy(out=po.ap, in_=pt.ap),
                                reads=[pt], writes=[po])
                        kk.dma("gpsimd", dst_slab.ap[gib, :, r * P:(r + 1) * P], po,
                               rbufs=[po], wbufs=[dst_slab])

        emit_dequant(gu_packed, gu_scales_T, OBLK, DM, wgu_dr, 1024)
        emit_dequant(dn_packed, dn_scales_T, DMH // P, 2 * IH, wd_dr, 1408)
        if "wgu" in tap_out:
            kk.dma("gpsimd", tap_out["wgu"], wgu_dr)
        if "wd" in tap_out:
            kk.dma("gpsimd", tap_out["wd"], wd_dr)

        kk.barrier()
        for part in range(2):
            w_src = w_in_x_T if part == 0 else w_in_z_T
            for i in range(MBLK):
                kk.dma("sync", wstg, w_src[i * P:(i + 1) * P, :])
                kk.emit("scalar", lambda e, i=i: e.copy(out=win_sb[i].ap, in_=wstg.ap),
                        reads=[wstg], writes=[win_sb[i]])

            for c in range(nch):
                t0 = c * tc
                if part == 0:
                    for tt in range(tc // P):
                        xk = xtok[tt % 2]
                        xcv = xc_t[tt % 2]
                        xnv = xn_tok[tt % 2]
                        kk.dma("sync", xk, x_full[t0 + tt * P: t0 + (tt + 1) * P, :])
                        kk.emit("vector", lambda e, xk=xk: e.tensor_reduce(
                            out=ln_m.ap, in_=xk.ap, axis=AX.X, op=ALU.add),
                            reads=[xk], writes=[ln_m])
                        kk.emit("vector", lambda e: e.tensor_scalar(out=ln_m.ap,
                                in0=ln_m.ap, scalar1=1.0 / DM, scalar2=None, op0=ALU.mult),
                                reads=[ln_m], writes=[ln_m])
                        kk.emit("vector", lambda e, xk=xk, xcv=xcv: e.tensor_scalar(
                            out=xcv.ap, in0=xk.ap, scalar1=ln_m.ap, scalar2=None,
                            op0=ALU.subtract), reads=[xk, ln_m], writes=[xcv])
                        kk.emit("scalar", lambda e, xcv=xcv: e.activation(
                            out=ln_sq.ap, in_=xcv.ap, func=AF.Square, accum_out=ln_ssq.ap),
                            reads=[xcv], writes=[ln_sq, ln_ssq])
                        kk.emit("scalar", lambda e: e.activation(out=ln_rs.ap,
                                in_=ln_ssq.ap, func=AF.Ln, scale=1.0 / DM, bias=LN_EPS),
                                reads=[ln_ssq, constv], writes=[ln_rs])
                        kk.emit("scalar", lambda e: e.activation(out=ln_rs.ap,
                                in_=ln_rs.ap, func=AF.Exp, scale=-0.5),
                                reads=[ln_rs], writes=[ln_rs])
                        kk.emit("vector", lambda e, xcv=xcv, xnv=xnv: e.tensor_scalar(
                            out=xnv.ap, in0=xcv.ap, scalar1=ln_rs.ap, scalar2=None,
                            op0=ALU.mult), reads=[xcv, ln_rs], writes=[xnv])
                        for mb in range(MBLK):
                            pt = pstb[mb % 2]
                            kk.emit("tensor", lambda e, mb=mb, pt=pt, xnv=xnv: e.transpose(
                                pt.ap, xnv.ap[:, mb * 128:(mb + 1) * 128], ident_s.ap),
                                reads=[xnv, ident_s], writes=[pt])
                            kk.emit("scalar", lambda e, mb=mb, tt=tt, pt=pt: e.copy(
                                out=xn_fm[mb].ap[:, tt * P:(tt + 1) * P], in_=pt.ap),
                                reads=[pt], writes=[xn_fm[mb]])
                    for mb in range(MBLK):
                        kk.dma("gpsimd", xn_dr.ap[mb * P:(mb + 1) * P, t0:t0 + tc],
                               xn_fm[mb], wbufs=[xn_dr])
                else:
                    for mb in range(MBLK):
                        kk.dma("sync", xn_fm[mb], xn_dr.ap[mb * P:(mb + 1) * P, t0:t0 + tc],
                               rbufs=[xn_dr])

                for eb in range(DBLK):
                    pm = psb[eb % 2]
                    for k in range(MBLK):
                        kk.emit("tensor", lambda e, k=k, eb=eb, pm=pm: e.matmul(
                            pm.ap, win_sb[k].ap[:, eb * 128:(eb + 1) * 128], xn_fm[k].ap,
                            start=(k == 0), stop=(k == MBLK - 1)),
                            reads=[win_sb[k], xn_fm[k]], writes=[pm])
                    if part == 0:
                        xl = xl_t[eb % 2]
                        cv = cv_t[eb % 2]
                        se_ = sil_e[eb % 2]
                        sr_ = sil_r[eb % 2]
                        xt_ = xt_t[eb % 2]
                        xb_ = xt_bf[eb % 2]
                        kk.emit("vector", lambda e, xl=xl, eb=eb: e.tensor_copy(
                            out=xl.ap[:, 0:3], in_=halo[eb].ap), reads=[halo[eb]], writes=[xl])
                        kk.emit("scalar", lambda e, xl=xl, pm=pm: e.copy(
                            out=xl.ap[:, 3:3 + tc], in_=pm.ap), reads=[pm], writes=[xl])
                        kk.emit("vector", lambda e, xl=xl, cv=cv, eb=eb: e.tensor_scalar(
                            out=cv.ap, in0=xl.ap[:, 0:tc], scalar1=cw_s[eb].ap[:, 0:1],
                            scalar2=None, op0=ALU.mult), reads=[xl, cw_s[eb]], writes=[cv])
                        for kq in range(1, 4):
                            kk.emit("vector", lambda e, xl=xl, cv=cv, eb=eb, kq=kq:
                                    e.scalar_tensor_tensor(out=cv.ap, in0=xl.ap[:, kq:kq + tc],
                                    scalar=cw_s[eb].ap[:, kq:kq + 1], in1=cv.ap,
                                    op0=ALU.mult, op1=ALU.add),
                                    reads=[xl, cw_s[eb], cv], writes=[cv])
                        kk.emit("vector", lambda e, xl=xl, eb=eb: e.tensor_copy(
                            out=halo[eb].ap, in_=xl.ap[:, tc:tc + 3]),
                            reads=[xl], writes=[halo[eb]])
                        kk.emit("scalar", lambda e, cv=cv, se_=se_, eb=eb: e.activation(
                            out=se_.ap, in_=cv.ap, func=AF.Exp, scale=-1.0, bias=ncb_s[eb].ap),
                            reads=[cv, ncb_s[eb]], writes=[se_])
                        kk.emit("vector", lambda e, se_=se_, sr_=sr_: e.tensor_scalar(
                            out=sr_.ap, in0=se_.ap, scalar1=1.0, scalar2=None, op0=ALU.add),
                            reads=[se_], writes=[sr_])
                        kk.emit("vector", lambda e, sr_=sr_: e.reciprocal(out=sr_.ap, in_=sr_.ap),
                                reads=[sr_], writes=[sr_])
                        kk.emit("vector", lambda e, cv=cv, sr_=sr_, xt_=xt_, eb=eb:
                                e.scalar_tensor_tensor(out=xt_.ap, in0=cv.ap,
                                scalar=cb_s[eb].ap, in1=sr_.ap, op0=ALU.add, op1=ALU.mult),
                                reads=[cv, cb_s[eb], sr_], writes=[xt_])
                        kk.emit("vector", lambda e, xt_=xt_, xb_=xb_: e.tensor_copy(
                            out=xb_.ap, in_=xt_.ap), reads=[xt_], writes=[xb_])
                        kk.dma("gpsimd", xt_dr.ap[eb * P:(eb + 1) * P, t0:t0 + tc], xt_,
                               wbufs=[xt_dr])
                        kk.emit("tensor", lambda e, eb=eb, xb_=xb_: e.matmul(
                            psb[2].ap, wxp_bf[eb].ap[:, 0:128], xb_.ap,
                            start=(eb == 0), stop=(eb == DBLK - 1)),
                            reads=[wxp_bf[eb], xb_], writes=[psb[2]])
                        kk.emit("tensor", lambda e, eb=eb, xb_=xb_: e.matmul(
                            psb[3].ap[:32, :], wxp_bf[eb].ap[:, 128:160], xb_.ap,
                            start=(eb == 0), stop=(eb == DBLK - 1)),
                            reads=[wxp_bf[eb], xb_], writes=[psb[3]])
                    else:
                        zv = z_t[eb % 2]
                        zs_ = zs_t[eb % 2]
                        se_ = sil_e[eb % 2]
                        sr_ = sil_r[eb % 2]
                        kk.emit("scalar", lambda e, se_=se_, pm=pm: e.activation(
                            out=se_.ap, in_=pm.ap, func=AF.Exp, scale=-1.0),
                            reads=[pm], writes=[se_])
                        kk.emit("scalar", lambda e, zv=zv, pm=pm: e.copy(out=zv.ap, in_=pm.ap),
                                reads=[pm], writes=[zv])
                        kk.emit("vector", lambda e, se_=se_, sr_=sr_: e.tensor_scalar(
                            out=sr_.ap, in0=se_.ap, scalar1=1.0, scalar2=None, op0=ALU.add),
                            reads=[se_], writes=[sr_])
                        kk.emit("vector", lambda e, sr_=sr_: e.reciprocal(out=sr_.ap, in_=sr_.ap),
                                reads=[sr_], writes=[sr_])
                        kk.emit("vector", lambda e, zv=zv, sr_=sr_, zs_=zs_: e.tensor_tensor(
                            out=zs_.ap, in0=zv.ap, in1=sr_.ap, op=ALU.mult),
                            reads=[zv, sr_], writes=[zs_])
                        kk.dma("gpsimd", zs_dr.ap[eb * P:(eb + 1) * P, t0:t0 + tc], zs_,
                               wbufs=[zs_dr])

                if part == 0:
                    kk.emit("scalar", lambda e: e.copy(out=pj1_s.ap, in_=psb[2].ap),
                            reads=[psb[2]], writes=[pj1_s])
                    kk.emit("scalar", lambda e: e.copy(out=pj2_s.ap, in_=psb[3].ap[:32, :]),
                            reads=[psb[3]], writes=[pj2_s])
                    kk.emit("gpsimd", g_branch_write(projsh[0][0:128, t0:t0 + tc],
                                                     projsh[1][0:128, t0:t0 + tc], pj1_s.ap),
                            reads=[pj1_s, j_s, bar_s], writes=[projsh_b])
                    kk.emit("gpsimd", g_branch_write(projsh[0][128:160, t0:t0 + tc],
                                                     projsh[1][128:160, t0:t0 + tc], pj2_s.ap),
                            reads=[pj2_s, j_s], writes=[projsh_b])
                    kk.emit("gpsimd", g_flag(c), reads=[one16, j_s, projsh_b])

        if "xn" in tap_out:
            kk.dma("gpsimd", tap_out["xn"], xn_dr)
        if "xt" in tap_out:
            kk.dma("gpsimd", tap_out["xt"], xt_dr)
        if "zs" in tap_out:
            kk.dma("gpsimd", tap_out["zs"], zs_dr)

        kk.barrier()
        poll_tok = Buf(None, "poll_tok")
        for c in range(nch):
            t0 = c * tc
            kk.emit("gpsimd", g_poll(c), reads=[j_s], writes=[poll_tok, flag_s])
            kk.dma("sync", pj1a, projsh[0][0:128, t0:t0 + tc], rbufs=[poll_tok])
            kk.dma("sync", pj1b, projsh[1][0:128, t0:t0 + tc])
            kk.dma("sync", pj2a, projsh[0][128:160, t0:t0 + tc])
            kk.dma("sync", pj2b, projsh[1][128:160, t0:t0 + tc])
            kk.emit("vector", lambda e: e.tensor_tensor(out=pj1a.ap, in0=pj1a.ap,
                    in1=pj1b.ap, op=ALU.add), reads=[pj1a, pj1b], writes=[pj1a])
            kk.emit("vector", lambda e: e.tensor_tensor(out=pj2a.ap, in0=pj2a.ap,
                    in1=pj2b.ap, op=ALU.add), reads=[pj2a, pj2b], writes=[pj2a])
            kk.emit("vector", lambda e: e.tensor_copy(out=pj1bf.ap, in_=pj1a.ap),
                    reads=[pj1a], writes=[pj1bf])
            kk.emit("vector", lambda e: e.tensor_copy(out=pjbc_bf.ap, in_=pj2a.ap),
                    reads=[pj2a], writes=[pjbc_bf])
            if "proj" in tap_out:
                kk.dma("gpsimd", tap_out["proj"][0:128, t0:t0 + tc], pj1a)
                kk.dma("gpsimd", tap_out["proj"][128:160, t0:t0 + tc], pj2a)

            for n in range(2 * NST):
                dstf = bb_t[n] if n < NST else cb_t[n - NST]
                pbc = psb[4 + (n % 2)]
                rsg = rowstg[n % 2]
                kk.dma("sync", rsg, pjbc_bf.ap[n:n + 1, :], rbufs=[pjbc_bf])
                kk.emit("tensor", lambda e, pbc=pbc, rsg=rsg: e.matmul(
                    pbc.ap, ones_bf.ap, rsg.ap, start=True, stop=True),
                    reads=[rsg, ones_bf], writes=[pbc])
                kk.emit("scalar", lambda e, pbc=pbc, dstf=dstf: e.copy(
                    out=dstf.ap, in_=pbc.ap), reads=[pbc], writes=[dstf])

            for db in range(DBLK):
                pm = psb[db % 2]
                kk.emit("tensor", lambda e, db=db, pm=pm: e.matmul(
                    pm.ap, wdt_bf.ap[:, db * 128:(db + 1) * 128], pj1bf.ap,
                    start=True, stop=True), reads=[wdt_bf, pj1bf], writes=[pm])
                kk.emit("scalar", lambda e, pm=pm, db=db: e.activation(
                    out=sp_r.ap, in_=pm.ap, func=AF.Relu, bias=dtb_s[db].ap),
                    reads=[pm, dtb_s[db]], writes=[sp_r])
                kk.emit("scalar", lambda e, pm=pm, db=db: e.activation(
                    out=sp_a.ap, in_=pm.ap, func=AF.Abs, bias=dtb_s[db].ap),
                    reads=[pm, dtb_s[db]], writes=[sp_a])
                kk.emit("scalar", lambda e: e.activation(out=sp_a.ap, in_=sp_a.ap,
                        func=AF.Exp, scale=-1.0), reads=[sp_a], writes=[sp_a])
                kk.emit("scalar", lambda e: e.activation(out=sp_a.ap, in_=sp_a.ap,
                        func=AF.Ln, bias=1.0), reads=[sp_a, constv], writes=[sp_a])
                kk.dma("sync", xt_rd, xt_dr.ap[db * P:(db + 1) * P, t0:t0 + tc],
                       rbufs=[xt_dr])
                kk.dma("sync", zs_rd, zs_dr.ap[db * P:(db + 1) * P, t0:t0 + tc],
                       rbufs=[zs_dr])
                kk.emit("vector", lambda e: e.tensor_tensor(out=dt_s.ap, in0=sp_r.ap,
                        in1=sp_a.ap, op=ALU.add), reads=[sp_r, sp_a], writes=[dt_s])
                kk.emit("vector", lambda e: e.tensor_tensor(out=w_s.ap, in0=dt_s.ap,
                        in1=xt_rd.ap, op=ALU.mult), reads=[dt_s, xt_rd], writes=[w_s])
                if "dt" in tap_out:
                    kk.dma("gpsimd", tap_out["dt"][db * P:(db + 1) * P, t0:t0 + tc], dt_s)

                for grp in range(2):
                    yp = ypart[grp]
                    for ni in range(8):
                        n = grp * 8 + ni
                        da = da_t[ni % 2]
                        wb = wb_t[ni % 2]
                        kk.emit("scalar", lambda e, db=db, n=n, da=da: e.activation(
                            out=da.ap, in_=dt_s.ap, func=AF.Exp,
                            scale=a_sc[db].ap[:, n:n + 1]),
                            reads=[dt_s, a_sc[db]], writes=[da])
                        kk.emit("vector", lambda e, n=n, wb=wb: e.tensor_tensor(
                            out=wb.ap, in0=w_s.ap, in1=bb_t[n].ap, op=ALU.mult),
                            reads=[w_s, bb_t[n]], writes=[wb])
                        kk.emit("vector", lambda e, db=db, n=n, ni=ni, da=da, wb=wb:
                                e.tensor_tensor_scan(
                                    out=h_slab.ap[:, ni * tc:(ni + 1) * tc],
                                    data0=da.ap, data1=wb.ap,
                                    initial=hstate[db].ap[:, n:n + 1],
                                    op0=ALU.mult, op1=ALU.add),
                                reads=[da, wb, hstate[db]], writes=[h_slab])
                    for ni in range(8):
                        n = grp * 8 + ni
                        kk.emit("vector", lambda e, db=db, n=n, ni=ni: e.tensor_copy(
                            out=hstate[db].ap[:, n:n + 1],
                            in_=h_slab.ap[:, (ni + 1) * tc - 1:(ni + 1) * tc]),
                            reads=[h_slab], writes=[hstate[db]])
                    for ni in range(8):
                        n = grp * 8 + ni
                        kk.emit("vector", lambda e, n=n, ni=ni: e.tensor_tensor(
                            out=hc_slab.ap[:, ni * tc:(ni + 1) * tc],
                            in0=h_slab.ap[:, ni * tc:(ni + 1) * tc],
                            in1=cb_t[n].ap, op=ALU.mult),
                            reads=[h_slab, cb_t[n]], writes=[hc_slab])
                    kk.emit("vector", lambda e, yp=yp: e.tensor_reduce(
                        out=yp.ap, in_=hc_slab.ap.rearrange("p (n t) -> p t n", n=8),
                        axis=AX.X, op=ALU.add), reads=[hc_slab], writes=[yp])
                kk.emit("vector", lambda e: e.tensor_tensor(out=y_s.ap, in0=ypart[0].ap,
                        in1=ypart[1].ap, op=ALU.add), reads=[ypart[0], ypart[1]], writes=[y_s])
                if "y" in tap_out:
                    kk.dma("gpsimd", tap_out["y"][db * P:(db + 1) * P, t0:t0 + tc], y_s)
                kk.emit("vector", lambda e, db=db: e.scalar_tensor_tensor(
                    out=y_s.ap, in0=xt_rd.ap, scalar=dvec_s[db].ap, in1=y_s.ap,
                    op0=ALU.mult, op1=ALU.add), reads=[xt_rd, dvec_s[db], y_s], writes=[y_s])
                kk.emit("vector", lambda e: e.tensor_tensor(out=y_s.ap, in0=y_s.ap,
                        in1=zs_rd.ap, op=ALU.mult), reads=[y_s, zs_rd], writes=[y_s])
                kk.emit("vector", lambda e, db=db: e.tensor_copy(
                    out=yg_slab.ap[:, db * tc:(db + 1) * tc], in_=y_s.ap),
                    reads=[y_s], writes=[yg_slab])

            src = yg_slab.ap.rearrange("p (b t) -> p b t", b=DBLK)
            d0 = ysh[0][:, t0:t0 + tc].rearrange("(b p) t -> p b t", p=P)
            d1 = ysh[1][:, t0:t0 + tc].rearrange("(b p) t -> p b t", p=P)
            kk.emit("gpsimd", g_branch_write(d0, d1, src),
                    reads=[yg_slab, j_s], writes=[ysh_b])
            kk.emit("gpsimd", g_flag(nch + c), reads=[one16, j_s, ysh_b])

        kk.barrier()

        for c in range(nch):
            kk.emit("gpsimd", g_poll(nch + c), reads=[j_s], writes=[poll_tok, flag_s])
        kk.barrier()

        for C in range(nbig):
            tb0 = C * tb
            for mb in range(MBLK):
                kk.dma("sync", xn3[mb], xn_dr.ap[mb * P:(mb + 1) * P, tb0:tb0 + tb],
                       rbufs=[xn_dr])
            for ip in range(IBLK2 // 2):
                for half in range(2):
                    ob = ip + half * (OBLK // 2)
                    for k in range(MBLK):
                        kk.dma("sync", wgu_t[half * MBLK + k],
                               wgu_dr.ap[k, :, ob * P:(ob + 1) * P], rbufs=[wgu_dr])
                for sl in range(tb // 512):
                    for half in range(2):
                        pm = psb[half]
                        for k in range(MBLK):
                            kk.emit("tensor", lambda e, k=k, half=half, sl=sl, pm=pm: e.matmul(
                                pm.ap, wgu_t[half * MBLK + k].ap,
                                xn3[k].ap[:, sl * 512:(sl + 1) * 512],
                                start=(k == 0), stop=(k == MBLK - 1)),
                                reads=[wgu_t[half * MBLK + k], xn3[k]], writes=[pm])
                    kk.emit("scalar", lambda e: e.activation(
                        out=g_e.ap, in_=psb[0].ap, func=AF.Exp, scale=-1.0),
                        reads=[psb[0]], writes=[g_e])
                    kk.emit("scalar", lambda e: e.copy(out=g_cp.ap, in_=psb[0].ap),
                            reads=[psb[0]], writes=[g_cp])
                    kk.emit("scalar", lambda e: e.copy(out=u_cp.ap, in_=psb[1].ap),
                            reads=[psb[1]], writes=[u_cp])
                    kk.emit("vector", lambda e: e.tensor_scalar(out=g_r.ap,
                            in0=g_e.ap, scalar1=1.0, scalar2=None, op0=ALU.add),
                            reads=[g_e], writes=[g_r])
                    kk.emit("vector", lambda e: e.reciprocal(out=g_r.ap, in_=g_r.ap),
                            reads=[g_r], writes=[g_r])
                    kk.emit("vector", lambda e: e.tensor_tensor(out=g_sg.ap,
                            in0=g_cp.ap, in1=g_r.ap, op=ALU.mult),
                            reads=[g_cp, g_r], writes=[g_sg])
                    kk.emit("vector", lambda e: e.tensor_tensor(out=g_sg.ap,
                            in0=g_sg.ap, in1=u_cp.ap, op=ALU.mult),
                            reads=[g_sg, u_cp], writes=[g_sg])
                    hb = h_bf[sl % 2]
                    kk.emit("vector", lambda e, hb=hb: e.tensor_copy(
                        out=hb.ap, in_=g_sg.ap), reads=[g_sg], writes=[hb])
                    dst0 = hsh[0][ip * P:(ip + 1) * P, tb0 + sl * 512: tb0 + (sl + 1) * 512]
                    dst1 = hsh[1][ip * P:(ip + 1) * P, tb0 + sl * 512: tb0 + (sl + 1) * 512]
                    kk.emit("gpsimd", g_branch_write(dst0, dst1, hb.ap),
                            reads=[hb, j_s], writes=[hsh_b])
            kk.emit("gpsimd", g_flag(2 * nch + C), reads=[one16, j_s, hsh_b])
            kk.emit("gpsimd", g_poll(2 * nch + C), reads=[j_s], writes=[poll_tok, flag_s])
            kk.barrier()

            for ib in range(IBLK2):
                kk.dma("sync", wd_sb[ib], wd_dr.ap[ib], rbufs=[wd_dr])
            for t128 in range(tb // P):
                trow = tb0 + t128 * P
                pm0 = psb[2 * (t128 % 2)]
                pm1 = psb[2 * (t128 % 2) + 1]
                for ib in range(IBLK2):
                    hl = h_ld[ib % 4]
                    srcsh = hsh[0] if ib < IBLK2 // 2 else hsh[1]
                    rb = ib % (IBLK2 // 2)
                    kk.dma("sync", hl, srcsh[rb * P:(rb + 1) * P, trow:trow + P],
                           rbufs=[poll_tok])
                    kk.emit("tensor", lambda e, hl=hl, ib=ib, pm0=pm0: e.matmul(
                        pm0.ap, hl.ap, wd_sb[ib].ap[:, 0:512],
                        start=(ib == 0), stop=(ib == IBLK2 - 1)),
                        reads=[hl, wd_sb[ib]], writes=[pm0])
                    kk.emit("tensor", lambda e, hl=hl, ib=ib, pm1=pm1: e.matmul(
                        pm1.ap, hl.ap, wd_sb[ib].ap[:, 512:1024],
                        start=(ib == 0), stop=(ib == IBLK2 - 1)),
                        reads=[hl, wd_sb[ib]], writes=[pm1])
                yf = yf_t[t128 % 2]
                kk.emit("scalar", lambda e, pm0=pm0, yf=yf: e.copy(
                    out=yf.ap[:, 0:512], in_=pm0.ap), reads=[pm0], writes=[yf])
                kk.emit("scalar", lambda e, pm1=pm1, yf=yf: e.copy(
                    out=yf.ap[:, 512:1024], in_=pm1.ap), reads=[pm1], writes=[yf])
                kk.dma("gpsimd", yffn_dr.ap[trow:trow + P, :], yf, wbufs=[yffn_dr])
                kk.emit("scalar", lambda e, yf=yf: e.activation(
                    out=sqsc.ap, in_=yf.ap, func=AF.Square, accum_out=sq_p.ap),
                    reads=[yf], writes=[sqsc, sq_p])
                rowi = trow // P
                dst0 = sqsh[0][rowi:rowi + 1, :].rearrange("r p -> p r")
                dst1 = sqsh[1][rowi:rowi + 1, :].rearrange("r p -> p r")
                kk.emit("gpsimd", g_branch_write(dst0, dst1, sq_p.ap),
                        reads=[sq_p, j_s], writes=[sqsh_b])
            kk.emit("gpsimd", g_flag(2 * nch + nbig + C), reads=[one16, j_s, sqsh_b])
            kk.emit("gpsimd", g_poll(2 * nch + nbig + C), reads=[j_s], writes=[poll_tok, flag_s])
            kk.barrier()

            for ib in range(WOBLK):
                kk.dma("sync", wout_sb[ib], wout_dr.ap[ib], rbufs=[wout_dr])
            for t128 in range(tb // P):
                trow = tb0 + t128 * P
                rowi = trow // P
                pm0 = psb[2 + (t128 % 2)]
                pm1 = psb[4 + (t128 % 2)]
                for ib in range(WOBLK):
                    yl = y_ld[ib % 4]
                    srcsh = ysh[0] if ib < WOBLK // 2 else ysh[1]
                    rb = ib % (WOBLK // 2)
                    kk.dma("sync", yl, srcsh[rb * P:(rb + 1) * P, trow:trow + P],
                           rbufs=[poll_tok])
                    kk.emit("tensor", lambda e, yl=yl, pm0=pm0, ib=ib: e.matmul(
                        pm0.ap, yl.ap, wout_sb[ib].ap[:, 0:512],
                        start=(ib == 0), stop=(ib == WOBLK - 1)),
                        reads=[yl, wout_sb[ib]], writes=[pm0])
                    kk.emit("tensor", lambda e, yl=yl, pm1=pm1, ib=ib: e.matmul(
                        pm1.ap, yl.ap, wout_sb[ib].ap[:, 512:1024],
                        start=(ib == 0), stop=(ib == WOBLK - 1)),
                        reads=[yl, wout_sb[ib]], writes=[pm1])
                sv = s_ev[t128 % 2]
                kk.emit("scalar", lambda e, pm0=pm0, sv=sv: e.copy(
                    out=sv.ap[:, 0:512], in_=pm0.ap), reads=[pm0], writes=[sv])
                kk.emit("scalar", lambda e, pm1=pm1, sv=sv: e.copy(
                    out=sv.ap[:, 512:1024], in_=pm1.ap), reads=[pm1], writes=[sv])
                if "s" in tap_out:
                    kk.dma("gpsimd", tap_out["s"][trow:trow + P, :], sv)
                kk.dma("sync", sq0, sqsh[0][rowi:rowi + 1, :].rearrange("r p -> p r"),
                       rbufs=[poll_tok])
                kk.dma("sync", sq1, sqsh[1][rowi:rowi + 1, :].rearrange("r p -> p r"),
                       rbufs=[poll_tok])
                kk.emit("vector", lambda e: e.tensor_tensor(out=sq0.ap, in0=sq0.ap,
                        in1=sq1.ap, op=ALU.add), reads=[sq0, sq1], writes=[sq0])
                kk.emit("scalar", lambda e: e.activation(out=rstd3.ap, in_=sq0.ap,
                        func=AF.Ln, scale=1.0 / DM, bias=RMS_EPS), reads=[sq0, constv], writes=[rstd3])
                kk.emit("scalar", lambda e: e.activation(out=rstd3.ap, in_=rstd3.ap,
                        func=AF.Exp, scale=-0.5), reads=[rstd3], writes=[rstd3])
                kk.dma("sync", yfr, yffn_dr.ap[trow:trow + P, :], rbufs=[yffn_dr])
                kk.emit("vector", lambda e: e.tensor_scalar(out=yfr.ap, in0=yfr.ap,
                        scalar1=rstd3.ap, scalar2=None, op0=ALU.mult),
                        reads=[yfr, rstd3], writes=[yfr])
                kk.emit("scalar", lambda e: e.activation(out=g3e.ap, in_=yfr.ap,
                        func=AF.Exp, scale=-1.0), reads=[yfr], writes=[g3e])
                kk.emit("vector", lambda e: e.tensor_scalar(out=g3r.ap, in0=g3e.ap,
                        scalar1=1.0, scalar2=None, op0=ALU.add), reads=[g3e], writes=[g3r])
                kk.emit("vector", lambda e: e.reciprocal(out=g3r.ap, in_=g3r.ap),
                        reads=[g3r], writes=[g3r])
                kk.emit("vector", lambda e: e.tensor_tensor(out=yfr.ap, in0=yfr.ap,
                        in1=g3r.ap, op=ALU.mult), reads=[yfr, g3r], writes=[yfr])
                ov = o_t[t128 % 2]
                kk.dma("sync", xh_t, x_halfi[trow:trow + P, :])
                kk.emit("vector", lambda e, sv=sv, ov=ov: e.tensor_tensor(
                    out=ov.ap, in0=sv.ap, in1=yfr.ap, op=ALU.mult),
                    reads=[sv, yfr], writes=[ov])
                kk.emit("vector", lambda e, ov=ov: e.tensor_tensor(
                    out=ov.ap, in0=ov.ap, in1=xh_t.ap, op=ALU.add),
                    reads=[ov, xh_t], writes=[ov])
                kk.dma("sync", out_half[trow:trow + P, :], ov)
            kk.barrier()

        if "yffn" in tap_out:
            kk.dma("gpsimd", tap_out["yffn"], yffn_dr)
        if "hh" in tap_out:
            kk.dma("gpsimd", tap_out["hh"][0], hsh[0], rbufs=[hsh_b])
            kk.dma("gpsimd", tap_out["hh"][1], hsh[1], rbufs=[hsh_b])
        if "wo" in tap_out:
            kk.dma("gpsimd", tap_out["wo"], wout_dr)
        if "yy" in tap_out:
            kk.dma("gpsimd", tap_out["yy"][0], ysh[0], rbufs=[ysh_b])
            kk.dma("gpsimd", tap_out["yy"][1], ysh[1], rbufs=[ysh_b])

        kk.barrier()
        for en in ("sync", "gpsimd"):
            for s in range(K.NSLOT):
                if kk.slot_cnt[en][s]:
                    kk._wait(en, (en, "d", s), kk.slot_cnt[en][s])

        kk.flush(block)

    nc._kk = kk
    return nc



_IDENT = np.eye(128, dtype=ml_dtypes.bfloat16)


def _core_inputs(core, inp):
    b, j = core // 2, core & 1
    x = np.ascontiguousarray(inp["L0_attn_hidden_states"][b])
    wi = inp["L0_attn_in_proj_w"]
    di0, di1 = j * DIH, (j + 1) * DIH
    return {
        "x_full": x,
        "x_halfi": np.ascontiguousarray(x[:, j * DMH:(j + 1) * DMH]),
        "w_in_x_T": np.ascontiguousarray(wi[di0:di1, :].T),
        "w_in_z_T": np.ascontiguousarray(wi[4096 + di0:4096 + di1, :].T),
        "conv_w": np.ascontiguousarray(inp["L0_attn_conv_w"][di0:di1]),
        "conv_b": np.ascontiguousarray(inp["L0_attn_conv_b"][di0:di1, None]),
        "w_xp_T": np.ascontiguousarray(inp["L0_attn_x_proj_w"][:, di0:di1].T),
        "w_dt_T": np.ascontiguousarray(inp["L0_attn_dt_proj_w"][di0:di1, :].T),
        "dt_b": np.ascontiguousarray(inp["L0_attn_dt_proj_b"][di0:di1, None]),
        "a_log": np.ascontiguousarray(inp["L0_attn_A_log"][di0:di1]),
        "d_vec": np.ascontiguousarray(inp["L0_attn_D"][di0:di1, None]),
        "w_out_T": np.ascontiguousarray(
            inp["L0_attn_out_proj_w"].T[:, j * DMH:(j + 1) * DMH]),
        "gu_packed": np.ascontiguousarray(np.concatenate([
            inp["L0_ffn_gate_up_packed"][j * IH:(j + 1) * IH],
            inp["L0_ffn_gate_up_packed"][5632 + j * IH:5632 + (j + 1) * IH]], axis=0)),
        "gu_scales_T": np.ascontiguousarray(np.concatenate([
            inp["L0_ffn_gate_up_scales"][:, j * IH:(j + 1) * IH],
            inp["L0_ffn_gate_up_scales"][:, 5632 + j * IH:5632 + (j + 1) * IH]],
            axis=1).T.copy()).view(np.uint16),
        "dn_packed": np.ascontiguousarray(
            inp["L0_ffn_down_packed"][j * DMH:(j + 1) * DMH]),
        "dn_scales_T": np.ascontiguousarray(
            inp["L0_ffn_down_scales"][:, j * DMH:(j + 1) * DMH].T).view(np.uint16),
        "j_in": np.array([[j]], dtype=np.uint32),
        "ident": _IDENT,
        "constv": np.broadcast_to(
            np.array([LN_EPS, RMS_EPS, 1.0], np.float32), (128, 3)).copy(),
    }


_prog_cache = {}


def kernel(**inputs):
    t_tokens = inputs["L0_attn_hidden_states"].shape[1]
    if t_tokens not in _prog_cache:
        _prog_cache[t_tokens] = build_program(t_tokens=t_tokens)
    nc = _prog_cache[t_tokens]
    nonce = np.random.randint(1, 2**31 - 1, size=(1, 4)).astype(np.uint32)
    in_maps = []
    for core in range(8):
        m = _core_inputs(core, inputs)
        m["nonce_in"] = nonce
        in_maps.append(m)
    res = run_bass_kernel_spmd(nc, in_maps, core_ids=list(range(8)))
    out = np.empty((4, t_tokens, DM), dtype=np.float32)
    for core in range(8):
        b, j = core // 2, core & 1
        out[b][:, j * DMH:(j + 1) * DMH] = res.results[core]["out_half"]
    return out


# version 02
# speedup vs baseline: 1.0148x; 1.0148x over previous
import numpy as np
import ml_dtypes
import concourse.bass as bass
import concourse.mybir as mybir
from concourse.bass_utils import run_bass_kernel_spmd

F32 = mybir.dt.float32
BF16 = mybir.dt.bfloat16
F16 = mybir.dt.float16
I32 = mybir.dt.int32
U32 = mybir.dt.uint32
U16 = mybir.dt.uint16
AF = mybir.ActivationFunctionType
ALU = mybir.AluOpType
AX = mybir.AxisListType

P = 128
DM = 2048
DIH = 2048
NST = 16
RK = 128
IH = 2816
GU = 2 * IH
DMH = 1024
LN_EPS = 1e-5
RMS_EPS = 1e-6
DBLK = DIH // P
MBLK = DM // P
OBLK = GU // P
IBLK2 = 2 * IH // P
WOBLK = 4096 // P

ENGS = ("sync", "vector", "scalar", "tensor", "gpsimd")


class Op:
    __slots__ = ("eng", "idx", "sem_val", "dma", "pend_inc", "label", "slot", "slot_val")

    def __init__(self, eng, idx, dma):
        self.eng = eng
        self.idx = idx
        self.dma = dma
        self.sem_val = None
        self.pend_inc = 0


class Buf:
    __slots__ = ("ap", "wop", "rops", "name")

    def __init__(self, ap, name=""):
        self.ap = ap
        self.wop = None
        self.rops = []
        self.name = name


class K:

    NSLOT = 16

    def __init__(self, nc):
        self.nc = nc
        self.ops = {e: [] for e in ENGS}
        self.clos = {e: [] for e in ENGS}
        self.milestones = {e: 0 for e in ENGS}
        self.dma_emitted = {e: 0 for e in ENGS}
        self.slot_cnt = {e: [0] * self.NSLOT for e in ENGS}
        self.waited = {e: {} for e in ENGS}
        self.sems = {}

    def _milestone(self, op):
        if op.dma:
            return (op.eng, "d", op.slot), op.slot_val
        if op.sem_val is None:
            last = self.ops[op.eng][-1]
            while last.dma:
                for o in reversed(self.ops[op.eng]):
                    if not o.dma and o.idx >= op.idx:
                        last = o
                        break
                else:
                    last = None
                break
            if last is None:
                lastd = self.ops[op.eng][-1]
                return (op.eng, "d", lastd.slot), lastd.slot_val
            if last.sem_val is None:
                self.milestones[op.eng] += 1
                last.sem_val = self.milestones[op.eng]
                last.pend_inc += 1
            return (op.eng, "c"), last.sem_val
        return (op.eng, "c"), op.sem_val

    def _wait(self, eng, key, val):
        if self.waited[eng].get(key, 0) >= val:
            return
        self.waited[eng][key] = val
        self.clos[eng].append(("wait", key, val))

    def _dep(self, eng, op):
        key, val = self._milestone(op)
        self._wait(eng, key, val)

    def emit(self, eng, fn, reads=(), writes=(), dma=False, label=""):
        for b in reads:
            if b.wop is not None:
                self._dep(eng, b.wop)
        for b in writes:
            if b.wop is not None and b.wop.eng != eng:
                self._dep(eng, b.wop)
            for r in b.rops:
                if r.eng != eng:
                    self._dep(eng, r)
        op = Op(eng, len(self.ops[eng]), dma)
        op.label = label
        if dma:
            i = self.dma_emitted[eng]
            slot = i % self.NSLOT
            if self.slot_cnt[eng][slot] > 0:
                self._wait(eng, (eng, "d", slot), self.slot_cnt[eng][slot])
            self.slot_cnt[eng][slot] += 16
            op.slot = slot
            op.slot_val = self.slot_cnt[eng][slot]
            self.dma_emitted[eng] += 1
        self.ops[eng].append(op)
        self.clos[eng].append(("op", op, fn))
        for b in reads:
            b.rops.append(op)
        for b in writes:
            b.wop = op
            b.rops = []
        return op

    def dma(self, eng, out, in_, rbufs=(), wbufs=(), **kw):
        reads = list(rbufs)
        writes = list(wbufs)
        if isinstance(in_, Buf):
            reads.append(in_)
            in_ap = in_.ap
        else:
            in_ap = in_
        if isinstance(out, Buf):
            writes.append(out)
            out_ap = out.ap
        else:
            out_ap = out
        return self.emit(eng, lambda e: e.dma_start(out=out_ap, in_=in_ap, **kw),
                         reads=reads, writes=writes, dma=True)

    def barrier(self):
        marks = []
        for en in ENGS:
            for s in range(self.NSLOT):
                if self.slot_cnt[en][s]:
                    marks.append(((en, "d", s), self.slot_cnt[en][s]))
            for o in reversed(self.ops[en]):
                if not o.dma:
                    key, val = self._milestone(o)
                    marks.append((key, val))
                    break
        for en in ENGS:
            for key, val in marks:
                if key[0] != en:
                    self._wait(en, key, val)
                elif len(key) == 3:
                    self._wait(en, key, val)

    def flush(self, block):
        makers = {"sync": block.sync, "vector": block.vector,
                  "scalar": block.scalar, "tensor": block.tensor,
                  "gpsimd": block.gpsimd}
        for en in ENGS:
            items = self.clos[en]
            if not items:
                continue

            def body(e, items=items, en=en):
                for it in items:
                    if it[0] == "wait":
                        _, key, val = it
                        e.wait_ge(self.sems[key], val)
                    else:
                        _, op, fn = it
                        inst = fn(e)
                        if op.dma:
                            inst.then_inc(self.sems[(en, "d", op.slot)], 16)
                        if op.pend_inc:
                            inst.then_inc(self.sems[(en, "c")], op.pend_inc)
            makers[en](body)


def build_program(t_tokens=8192, tc=512, taps=()):
    nch = t_tokens // tc
    nbig = 4
    cpb = nch // nbig
    tb = cpb * tc
    nc = bass.Bass("TRN2", target_bir_lowering=False)

    def din(name, shape, dt=F32):
        return nc.dram_tensor(name, shape, dt, kind="ExternalInput").ap()

    x_full = din("x_full", [t_tokens, DM])
    x_halfi = din("x_halfi", [t_tokens, DMH])
    w_in_x_T = din("w_in_x_T", [DM, DIH])
    w_in_z_T = din("w_in_z_T", [DM, DIH])
    conv_w = din("conv_w", [DIH, 4])
    conv_b = din("conv_b", [DIH, 1])
    w_xp_T = din("w_xp_T", [DIH, 160])
    w_dt_T = din("w_dt_T", [RK, DIH])
    dt_b = din("dt_b", [DIH, 1])
    a_log = din("a_log", [DIH, NST])
    d_vec = din("d_vec", [DIH, 1])
    w_out_T = din("w_out_T", [4096, DMH])
    gu_packed = din("gu_packed", [GU, DM // 8], I32)
    gu_scales_T = din("gu_scales_T", [GU, DM // 128], U16)
    dn_packed = din("dn_packed", [DMH, 2 * IH // 8], I32)
    dn_scales_T = din("dn_scales_T", [DMH, 2 * IH // 128], U16)
    j_in = din("j_in", [1, 1], U32)
    nonce_in = din("nonce_in", [1, 4], U32)
    ident_in = din("ident", [128, 128], BF16)
    constv_in = din("constv", [128, 3])

    out_half = nc.dram_tensor("out_half", [t_tokens, DMH], F32, kind="ExternalOutput").ap()

    def dloc(name, shape, dt):
        return Buf(nc.dram_tensor(name, shape, dt, kind="Internal").ap(), name)

    xn_dr = dloc("xn_dr", [DM, t_tokens], BF16)
    xt_dr = dloc("xt_dr", [DIH, t_tokens], F32)
    zs_dr = dloc("zs_dr", [DIH, t_tokens], F32)
    wout_dr = dloc("wout_dr", [WOBLK, 128, DMH], BF16)
    wgu_dr = dloc("wgu_dr", [MBLK, 128, GU], BF16)
    wd_dr = dloc("wd_dr", [IBLK2, 128, DMH], BF16)
    yffn_dr = dloc("yffn_dr", [t_tokens, DMH], F32)

    def dsh(name, shape, dt):
        return nc.dram_tensor(name, shape, dt, kind="Internal", addr_space="Shared").ap()

    projsh = [dsh(f"projsh{k}", [160, t_tokens], F32) for k in range(2)]
    ysh = [dsh(f"ysh{k}", [DIH, t_tokens], BF16) for k in range(2)]
    hsh = [dsh(f"hsh{k}", [IH, t_tokens], BF16) for k in range(2)]
    sqsh = [dsh(f"sqsh{k}", [t_tokens // P, P], F32) for k in range(2)]
    NFLAG = 2 * nch + 2 * nbig
    flagsh = [dsh(f"flagsh{k}", [NFLAG, 16], U32) for k in range(2)]
    barsh = [dsh(f"barsh{k}", [1, 4], U32) for k in range(2)]
    projsh_b = Buf(None, "projsh")
    ysh_b = Buf(None, "ysh")
    hsh_b = Buf(None, "hsh")
    sqsh_b = Buf(None, "sqsh")

    tap_out = {}
    tap_shapes = {
        "xn": ([DM, t_tokens], BF16), "xt": ([DIH, t_tokens], F32),
        "zs": ([DIH, t_tokens], F32), "proj": ([160, t_tokens], F32),
        "dt": ([DIH, t_tokens], F32), "y": ([DIH, t_tokens], F32),
        "wgu": ([MBLK, 128, GU], BF16), "wd": ([IBLK2, 128, DMH], BF16),
        "s": ([t_tokens, DMH], F32), "yffn": ([t_tokens, DMH], F32),
        "dqx": ([128, DM], BF16), "dqn": ([128, DM], U16),
        "hh": ([2, IH, t_tokens], BF16), "yy": ([2, DIH, t_tokens], BF16),
        "wo": ([WOBLK, 128, DMH], BF16),
    }
    for t in taps:
        sh, dt = tap_shapes[t]
        tap_out[t] = nc.dram_tensor("tap_" + t, sh, dt, kind="ExternalOutput").ap()

    def sbraw(name, shape, dt):
        return nc.alloc_sbuf_tensor(name, shape, dt).ap()

    def sb(name, shape, dt):
        return Buf(sbraw(name, shape, dt), name)

    ident_s = sb("ident_s", [128, 128], BF16)
    j_s = sb("j_s", [1, 1], U32)
    nonce_s = sb("nonce_s", [1, 4], U32)
    bar_s = sb("bar_s", [1, 4], U32)
    zero16 = sb("zero16", [1, 16], U32)
    one16 = sb("one16", [1, 16], U32)
    flag_s = sb("flag_s", [1, 1], U32)
    ones_bf = sb("ones_bf", [1, 128], BF16)
    ones_f = sb("ones_f", [128, 128], F32)

    a_sc = [sb(f"a_sc{i}", [P, NST], F32) for i in range(DBLK)]
    cw_s = [sb(f"cw_s{i}", [P, 4], F32) for i in range(DBLK)]
    cb_s = [sb(f"cb_s{i}", [P, 1], F32) for i in range(DBLK)]
    ncb_s = [sb(f"ncb_s{i}", [P, 1], F32) for i in range(DBLK)]
    dtb_s = [sb(f"dtb_s{i}", [P, 1], F32) for i in range(DBLK)]
    dvec_s = [sb(f"dvec_s{i}", [P, 1], F32) for i in range(DBLK)]
    hstate = [sb(f"hstate{i}", [P, NST], F32) for i in range(DBLK)]
    halo = [sb(f"halo{i}", [P, 3], F32) for i in range(DBLK)]
    wdt_bf = sb("wdt_bf", [RK, DIH], BF16)
    wxp_stg = sb("wxp_stg", [P, 160], F32)
    wxp_bf = [sb(f"wxp_bf{i}", [P, 160], BF16) for i in range(DBLK)]
    wstg = sb("wstg", [P, DIH], F32)
    ln_m = sb("ln_m", [P, 1], F32)
    ln_ssq = sb("ln_ssq", [P, 1], F32)
    ln_rs = sb("ln_rs", [P, 1], F32)
    sq_p = sb("sq_p", [P, 1], F32)
    sq0 = sb("sq0", [P, 1], F32)
    sq1 = sb("sq1", [P, 1], F32)
    rstd3 = sb("rstd3", [P, 1], F32)
    constv = sb("constv_s", [P, 3], F32)

    ARENA_HW = 94000
    arena_ap = sbraw("arena", [P, ARENA_HW], BF16)
    _dt_bytes = {F32: 4, BF16: 2, F16: 2, I32: 4, U16: 2, U32: 4}

    class Arena:
        def __init__(self):
            self.cur = 0

        def reset(self):
            self.cur = 0

        def al(self, name, cols, dt, parts=P):
            b = _dt_bytes[dt]
            off = (self.cur + 3) & ~3
            w_hw = (cols * b + 1) // 2
            assert off + w_hw * 2 <= ARENA_HW * 2, f"arena overflow at {name}"
            view = arena_ap[:parts, off // 2: off // 2 + w_hw]
            self.cur = off + w_hw * 2
            if dt != BF16:
                view = view.bitcast(dt)
                if view.shape[1] != cols:
                    view = view[:, :cols]
            else:
                view = view[:, :cols]
            return Buf(view, name)

    AR = Arena()

    wstg2 = AR.al("wstg2", DMH, F32)
    wo_bf = AR.al("wo_bf", DMH, BF16)
    DQCH_GU = 1024
    DQCH_DN = 1408
    DQCH = 1408
    dq_pk = AR.al("dq_pk", 704, I32)
    dq_nib = AR.al("dq_nib", DQCH, U16)
    dq_t1 = AR.al("dq_t1", DQCH, U16)
    dq_t2 = AR.al("dq_t2", DQCH, U16)
    dq_t3 = AR.al("dq_t3", DQCH, U16)
    dq_sc16 = AR.al("dq_sc16", 44, U16)
    dq_sc = AR.al("dq_sc", 44, F32)
    dq_srep = AR.al("dq_srep", DQCH, F32)
    dq_out = AR.al("dq_out", DQCH, BF16)
    dq_tpo = [AR.al(f"dq_tpo{i}", 128, BF16) for i in range(2)]

    AR.reset()
    win_sb = [AR.al(f"win{i}", DIH, BF16) for i in range(MBLK)]
    xtok = [AR.al(f"xtok{i}", DM, F32) for i in range(2)]
    xc_t = [AR.al(f"xc_t{i}", DM, F32) for i in range(2)]
    xn_tok = [AR.al(f"xn_tok{i}", DM, BF16) for i in range(2)]
    ln_sq = AR.al("ln_sq", DM, F32)
    xn_fm = [AR.al(f"xn_fm{i}", tc, BF16) for i in range(MBLK)]
    xl_t = [AR.al(f"xl_t{i}", tc + 3, F32) for i in range(2)]
    cv_t = [AR.al(f"cv_t{i}", tc, F32) for i in range(2)]
    sil_e = [AR.al(f"sil_e{i}", tc, F32) for i in range(2)]
    sil_r = [AR.al(f"sil_r{i}", tc, F32) for i in range(2)]
    xt_t = [AR.al(f"xt_t{i}", tc, F32) for i in range(2)]
    xt_bf = [AR.al(f"xt_bf{i}", tc, BF16) for i in range(2)]
    z_t = [AR.al(f"z_t{i}", tc, F32) for i in range(2)]
    zs_t = [AR.al(f"zs_t{i}", tc, F32) for i in range(2)]
    pj1_s = AR.al("pj1_s", tc, F32)
    pj2_s = AR.al("pj2_s", tc, F32, parts=32)

    AR.reset()
    pj1a = AR.al("pj1a", tc, F32)
    pj1b = AR.al("pj1b", tc, F32)
    pj1bf = AR.al("pj1bf", tc, BF16)
    pj2a = AR.al("pj2a", tc, F32, parts=32)
    pj2b = AR.al("pj2b", tc, F32, parts=32)
    pjbc_bf = AR.al("pjbc_bf", tc, BF16, parts=32)
    rowstg = [AR.al(f"rowstg{i}", tc, BF16, parts=1) for i in range(2)]
    bb_t = [AR.al(f"bb_t{i}", tc, F32) for i in range(NST)]
    cb_t = [AR.al(f"cb_t{i}", tc, F32) for i in range(NST)]
    dt_s = AR.al("dt_s", tc, F32)
    sp_r = AR.al("sp_r", tc, F32)
    sp_a = AR.al("sp_a", tc, F32)
    w_s = AR.al("w_s", tc, F32)
    da_t = [AR.al(f"da_t{i}", tc, F32) for i in range(2)]
    wb_t = [AR.al(f"wb_t{i}", tc, F32) for i in range(2)]
    h_slab = AR.al("h_slab", 8 * tc, F32)
    hc_slab = AR.al("hc_slab", 8 * tc, F32)
    ypart = [AR.al(f"ypart{i}", tc, F32) for i in range(2)]
    y_s = AR.al("y_s", tc, F32)
    yg_slab = AR.al("yg_slab", DBLK * tc, BF16)
    xt_rd = AR.al("xt_rd", tc, F32)
    zs_rd = AR.al("zs_rd", tc, F32)

    AR.reset()
    xn3 = [AR.al(f"xn3_{i}", tb, BF16) for i in range(MBLK)]
    ph3_lo_hw = AR.cur
    AR.reset()
    wd_sb = [AR.al(f"wd{i}", DMH, BF16) for i in range(IBLK2)]
    AR.reset()
    wout_sb = [AR.al(f"wout{i}", DMH, BF16) for i in range(WOBLK)]
    AR.cur = max(ph3_lo_hw, IBLK2 * 2048 + 64, WOBLK * 2048 + 64, AR.cur)
    g_e = AR.al("g_e", 512, F32)
    g_cp = AR.al("g_cp", 512, F32)
    g_r = AR.al("g_r", 512, F32)
    g_sg = AR.al("g_sg", 512, F32)
    u_cp = AR.al("u_cp", 512, F32)
    h_bf = [AR.al(f"h_bf{i}", 512, BF16) for i in range(2)]
    wgu_slab = [AR.al(f"wgu_slab{i}", MBLK * 128, BF16) for i in range(2)]
    h_slab3 = [AR.al(f"h_slab3{i}", IBLK2 * 128, BF16) for i in range(1)]
    yf_t = [AR.al(f"yf_t{i}", DMH, F32) for i in range(2)]
    sqsc = AR.al("sqsc", DMH, F32)
    y_slab = [AR.al(f"y_slab{i}", 32 * 128, BF16) for i in range(2)]
    s_ev = [AR.al(f"s_ev{i}", DMH, F32) for i in range(2)]
    yfr = AR.al("yfr", DMH, F32)
    g3e = AR.al("g3e", DMH, F32)
    g3r = AR.al("g3r", DMH, F32)
    xh_t = AR.al("xh_t", DMH, F32)
    o_t = [AR.al(f"o_t{i}", DMH, F32) for i in range(2)]

    import contextlib
    with contextlib.ExitStack() as _stack:
        ent = _stack.enter_context
        cs_sync = ent(nc.semaphore("cs_sync")); cs_vec = ent(nc.semaphore("cs_vec"))
        cs_sca = ent(nc.semaphore("cs_sca")); cs_ten = ent(nc.semaphore("cs_ten"))
        cs_gps = ent(nc.semaphore("cs_gps"))
        sem_poll = ent(nc.semaphore("s_poll")); sem_gq = ent(nc.semaphore("s_gq"))
        dpool = {}
        for qe in ("sync", "gpsimd"):
            for s in range(K.NSLOT):
                dpool[(qe, s)] = ent(nc.semaphore(f"dq_{qe}_{s}"))
        ps0_h = ent(nc.psum_tensor("ps0", [P, 512], F32))
        ps1_h = ent(nc.psum_tensor("ps1", [P, 512], F32))
        ps2_h = ent(nc.psum_tensor("ps2", [P, 512], F32))
        ps3_h = ent(nc.psum_tensor("ps3", [P, 512], F32))
        ps4_h = ent(nc.psum_tensor("ps4", [P, 512], F32))
        ps5_h = ent(nc.psum_tensor("ps5", [P, 512], F32))
        pst0_h = ent(nc.psum_tensor("pst0", [P, 128], BF16))
        pst1_h = ent(nc.psum_tensor("pst1", [P, 128], BF16))
        block = ent(nc.Block())
        kk = K(nc)
        for en, ch in (("sync", cs_sync), ("vector", cs_vec),
                       ("scalar", cs_sca), ("tensor", cs_ten),
                       ("gpsimd", cs_gps)):
            kk.sems[(en, "c")] = ch
        for (qe, s), h in dpool.items():
            kk.sems[(qe, "d", s)] = h
        psb = [Buf(h.ap(), f"ps{i}") for i, h in enumerate(
            [ps0_h, ps1_h, ps2_h, ps3_h, ps4_h, ps5_h])]
        pstb = [Buf(h.ap(), f"pst{i}") for i, h in enumerate([pst0_h, pst1_h])]

        jr = nc.gpsimd.alloc_register("jr")
        r_fl = nc.gpsimd.alloc_register("r_fl")
        r_nd = nc.gpsimd.alloc_register("r_nd")
        r_a = nc.gpsimd.alloc_register("r_a")
        r_b = nc.gpsimd.alloc_register("r_b")
        r_pc = nc.gpsimd.alloc_register("r_pc")
        gqn = [0]

        def g_branch_write(dst0_ap, dst1_ap, src_ap):
            def fn(g):
                with g.If_eq(jr, 0):
                    g.dma_start(out=dst0_ap, in_=src_ap).then_inc(sem_gq, 16)
                with g.Else():
                    g.dma_start(out=dst1_ap, in_=src_ap).then_inc(sem_gq, 16)
                gqn[0] += 16
                g.wait_ge(sem_gq, gqn[0])
                return g.engine_nop()
            return fn

        def g_flag(fid):
            return g_branch_write(flagsh[0][fid:fid + 1, :],
                                  flagsh[1][fid:fid + 1, :], one16.ap)

        def g_poll(fid):
            def fn(g):
                for k in range(2):
                    g.reg_mov(r_nd, 1)
                    with g.While(r_nd):
                        g.reg_add(r_pc, r_pc, 16)
                        g.dma_start(out=flag_s.ap, in_=flagsh[k][fid:fid + 1, 0:1]).then_inc(sem_poll, 16)
                        g.wait_ge(sem_poll, r_pc)
                        g.reg_load(r_fl, flag_s.ap[:1, :1])
                        g.reg_alu(r_nd, r_fl, 0, ALU.is_equal)
                return g.engine_nop()
            return fn

        nc.const_aps.register(F32, LN_EPS, constv.ap[:, 0:1])
        nc.const_aps.register(F32, RMS_EPS, constv.ap[:, 1:2])
        nc.const_aps.register(F32, 1.0, constv.ap[:, 2:3])
        kk.dma("sync", constv, constv_in)
        kk.dma("sync", ident_s, ident_in)
        kk.dma("sync", j_s, j_in)
        kk.dma("sync", nonce_s, nonce_in)
        for i in range(DBLK):
            kk.dma("sync", a_sc[i], a_log[i * P:(i + 1) * P, :])
            kk.dma("sync", cw_s[i], conv_w[i * P:(i + 1) * P, :])
            kk.dma("sync", cb_s[i], conv_b[i * P:(i + 1) * P, :])
            kk.dma("sync", dtb_s[i], dt_b[i * P:(i + 1) * P, :])
            kk.dma("sync", dvec_s[i], d_vec[i * P:(i + 1) * P, :])
        wdt_f32 = Buf(wstg.ap[:RK, :], "wdt_f32")
        kk.dma("sync", wdt_f32, w_dt_T)

        kk.emit("vector", lambda e: e.memset(zero16.ap, 0), writes=[zero16])
        kk.emit("vector", lambda e: e.memset(one16.ap, 1), writes=[one16])
        kk.emit("vector", lambda e: e.memset(ones_bf.ap, 1.0), writes=[ones_bf])
        kk.emit("vector", lambda e: e.memset(ones_f.ap, 1.0), writes=[ones_f])
        for i in range(DBLK):
            kk.emit("scalar", lambda e, i=i: e.activation(out=a_sc[i].ap, in_=a_sc[i].ap, func=AF.Exp),
                    reads=[a_sc[i]], writes=[a_sc[i]])
            kk.emit("vector", lambda e, i=i: e.tensor_scalar(out=a_sc[i].ap, in0=a_sc[i].ap,
                    scalar1=-1.0, scalar2=None, op0=ALU.mult), reads=[a_sc[i]], writes=[a_sc[i]])
            kk.emit("vector", lambda e, i=i: e.tensor_scalar(out=ncb_s[i].ap, in0=cb_s[i].ap,
                    scalar1=-1.0, scalar2=None, op0=ALU.mult), reads=[cb_s[i]], writes=[ncb_s[i]])
            kk.emit("vector", lambda e, i=i: e.memset(halo[i].ap, 0.0), writes=[halo[i]])
            kk.emit("vector", lambda e, i=i: e.memset(hstate[i].ap, 0.0), writes=[hstate[i]])
        kk.emit("scalar", lambda e: e.copy(out=wdt_bf.ap, in_=wdt_f32.ap),
                reads=[wdt_f32], writes=[wdt_bf])
        for i in range(DBLK):
            kk.dma("sync", wxp_stg, w_xp_T[i * P:(i + 1) * P, :])
            kk.emit("scalar", lambda e, i=i: e.copy(out=wxp_bf[i].ap, in_=wxp_stg.ap),
                    reads=[wxp_stg], writes=[wxp_bf[i]])

        def init_fn(g):
            g.reg_load(jr, j_s.ap[:1, :1])
            g.reg_mov(r_pc, 0)
            with g.If_eq(jr, 0):
                for f in range(NFLAG):
                    g.dma_start(out=flagsh[0][f:f + 1, :], in_=zero16.ap).then_inc(sem_gq, 16)
                g.dma_start(out=barsh[0], in_=nonce_s.ap).then_inc(sem_gq, 16)
            with g.Else():
                for f in range(NFLAG):
                    g.dma_start(out=flagsh[1][f:f + 1, :], in_=zero16.ap).then_inc(sem_gq, 16)
                g.dma_start(out=barsh[1], in_=nonce_s.ap).then_inc(sem_gq, 16)
            gqn[0] += 16 * (NFLAG + 1)
            g.wait_ge(sem_gq, gqn[0])
            for barx in (barsh[0], barsh[1]):
                g.reg_mov(r_nd, 1)
                with g.While(r_nd):
                    g.reg_add(r_pc, r_pc, 16)
                    g.dma_start(out=bar_s.ap, in_=barx).then_inc(sem_poll, 16)
                    g.wait_ge(sem_poll, r_pc)
                    g.reg_mov(r_nd, 0)
                    for kq in range(4):
                        g.reg_load(r_a, bar_s.ap[:1, kq:kq + 1])
                        g.reg_load(r_b, nonce_s.ap[:1, kq:kq + 1])
                        g.reg_alu(r_a, r_a, r_b, ALU.not_equal)
                        g.reg_alu(r_nd, r_nd, r_a, ALU.bitwise_or)
            return g.engine_nop()

        kk.emit("gpsimd", init_fn, reads=[j_s, nonce_s, zero16, one16], writes=[bar_s])

        kk.dma("gpsimd", wout_dr.ap.rearrange("b p c -> (b p) c"), w_out_T,
               wbufs=[wout_dr])

        def emit_dequant(src_packed, src_scales_T, n_rows_blk, n_in, dst_slab, ch_w):
            n_ch = n_in // ch_w
            pkv = Buf(dq_pk.ap[:, :n_in // 8], "pkv")
            scv16 = Buf(dq_sc16.ap[:, :n_in // 128], "scv16")
            scv = Buf(dq_sc.ap[:, :n_in // 128], "scv")
            nibv = Buf(dq_nib.ap[:, :ch_w], "nibv")
            t1 = Buf(dq_t1.ap[:, :ch_w], "t1")
            t2 = Buf(dq_t2.ap[:, :ch_w], "t2")
            t3 = Buf(dq_t3.ap[:, :ch_w], "t3")
            srep = Buf(dq_srep.ap[:, :ch_w], "srep")
            dqo = Buf(dq_out.ap[:, :ch_w], "dqo")
            for r in range(n_rows_blk):
                kk.barrier()
                kk.dma("sync", pkv, src_packed[r * P:(r + 1) * P, :])
                kk.dma("sync", scv16, src_scales_T[r * P:(r + 1) * P, :])
                kk.emit("vector", lambda e, scv=scv, scv16=scv16: e.tensor_copy(
                    out=scv.ap, in_=scv16.ap.bitcast(F16)), reads=[scv16], writes=[scv])
                for chi in range(n_ch):
                    c0 = chi * ch_w
                    pk16 = pkv.ap.bitcast(U16)[:, c0 // 4:(c0 + ch_w) // 4]
                    for kq in range(4):
                        outv = nibv.ap.rearrange("p (j four) -> p j four", four=4)[:, :, kq]
                        kk.emit("vector", lambda e, kq=kq, outv=outv, pk16=pk16: e.tensor_scalar(
                            out=outv, in0=pk16, scalar1=4 * kq, scalar2=15,
                            op0=ALU.logical_shift_right, op1=ALU.bitwise_and),
                            reads=[pkv], writes=[nibv])
                    kk.emit("vector", lambda e, t1=t1, nibv=nibv: e.tensor_scalar(
                        out=t1.ap, in0=nibv.ap, scalar1=7, scalar2=6,
                        op0=ALU.bitwise_and, op1=ALU.logical_shift_left),
                        reads=[nibv], writes=[t1])
                    kk.emit("vector", lambda e, t2=t2, nibv=nibv: e.tensor_scalar(
                        out=t2.ap, in0=nibv.ap, scalar1=8, scalar2=12,
                        op0=ALU.bitwise_and, op1=ALU.logical_shift_left),
                        reads=[nibv], writes=[t2])
                    kk.emit("vector", lambda e, t1=t1, t2=t2: e.scalar_tensor_tensor(
                        out=t1.ap, in0=t1.ap, scalar=16128, in1=t2.ap,
                        op0=ALU.add, op1=ALU.add), reads=[t1, t2], writes=[t1])
                    kk.emit("vector", lambda e, t3=t3, nibv=nibv: e.tensor_scalar(
                        out=t3.ap, in0=nibv.ap, scalar1=1, scalar2=None,
                        op0=ALU.bitwise_and), reads=[nibv], writes=[t3])
                    kk.emit("vector", lambda e, t3=t3: e.tensor_scalar(
                        out=t3.ap, in0=t3.ap, scalar1=16128, scalar2=None,
                        op0=ALU.mult), reads=[t3], writes=[t3])
                    kk.emit("vector", lambda e, t3=t3, t2=t2: e.tensor_tensor(
                        out=t3.ap, in0=t3.ap, in1=t2.ap, op=ALU.add),
                        reads=[t3, t2], writes=[t3])
                    kk.emit("vector", lambda e, nibv=nibv: e.tensor_scalar(
                        out=nibv.ap, in0=nibv.ap, scalar1=6, scalar2=None,
                        op0=ALU.bitwise_and), reads=[nibv], writes=[nibv])
                    kk.emit("vector", lambda e, t2=t2, nibv=nibv, t1=t1, t3=t3: e.select(
                        out=t2.ap, mask=nibv.ap, on_true=t1.ap, on_false=t3.ap),
                        reads=[nibv, t1, t3], writes=[t2])
                    for gi in range(ch_w // 128):
                        gg = c0 // 128 + gi
                        kk.emit("vector", lambda e, gi=gi, gg=gg, srep=srep, scv=scv:
                                e.tensor_scalar(
                            out=srep.ap[:, gi * 128:(gi + 1) * 128], in0=ones_f.ap,
                            scalar1=scv.ap[:, gg:gg + 1], scalar2=None, op0=ALU.mult),
                            reads=[ones_f, scv], writes=[srep])
                    kk.emit("vector", lambda e, dqo=dqo, t2=t2, srep=srep: e.tensor_tensor(
                        out=dqo.ap, in0=t2.ap.bitcast(BF16), in1=srep.ap, op=ALU.mult),
                        reads=[t2, srep], writes=[dqo])
                    if "dqx" in tap_out and dst_slab is wgu_dr and r == 0:
                        kk.dma("gpsimd", tap_out["dqx"][:, c0:c0 + ch_w], dqo)
                        kk.dma("gpsimd", tap_out["dqn"][:, c0:c0 + ch_w], nibv)
                    for ib in range(ch_w // 128):
                        gib = c0 // 128 + ib
                        pt = pstb[ib % 2]
                        po = dq_tpo[ib % 2]
                        kk.emit("tensor", lambda e, ib=ib, pt=pt, dqo=dqo: e.transpose(
                            pt.ap, dqo.ap[:, ib * 128:(ib + 1) * 128], ident_s.ap),
                            reads=[dqo, ident_s], writes=[pt])
                        kk.emit("scalar", lambda e, pt=pt, po=po: e.copy(out=po.ap, in_=pt.ap),
                                reads=[pt], writes=[po])
                        kk.dma("gpsimd", dst_slab.ap[gib, :, r * P:(r + 1) * P], po,
                               rbufs=[po], wbufs=[dst_slab])

        emit_dequant(gu_packed, gu_scales_T, OBLK, DM, wgu_dr, 1024)
        emit_dequant(dn_packed, dn_scales_T, DMH // P, 2 * IH, wd_dr, 1408)
        if "wgu" in tap_out:
            kk.dma("gpsimd", tap_out["wgu"], wgu_dr)
        if "wd" in tap_out:
            kk.dma("gpsimd", tap_out["wd"], wd_dr)

        kk.barrier()
        for part in range(2):
            w_src = w_in_x_T if part == 0 else w_in_z_T
            for i in range(MBLK):
                kk.dma("sync", wstg, w_src[i * P:(i + 1) * P, :])
                kk.emit("scalar", lambda e, i=i: e.copy(out=win_sb[i].ap, in_=wstg.ap),
                        reads=[wstg], writes=[win_sb[i]])

            for c in range(nch):
                t0 = c * tc
                if part == 0:
                    for tt in range(tc // P):
                        xk = xtok[tt % 2]
                        xcv = xc_t[tt % 2]
                        xnv = xn_tok[tt % 2]
                        kk.dma("sync", xk, x_full[t0 + tt * P: t0 + (tt + 1) * P, :])
                        kk.emit("vector", lambda e, xk=xk: e.tensor_reduce(
                            out=ln_m.ap, in_=xk.ap, axis=AX.X, op=ALU.add),
                            reads=[xk], writes=[ln_m])
                        kk.emit("vector", lambda e: e.tensor_scalar(out=ln_m.ap,
                                in0=ln_m.ap, scalar1=1.0 / DM, scalar2=None, op0=ALU.mult),
                                reads=[ln_m], writes=[ln_m])
                        kk.emit("vector", lambda e, xk=xk, xcv=xcv: e.tensor_scalar(
                            out=xcv.ap, in0=xk.ap, scalar1=ln_m.ap, scalar2=None,
                            op0=ALU.subtract), reads=[xk, ln_m], writes=[xcv])
                        kk.emit("scalar", lambda e, xcv=xcv: e.activation(
                            out=ln_sq.ap, in_=xcv.ap, func=AF.Square, accum_out=ln_ssq.ap),
                            reads=[xcv], writes=[ln_sq, ln_ssq])
                        kk.emit("scalar", lambda e: e.activation(out=ln_rs.ap,
                                in_=ln_ssq.ap, func=AF.Ln, scale=1.0 / DM, bias=LN_EPS),
                                reads=[ln_ssq, constv], writes=[ln_rs])
                        kk.emit("scalar", lambda e: e.activation(out=ln_rs.ap,
                                in_=ln_rs.ap, func=AF.Exp, scale=-0.5),
                                reads=[ln_rs], writes=[ln_rs])
                        kk.emit("vector", lambda e, xcv=xcv, xnv=xnv: e.tensor_scalar(
                            out=xnv.ap, in0=xcv.ap, scalar1=ln_rs.ap, scalar2=None,
                            op0=ALU.mult), reads=[xcv, ln_rs], writes=[xnv])
                        for mb in range(MBLK):
                            pt = pstb[mb % 2]
                            kk.emit("tensor", lambda e, mb=mb, pt=pt, xnv=xnv: e.transpose(
                                pt.ap, xnv.ap[:, mb * 128:(mb + 1) * 128], ident_s.ap),
                                reads=[xnv, ident_s], writes=[pt])
                            kk.emit("scalar", lambda e, mb=mb, tt=tt, pt=pt: e.copy(
                                out=xn_fm[mb].ap[:, tt * P:(tt + 1) * P], in_=pt.ap),
                                reads=[pt], writes=[xn_fm[mb]])
                    for mb in range(MBLK):
                        kk.dma("gpsimd", xn_dr.ap[mb * P:(mb + 1) * P, t0:t0 + tc],
                               xn_fm[mb], wbufs=[xn_dr])
                else:
                    for mb in range(MBLK):
                        kk.dma("sync", xn_fm[mb], xn_dr.ap[mb * P:(mb + 1) * P, t0:t0 + tc],
                               rbufs=[xn_dr])

                for eb in range(DBLK):
                    pm = psb[eb % 2]
                    for k in range(MBLK):
                        kk.emit("tensor", lambda e, k=k, eb=eb, pm=pm: e.matmul(
                            pm.ap, win_sb[k].ap[:, eb * 128:(eb + 1) * 128], xn_fm[k].ap,
                            start=(k == 0), stop=(k == MBLK - 1)),
                            reads=[win_sb[k], xn_fm[k]], writes=[pm])
                    if part == 0:
                        xl = xl_t[eb % 2]
                        cv = cv_t[eb % 2]
                        se_ = sil_e[eb % 2]
                        sr_ = sil_r[eb % 2]
                        xt_ = xt_t[eb % 2]
                        xb_ = xt_bf[eb % 2]
                        kk.emit("vector", lambda e, xl=xl, eb=eb: e.tensor_copy(
                            out=xl.ap[:, 0:3], in_=halo[eb].ap), reads=[halo[eb]], writes=[xl])
                        kk.emit("scalar", lambda e, xl=xl, pm=pm: e.copy(
                            out=xl.ap[:, 3:3 + tc], in_=pm.ap), reads=[pm], writes=[xl])
                        kk.emit("vector", lambda e, xl=xl, cv=cv, eb=eb: e.tensor_scalar(
                            out=cv.ap, in0=xl.ap[:, 0:tc], scalar1=cw_s[eb].ap[:, 0:1],
                            scalar2=None, op0=ALU.mult), reads=[xl, cw_s[eb]], writes=[cv])
                        for kq in range(1, 4):
                            kk.emit("vector", lambda e, xl=xl, cv=cv, eb=eb, kq=kq:
                                    e.scalar_tensor_tensor(out=cv.ap, in0=xl.ap[:, kq:kq + tc],
                                    scalar=cw_s[eb].ap[:, kq:kq + 1], in1=cv.ap,
                                    op0=ALU.mult, op1=ALU.add),
                                    reads=[xl, cw_s[eb], cv], writes=[cv])
                        kk.emit("vector", lambda e, xl=xl, eb=eb: e.tensor_copy(
                            out=halo[eb].ap, in_=xl.ap[:, tc:tc + 3]),
                            reads=[xl], writes=[halo[eb]])
                        kk.emit("scalar", lambda e, cv=cv, se_=se_, eb=eb: e.activation(
                            out=se_.ap, in_=cv.ap, func=AF.Exp, scale=-1.0, bias=ncb_s[eb].ap),
                            reads=[cv, ncb_s[eb]], writes=[se_])
                        kk.emit("vector", lambda e, se_=se_, sr_=sr_: e.tensor_scalar(
                            out=sr_.ap, in0=se_.ap, scalar1=1.0, scalar2=None, op0=ALU.add),
                            reads=[se_], writes=[sr_])
                        kk.emit("vector", lambda e, sr_=sr_: e.reciprocal(out=sr_.ap, in_=sr_.ap),
                                reads=[sr_], writes=[sr_])
                        kk.emit("vector", lambda e, cv=cv, sr_=sr_, xt_=xt_, eb=eb:
                                e.scalar_tensor_tensor(out=xt_.ap, in0=cv.ap,
                                scalar=cb_s[eb].ap, in1=sr_.ap, op0=ALU.add, op1=ALU.mult),
                                reads=[cv, cb_s[eb], sr_], writes=[xt_])
                        kk.emit("vector", lambda e, xt_=xt_, xb_=xb_: e.tensor_copy(
                            out=xb_.ap, in_=xt_.ap), reads=[xt_], writes=[xb_])
                        kk.dma("gpsimd", xt_dr.ap[eb * P:(eb + 1) * P, t0:t0 + tc], xt_,
                               wbufs=[xt_dr])
                        kk.emit("tensor", lambda e, eb=eb, xb_=xb_: e.matmul(
                            psb[2].ap, wxp_bf[eb].ap[:, 0:128], xb_.ap,
                            start=(eb == 0), stop=(eb == DBLK - 1)),
                            reads=[wxp_bf[eb], xb_], writes=[psb[2]])
                        kk.emit("tensor", lambda e, eb=eb, xb_=xb_: e.matmul(
                            psb[3].ap[:32, :], wxp_bf[eb].ap[:, 128:160], xb_.ap,
                            start=(eb == 0), stop=(eb == DBLK - 1)),
                            reads=[wxp_bf[eb], xb_], writes=[psb[3]])
                    else:
                        zv = z_t[eb % 2]
                        zs_ = zs_t[eb % 2]
                        se_ = sil_e[eb % 2]
                        sr_ = sil_r[eb % 2]
                        kk.emit("scalar", lambda e, se_=se_, pm=pm: e.activation(
                            out=se_.ap, in_=pm.ap, func=AF.Exp, scale=-1.0),
                            reads=[pm], writes=[se_])
                        kk.emit("scalar", lambda e, zv=zv, pm=pm: e.copy(out=zv.ap, in_=pm.ap),
                                reads=[pm], writes=[zv])
                        kk.emit("vector", lambda e, se_=se_, sr_=sr_: e.tensor_scalar(
                            out=sr_.ap, in0=se_.ap, scalar1=1.0, scalar2=None, op0=ALU.add),
                            reads=[se_], writes=[sr_])
                        kk.emit("vector", lambda e, sr_=sr_: e.reciprocal(out=sr_.ap, in_=sr_.ap),
                                reads=[sr_], writes=[sr_])
                        kk.emit("vector", lambda e, zv=zv, sr_=sr_, zs_=zs_: e.tensor_tensor(
                            out=zs_.ap, in0=zv.ap, in1=sr_.ap, op=ALU.mult),
                            reads=[zv, sr_], writes=[zs_])
                        kk.dma("gpsimd", zs_dr.ap[eb * P:(eb + 1) * P, t0:t0 + tc], zs_,
                               wbufs=[zs_dr])

                if part == 0:
                    kk.emit("scalar", lambda e: e.copy(out=pj1_s.ap, in_=psb[2].ap),
                            reads=[psb[2]], writes=[pj1_s])
                    kk.emit("scalar", lambda e: e.copy(out=pj2_s.ap, in_=psb[3].ap[:32, :]),
                            reads=[psb[3]], writes=[pj2_s])
                    kk.emit("gpsimd", g_branch_write(projsh[0][0:128, t0:t0 + tc],
                                                     projsh[1][0:128, t0:t0 + tc], pj1_s.ap),
                            reads=[pj1_s, j_s, bar_s], writes=[projsh_b])
                    kk.emit("gpsimd", g_branch_write(projsh[0][128:160, t0:t0 + tc],
                                                     projsh[1][128:160, t0:t0 + tc], pj2_s.ap),
                            reads=[pj2_s, j_s], writes=[projsh_b])
                    kk.emit("gpsimd", g_flag(c), reads=[one16, j_s, projsh_b])

        if "xn" in tap_out:
            kk.dma("gpsimd", tap_out["xn"], xn_dr)
        if "xt" in tap_out:
            kk.dma("gpsimd", tap_out["xt"], xt_dr)
        if "zs" in tap_out:
            kk.dma("gpsimd", tap_out["zs"], zs_dr)

        kk.barrier()
        poll_tok = Buf(None, "poll_tok")
        for c in range(nch):
            t0 = c * tc
            kk.emit("gpsimd", g_poll(c), reads=[j_s], writes=[poll_tok, flag_s])
            kk.dma("sync", pj1a, projsh[0][0:128, t0:t0 + tc], rbufs=[poll_tok])
            kk.dma("sync", pj1b, projsh[1][0:128, t0:t0 + tc])
            kk.dma("sync", pj2a, projsh[0][128:160, t0:t0 + tc])
            kk.dma("sync", pj2b, projsh[1][128:160, t0:t0 + tc])
            kk.emit("vector", lambda e: e.tensor_tensor(out=pj1a.ap, in0=pj1a.ap,
                    in1=pj1b.ap, op=ALU.add), reads=[pj1a, pj1b], writes=[pj1a])
            kk.emit("vector", lambda e: e.tensor_tensor(out=pj2a.ap, in0=pj2a.ap,
                    in1=pj2b.ap, op=ALU.add), reads=[pj2a, pj2b], writes=[pj2a])
            kk.emit("vector", lambda e: e.tensor_copy(out=pj1bf.ap, in_=pj1a.ap),
                    reads=[pj1a], writes=[pj1bf])
            kk.emit("vector", lambda e: e.tensor_copy(out=pjbc_bf.ap, in_=pj2a.ap),
                    reads=[pj2a], writes=[pjbc_bf])
            if "proj" in tap_out:
                kk.dma("gpsimd", tap_out["proj"][0:128, t0:t0 + tc], pj1a)
                kk.dma("gpsimd", tap_out["proj"][128:160, t0:t0 + tc], pj2a)

            for n in range(2 * NST):
                dstf = bb_t[n] if n < NST else cb_t[n - NST]
                pbc = psb[4 + (n % 2)]
                rsg = rowstg[n % 2]
                kk.dma("sync", rsg, pjbc_bf.ap[n:n + 1, :], rbufs=[pjbc_bf])
                kk.emit("tensor", lambda e, pbc=pbc, rsg=rsg: e.matmul(
                    pbc.ap, ones_bf.ap, rsg.ap, start=True, stop=True),
                    reads=[rsg, ones_bf], writes=[pbc])
                kk.emit("scalar", lambda e, pbc=pbc, dstf=dstf: e.copy(
                    out=dstf.ap, in_=pbc.ap), reads=[pbc], writes=[dstf])

            for db in range(DBLK):
                pm = psb[db % 2]
                kk.emit("tensor", lambda e, db=db, pm=pm: e.matmul(
                    pm.ap, wdt_bf.ap[:, db * 128:(db + 1) * 128], pj1bf.ap,
                    start=True, stop=True), reads=[wdt_bf, pj1bf], writes=[pm])
                kk.emit("scalar", lambda e, pm=pm, db=db: e.activation(
                    out=sp_r.ap, in_=pm.ap, func=AF.Relu, bias=dtb_s[db].ap),
                    reads=[pm, dtb_s[db]], writes=[sp_r])
                kk.emit("scalar", lambda e, pm=pm, db=db: e.activation(
                    out=sp_a.ap, in_=pm.ap, func=AF.Abs, bias=dtb_s[db].ap),
                    reads=[pm, dtb_s[db]], writes=[sp_a])
                kk.emit("scalar", lambda e: e.activation(out=sp_a.ap, in_=sp_a.ap,
                        func=AF.Exp, scale=-1.0), reads=[sp_a], writes=[sp_a])
                kk.emit("scalar", lambda e: e.activation(out=sp_a.ap, in_=sp_a.ap,
                        func=AF.Ln, bias=1.0), reads=[sp_a, constv], writes=[sp_a])
                kk.dma("sync", xt_rd, xt_dr.ap[db * P:(db + 1) * P, t0:t0 + tc],
                       rbufs=[xt_dr])
                kk.dma("sync", zs_rd, zs_dr.ap[db * P:(db + 1) * P, t0:t0 + tc],
                       rbufs=[zs_dr])
                kk.emit("vector", lambda e: e.tensor_tensor(out=dt_s.ap, in0=sp_r.ap,
                        in1=sp_a.ap, op=ALU.add), reads=[sp_r, sp_a], writes=[dt_s])
                kk.emit("vector", lambda e: e.tensor_tensor(out=w_s.ap, in0=dt_s.ap,
                        in1=xt_rd.ap, op=ALU.mult), reads=[dt_s, xt_rd], writes=[w_s])
                if "dt" in tap_out:
                    kk.dma("gpsimd", tap_out["dt"][db * P:(db + 1) * P, t0:t0 + tc], dt_s)

                for grp in range(2):
                    yp = ypart[grp]
                    for ni in range(8):
                        n = grp * 8 + ni
                        da = da_t[ni % 2]
                        wb = wb_t[ni % 2]
                        kk.emit("scalar", lambda e, db=db, n=n, da=da: e.activation(
                            out=da.ap, in_=dt_s.ap, func=AF.Exp,
                            scale=a_sc[db].ap[:, n:n + 1]),
                            reads=[dt_s, a_sc[db]], writes=[da])
                        kk.emit("vector", lambda e, n=n, wb=wb: e.tensor_tensor(
                            out=wb.ap, in0=w_s.ap, in1=bb_t[n].ap, op=ALU.mult),
                            reads=[w_s, bb_t[n]], writes=[wb])
                        kk.emit("vector", lambda e, db=db, n=n, ni=ni, da=da, wb=wb:
                                e.tensor_tensor_scan(
                                    out=h_slab.ap[:, ni * tc:(ni + 1) * tc],
                                    data0=da.ap, data1=wb.ap,
                                    initial=hstate[db].ap[:, n:n + 1],
                                    op0=ALU.mult, op1=ALU.add),
                                reads=[da, wb, hstate[db]], writes=[h_slab])
                    for ni in range(8):
                        n = grp * 8 + ni
                        kk.emit("vector", lambda e, db=db, n=n, ni=ni: e.tensor_copy(
                            out=hstate[db].ap[:, n:n + 1],
                            in_=h_slab.ap[:, (ni + 1) * tc - 1:(ni + 1) * tc]),
                            reads=[h_slab], writes=[hstate[db]])
                    for ni in range(8):
                        n = grp * 8 + ni
                        kk.emit("vector", lambda e, n=n, ni=ni: e.tensor_tensor(
                            out=hc_slab.ap[:, ni * tc:(ni + 1) * tc],
                            in0=h_slab.ap[:, ni * tc:(ni + 1) * tc],
                            in1=cb_t[n].ap, op=ALU.mult),
                            reads=[h_slab, cb_t[n]], writes=[hc_slab])
                    kk.emit("vector", lambda e, yp=yp: e.tensor_reduce(
                        out=yp.ap, in_=hc_slab.ap.rearrange("p (n t) -> p t n", n=8),
                        axis=AX.X, op=ALU.add), reads=[hc_slab], writes=[yp])
                kk.emit("vector", lambda e: e.tensor_tensor(out=y_s.ap, in0=ypart[0].ap,
                        in1=ypart[1].ap, op=ALU.add), reads=[ypart[0], ypart[1]], writes=[y_s])
                if "y" in tap_out:
                    kk.dma("gpsimd", tap_out["y"][db * P:(db + 1) * P, t0:t0 + tc], y_s)
                kk.emit("vector", lambda e, db=db: e.scalar_tensor_tensor(
                    out=y_s.ap, in0=xt_rd.ap, scalar=dvec_s[db].ap, in1=y_s.ap,
                    op0=ALU.mult, op1=ALU.add), reads=[xt_rd, dvec_s[db], y_s], writes=[y_s])
                kk.emit("vector", lambda e: e.tensor_tensor(out=y_s.ap, in0=y_s.ap,
                        in1=zs_rd.ap, op=ALU.mult), reads=[y_s, zs_rd], writes=[y_s])
                kk.emit("vector", lambda e, db=db: e.tensor_copy(
                    out=yg_slab.ap[:, db * tc:(db + 1) * tc], in_=y_s.ap),
                    reads=[y_s], writes=[yg_slab])

            src = yg_slab.ap.rearrange("p (b t) -> p b t", b=DBLK)
            d0 = ysh[0][:, t0:t0 + tc].rearrange("(b p) t -> p b t", p=P)
            d1 = ysh[1][:, t0:t0 + tc].rearrange("(b p) t -> p b t", p=P)
            kk.emit("gpsimd", g_branch_write(d0, d1, src),
                    reads=[yg_slab, j_s], writes=[ysh_b])
            kk.emit("gpsimd", g_flag(nch + c), reads=[one16, j_s, ysh_b])

        kk.barrier()

        for c in range(nch):
            kk.emit("gpsimd", g_poll(nch + c), reads=[j_s], writes=[poll_tok, flag_s])
        kk.barrier()

        for C in range(nbig):
            tb0 = C * tb
            for mb in range(MBLK):
                kk.dma("sync", xn3[mb], xn_dr.ap[mb * P:(mb + 1) * P, tb0:tb0 + tb],
                       rbufs=[xn_dr])
            for ip in range(IBLK2 // 2):
                for half in range(2):
                    ob = ip + half * (OBLK // 2)
                    for k in range(MBLK):
                        kk.dma("sync",
                               Buf(wgu_slab[half].ap[:, k * 128:(k + 1) * 128], "wgu_t"),
                               wgu_dr.ap[k, :, ob * P:(ob + 1) * P],
                               rbufs=[wgu_dr], wbufs=[wgu_slab[half]])
                for sl in range(tb // 512):
                    for half in range(2):
                        pm = psb[half]
                        for k in range(MBLK):
                            kk.emit("tensor", lambda e, k=k, half=half, sl=sl, pm=pm: e.matmul(
                                pm.ap, wgu_slab[half].ap[:, k * 128:(k + 1) * 128],
                                xn3[k].ap[:, sl * 512:(sl + 1) * 512],
                                start=(k == 0), stop=(k == MBLK - 1)),
                                reads=[wgu_slab[half], xn3[k]], writes=[pm])
                    kk.emit("scalar", lambda e: e.activation(
                        out=g_e.ap, in_=psb[0].ap, func=AF.Exp, scale=-1.0),
                        reads=[psb[0]], writes=[g_e])
                    kk.emit("scalar", lambda e: e.copy(out=g_cp.ap, in_=psb[0].ap),
                            reads=[psb[0]], writes=[g_cp])
                    kk.emit("scalar", lambda e: e.copy(out=u_cp.ap, in_=psb[1].ap),
                            reads=[psb[1]], writes=[u_cp])
                    kk.emit("vector", lambda e: e.tensor_scalar(out=g_r.ap,
                            in0=g_e.ap, scalar1=1.0, scalar2=None, op0=ALU.add),
                            reads=[g_e], writes=[g_r])
                    kk.emit("vector", lambda e: e.reciprocal(out=g_r.ap, in_=g_r.ap),
                            reads=[g_r], writes=[g_r])
                    kk.emit("vector", lambda e: e.tensor_tensor(out=g_sg.ap,
                            in0=g_cp.ap, in1=g_r.ap, op=ALU.mult),
                            reads=[g_cp, g_r], writes=[g_sg])
                    kk.emit("vector", lambda e: e.tensor_tensor(out=g_sg.ap,
                            in0=g_sg.ap, in1=u_cp.ap, op=ALU.mult),
                            reads=[g_sg, u_cp], writes=[g_sg])
                    hb = h_bf[sl % 2]
                    kk.emit("vector", lambda e, hb=hb: e.tensor_copy(
                        out=hb.ap, in_=g_sg.ap), reads=[g_sg], writes=[hb])
                    dst0 = hsh[0][ip * P:(ip + 1) * P, tb0 + sl * 512: tb0 + (sl + 1) * 512]
                    dst1 = hsh[1][ip * P:(ip + 1) * P, tb0 + sl * 512: tb0 + (sl + 1) * 512]
                    kk.emit("gpsimd", g_branch_write(dst0, dst1, hb.ap),
                            reads=[hb, j_s], writes=[hsh_b])
            kk.emit("gpsimd", g_flag(2 * nch + C), reads=[one16, j_s, hsh_b])
            kk.emit("gpsimd", g_poll(2 * nch + C), reads=[j_s], writes=[poll_tok, flag_s])
            kk.barrier()

            for ib in range(IBLK2):
                kk.dma("sync", wd_sb[ib], wd_dr.ap[ib], rbufs=[wd_dr])
            for t128 in range(tb // P):
                trow = tb0 + t128 * P
                pm0 = psb[2 * (t128 % 2)]
                pm1 = psb[2 * (t128 % 2) + 1]
                hsl = h_slab3[0]
                for k2 in range(2):
                    kk.dma("sync",
                           hsl.ap.rearrange("p (b t) -> p b t", b=IBLK2)[:, k2 * 22:(k2 + 1) * 22, :],
                           hsh[k2][:, trow:trow + P].rearrange("(b p) t -> p b t", p=P),
                           rbufs=[poll_tok], wbufs=[hsl])
                for ib in range(IBLK2):
                    hl_ap = hsl.ap[:, ib * 128:(ib + 1) * 128]
                    kk.emit("tensor", lambda e, hl_ap=hl_ap, ib=ib, pm0=pm0: e.matmul(
                        pm0.ap, hl_ap, wd_sb[ib].ap[:, 0:512],
                        start=(ib == 0), stop=(ib == IBLK2 - 1)),
                        reads=[hsl, wd_sb[ib]], writes=[pm0])
                    kk.emit("tensor", lambda e, hl_ap=hl_ap, ib=ib, pm1=pm1: e.matmul(
                        pm1.ap, hl_ap, wd_sb[ib].ap[:, 512:1024],
                        start=(ib == 0), stop=(ib == IBLK2 - 1)),
                        reads=[hsl, wd_sb[ib]], writes=[pm1])
                yf = yf_t[t128 % 2]
                kk.emit("scalar", lambda e, pm0=pm0, yf=yf: e.copy(
                    out=yf.ap[:, 0:512], in_=pm0.ap), reads=[pm0], writes=[yf])
                kk.emit("scalar", lambda e, pm1=pm1, yf=yf: e.copy(
                    out=yf.ap[:, 512:1024], in_=pm1.ap), reads=[pm1], writes=[yf])
                kk.dma("gpsimd", yffn_dr.ap[trow:trow + P, :], yf, wbufs=[yffn_dr])
                kk.emit("scalar", lambda e, yf=yf: e.activation(
                    out=sqsc.ap, in_=yf.ap, func=AF.Square, accum_out=sq_p.ap),
                    reads=[yf], writes=[sqsc, sq_p])
                rowi = trow // P
                dst0 = sqsh[0][rowi:rowi + 1, :].rearrange("r p -> p r")
                dst1 = sqsh[1][rowi:rowi + 1, :].rearrange("r p -> p r")
                kk.emit("gpsimd", g_branch_write(dst0, dst1, sq_p.ap),
                        reads=[sq_p, j_s], writes=[sqsh_b])
            kk.emit("gpsimd", g_flag(2 * nch + nbig + C), reads=[one16, j_s, sqsh_b])
            kk.emit("gpsimd", g_poll(2 * nch + nbig + C), reads=[j_s], writes=[poll_tok, flag_s])
            kk.barrier()

            for ib in range(WOBLK):
                kk.dma("sync", wout_sb[ib], wout_dr.ap[ib], rbufs=[wout_dr])
            for t128 in range(tb // P):
                trow = tb0 + t128 * P
                rowi = trow // P
                pm0 = psb[2 + (t128 % 2)]
                pm1 = psb[4 + (t128 % 2)]
                ysl = y_slab[t128 % 2]
                for k2 in range(2):
                    kk.dma("sync",
                           ysl.ap.rearrange("p (b t) -> p b t", b=32)[:, k2 * 16:(k2 + 1) * 16, :],
                           ysh[k2][:, trow:trow + P].rearrange("(b p) t -> p b t", p=P),
                           rbufs=[poll_tok], wbufs=[ysl])
                for ib in range(WOBLK):
                    yl_ap = ysl.ap[:, ib * 128:(ib + 1) * 128]
                    kk.emit("tensor", lambda e, yl_ap=yl_ap, pm0=pm0, ib=ib: e.matmul(
                        pm0.ap, yl_ap, wout_sb[ib].ap[:, 0:512],
                        start=(ib == 0), stop=(ib == WOBLK - 1)),
                        reads=[ysl, wout_sb[ib]], writes=[pm0])
                    kk.emit("tensor", lambda e, yl_ap=yl_ap, pm1=pm1, ib=ib: e.matmul(
                        pm1.ap, yl_ap, wout_sb[ib].ap[:, 512:1024],
                        start=(ib == 0), stop=(ib == WOBLK - 1)),
                        reads=[ysl, wout_sb[ib]], writes=[pm1])
                sv = s_ev[t128 % 2]
                kk.emit("scalar", lambda e, pm0=pm0, sv=sv: e.copy(
                    out=sv.ap[:, 0:512], in_=pm0.ap), reads=[pm0], writes=[sv])
                kk.emit("scalar", lambda e, pm1=pm1, sv=sv: e.copy(
                    out=sv.ap[:, 512:1024], in_=pm1.ap), reads=[pm1], writes=[sv])
                if "s" in tap_out:
                    kk.dma("gpsimd", tap_out["s"][trow:trow + P, :], sv)
                kk.dma("sync", sq0, sqsh[0][rowi:rowi + 1, :].rearrange("r p -> p r"),
                       rbufs=[poll_tok])
                kk.dma("sync", sq1, sqsh[1][rowi:rowi + 1, :].rearrange("r p -> p r"),
                       rbufs=[poll_tok])
                kk.emit("vector", lambda e: e.tensor_tensor(out=sq0.ap, in0=sq0.ap,
                        in1=sq1.ap, op=ALU.add), reads=[sq0, sq1], writes=[sq0])
                kk.emit("scalar", lambda e: e.activation(out=rstd3.ap, in_=sq0.ap,
                        func=AF.Ln, scale=1.0 / DM, bias=RMS_EPS), reads=[sq0, constv], writes=[rstd3])
                kk.emit("scalar", lambda e: e.activation(out=rstd3.ap, in_=rstd3.ap,
                        func=AF.Exp, scale=-0.5), reads=[rstd3], writes=[rstd3])
                kk.dma("sync", yfr, yffn_dr.ap[trow:trow + P, :], rbufs=[yffn_dr])
                kk.emit("vector", lambda e: e.tensor_scalar(out=yfr.ap, in0=yfr.ap,
                        scalar1=rstd3.ap, scalar2=None, op0=ALU.mult),
                        reads=[yfr, rstd3], writes=[yfr])
                kk.emit("scalar", lambda e: e.activation(out=g3e.ap, in_=yfr.ap,
                        func=AF.Exp, scale=-1.0), reads=[yfr], writes=[g3e])
                kk.emit("vector", lambda e: e.tensor_scalar(out=g3r.ap, in0=g3e.ap,
                        scalar1=1.0, scalar2=None, op0=ALU.add), reads=[g3e], writes=[g3r])
                kk.emit("vector", lambda e: e.reciprocal(out=g3r.ap, in_=g3r.ap),
                        reads=[g3r], writes=[g3r])
                kk.emit("vector", lambda e: e.tensor_tensor(out=yfr.ap, in0=yfr.ap,
                        in1=g3r.ap, op=ALU.mult), reads=[yfr, g3r], writes=[yfr])
                ov = o_t[t128 % 2]
                kk.dma("sync", xh_t, x_halfi[trow:trow + P, :])
                kk.emit("vector", lambda e, sv=sv, ov=ov: e.tensor_tensor(
                    out=ov.ap, in0=sv.ap, in1=yfr.ap, op=ALU.mult),
                    reads=[sv, yfr], writes=[ov])
                kk.emit("vector", lambda e, ov=ov: e.tensor_tensor(
                    out=ov.ap, in0=ov.ap, in1=xh_t.ap, op=ALU.add),
                    reads=[ov, xh_t], writes=[ov])
                kk.dma("sync", out_half[trow:trow + P, :], ov)
            kk.barrier()

        if "yffn" in tap_out:
            kk.dma("gpsimd", tap_out["yffn"], yffn_dr)
        if "hh" in tap_out:
            kk.dma("gpsimd", tap_out["hh"][0], hsh[0], rbufs=[hsh_b])
            kk.dma("gpsimd", tap_out["hh"][1], hsh[1], rbufs=[hsh_b])
        if "wo" in tap_out:
            kk.dma("gpsimd", tap_out["wo"], wout_dr)
        if "yy" in tap_out:
            kk.dma("gpsimd", tap_out["yy"][0], ysh[0], rbufs=[ysh_b])
            kk.dma("gpsimd", tap_out["yy"][1], ysh[1], rbufs=[ysh_b])

        kk.barrier()
        for en in ("sync", "gpsimd"):
            for s in range(K.NSLOT):
                if kk.slot_cnt[en][s]:
                    kk._wait(en, (en, "d", s), kk.slot_cnt[en][s])

        kk.flush(block)

    nc._kk = kk
    return nc



_IDENT = np.eye(128, dtype=ml_dtypes.bfloat16)


def _core_inputs(core, inp):
    b, j = core // 2, core & 1
    x = np.ascontiguousarray(inp["L0_attn_hidden_states"][b])
    wi = inp["L0_attn_in_proj_w"]
    di0, di1 = j * DIH, (j + 1) * DIH
    return {
        "x_full": x,
        "x_halfi": np.ascontiguousarray(x[:, j * DMH:(j + 1) * DMH]),
        "w_in_x_T": np.ascontiguousarray(wi[di0:di1, :].T),
        "w_in_z_T": np.ascontiguousarray(wi[4096 + di0:4096 + di1, :].T),
        "conv_w": np.ascontiguousarray(inp["L0_attn_conv_w"][di0:di1]),
        "conv_b": np.ascontiguousarray(inp["L0_attn_conv_b"][di0:di1, None]),
        "w_xp_T": np.ascontiguousarray(inp["L0_attn_x_proj_w"][:, di0:di1].T),
        "w_dt_T": np.ascontiguousarray(inp["L0_attn_dt_proj_w"][di0:di1, :].T),
        "dt_b": np.ascontiguousarray(inp["L0_attn_dt_proj_b"][di0:di1, None]),
        "a_log": np.ascontiguousarray(inp["L0_attn_A_log"][di0:di1]),
        "d_vec": np.ascontiguousarray(inp["L0_attn_D"][di0:di1, None]),
        "w_out_T": np.ascontiguousarray(
            inp["L0_attn_out_proj_w"].T[:, j * DMH:(j + 1) * DMH]),
        "gu_packed": np.ascontiguousarray(np.concatenate([
            inp["L0_ffn_gate_up_packed"][j * IH:(j + 1) * IH],
            inp["L0_ffn_gate_up_packed"][5632 + j * IH:5632 + (j + 1) * IH]], axis=0)),
        "gu_scales_T": np.ascontiguousarray(np.concatenate([
            inp["L0_ffn_gate_up_scales"][:, j * IH:(j + 1) * IH],
            inp["L0_ffn_gate_up_scales"][:, 5632 + j * IH:5632 + (j + 1) * IH]],
            axis=1).T.copy()).view(np.uint16),
        "dn_packed": np.ascontiguousarray(
            inp["L0_ffn_down_packed"][j * DMH:(j + 1) * DMH]),
        "dn_scales_T": np.ascontiguousarray(
            inp["L0_ffn_down_scales"][:, j * DMH:(j + 1) * DMH].T).view(np.uint16),
        "j_in": np.array([[j]], dtype=np.uint32),
        "ident": _IDENT,
        "constv": np.broadcast_to(
            np.array([LN_EPS, RMS_EPS, 1.0], np.float32), (128, 3)).copy(),
    }


_prog_cache = {}


def kernel(**inputs):
    t_tokens = inputs["L0_attn_hidden_states"].shape[1]
    if t_tokens not in _prog_cache:
        _prog_cache[t_tokens] = build_program(t_tokens=t_tokens)
    nc = _prog_cache[t_tokens]
    nonce = np.random.randint(1, 2**31 - 1, size=(1, 4)).astype(np.uint32)
    in_maps = []
    for core in range(8):
        m = _core_inputs(core, inputs)
        m["nonce_in"] = nonce
        in_maps.append(m)
    res = run_bass_kernel_spmd(nc, in_maps, core_ids=list(range(8)))
    out = np.empty((4, t_tokens, DM), dtype=np.float32)
    for core in range(8):
        b, j = core // 2, core & 1
        out[b][:, j * DMH:(j + 1) * DMH] = res.results[core]["out_half"]
    return out


# version 03
# speedup vs baseline: 1.1975x; 1.1800x over previous
import numpy as np
import ml_dtypes
import concourse.bass as bass
import concourse.mybir as mybir
from concourse.bass_utils import run_bass_kernel_spmd

F32 = mybir.dt.float32
BF16 = mybir.dt.bfloat16
F16 = mybir.dt.float16
I32 = mybir.dt.int32
U32 = mybir.dt.uint32
U16 = mybir.dt.uint16
AF = mybir.ActivationFunctionType
ALU = mybir.AluOpType
AX = mybir.AxisListType

P = 128
DM = 2048
DIH = 2048
NST = 16
RK = 128
IH = 2816
GU = 2 * IH
DMH = 1024
LN_EPS = 1e-5
RMS_EPS = 1e-6
DBLK = DIH // P
MBLK = DM // P
OBLK = GU // P
IBLK2 = 2 * IH // P
WOBLK = 4096 // P

ENGS = ("sync", "vector", "scalar", "tensor", "gpsimd")


class Op:
    __slots__ = ("eng", "idx", "sem_val", "dma", "pend_inc", "label", "slot", "slot_val")

    def __init__(self, eng, idx, dma):
        self.eng = eng
        self.idx = idx
        self.dma = dma
        self.sem_val = None
        self.pend_inc = 0


class Buf:
    __slots__ = ("ap", "wop", "rops", "name")

    def __init__(self, ap, name=""):
        self.ap = ap
        self.wop = None
        self.rops = []
        self.name = name


class K:

    NSLOT = 16

    def __init__(self, nc):
        self.nc = nc
        self.ops = {e: [] for e in ENGS}
        self.clos = {e: [] for e in ENGS}
        self.milestones = {e: 0 for e in ENGS}
        self.dma_emitted = {e: 0 for e in ENGS}
        self.slot_cnt = {e: [0] * self.NSLOT for e in ENGS}
        self.waited = {e: {} for e in ENGS}
        self.sems = {}

    def _milestone(self, op):
        if op.dma:
            return (op.eng, "d", op.slot), op.slot_val
        if op.sem_val is None:
            last = self.ops[op.eng][-1]
            while last.dma:
                for o in reversed(self.ops[op.eng]):
                    if not o.dma and o.idx >= op.idx:
                        last = o
                        break
                else:
                    last = None
                break
            if last is None:
                lastd = self.ops[op.eng][-1]
                return (op.eng, "d", lastd.slot), lastd.slot_val
            if last.sem_val is None:
                self.milestones[op.eng] += 1
                last.sem_val = self.milestones[op.eng]
                last.pend_inc += 1
            return (op.eng, "c"), last.sem_val
        return (op.eng, "c"), op.sem_val

    def _wait(self, eng, key, val):
        if self.waited[eng].get(key, 0) >= val:
            return
        self.waited[eng][key] = val
        self.clos[eng].append(("wait", key, val))

    def _dep(self, eng, op):
        key, val = self._milestone(op)
        self._wait(eng, key, val)

    def emit(self, eng, fn, reads=(), writes=(), dma=False, label=""):
        for b in reads:
            if b.wop is not None:
                self._dep(eng, b.wop)
        for b in writes:
            if b.wop is not None and b.wop.eng != eng:
                self._dep(eng, b.wop)
            for r in b.rops:
                if r.eng != eng:
                    self._dep(eng, r)
        op = Op(eng, len(self.ops[eng]), dma)
        op.label = label
        if dma:
            i = self.dma_emitted[eng]
            slot = i % self.NSLOT
            if self.slot_cnt[eng][slot] > 0:
                self._wait(eng, (eng, "d", slot), self.slot_cnt[eng][slot])
            self.slot_cnt[eng][slot] += 16
            op.slot = slot
            op.slot_val = self.slot_cnt[eng][slot]
            self.dma_emitted[eng] += 1
        self.ops[eng].append(op)
        self.clos[eng].append(("op", op, fn))
        for b in reads:
            b.rops.append(op)
        for b in writes:
            b.wop = op
            b.rops = []
        return op

    def dma(self, eng, out, in_, rbufs=(), wbufs=(), **kw):
        reads = list(rbufs)
        writes = list(wbufs)
        if isinstance(in_, Buf):
            reads.append(in_)
            in_ap = in_.ap
        else:
            in_ap = in_
        if isinstance(out, Buf):
            writes.append(out)
            out_ap = out.ap
        else:
            out_ap = out
        return self.emit(eng, lambda e: e.dma_start(out=out_ap, in_=in_ap, **kw),
                         reads=reads, writes=writes, dma=True)

    def barrier(self):
        marks = []
        for en in ENGS:
            for s in range(self.NSLOT):
                if self.slot_cnt[en][s]:
                    marks.append(((en, "d", s), self.slot_cnt[en][s]))
            for o in reversed(self.ops[en]):
                if not o.dma:
                    key, val = self._milestone(o)
                    marks.append((key, val))
                    break
        for en in ENGS:
            for key, val in marks:
                if key[0] != en:
                    self._wait(en, key, val)
                elif len(key) == 3:
                    self._wait(en, key, val)

    def flush(self, block):
        makers = {"sync": block.sync, "vector": block.vector,
                  "scalar": block.scalar, "tensor": block.tensor,
                  "gpsimd": block.gpsimd}
        for en in ENGS:
            items = self.clos[en]
            if not items:
                continue

            def body(e, items=items, en=en):
                for it in items:
                    if it[0] == "wait":
                        _, key, val = it
                        e.wait_ge(self.sems[key], val)
                    else:
                        _, op, fn = it
                        inst = fn(e)
                        if op.dma:
                            inst.then_inc(self.sems[(en, "d", op.slot)], 16)
                        if op.pend_inc:
                            inst.then_inc(self.sems[(en, "c")], op.pend_inc)
            makers[en](body)


def build_program(t_tokens=8192, tc=512, taps=()):
    nch = t_tokens // tc
    nbig = 4
    cpb = nch // nbig
    tb = cpb * tc
    nc = bass.Bass("TRN2", target_bir_lowering=False)

    def din(name, shape, dt=F32):
        return nc.dram_tensor(name, shape, dt, kind="ExternalInput").ap()

    x_full = din("x_full", [t_tokens, DM])
    x_halfi = din("x_halfi", [t_tokens, DMH])
    w_in_x_T = din("w_in_x_T", [DM, DIH])
    w_in_z_T = din("w_in_z_T", [DM, DIH])
    conv_w = din("conv_w", [DIH, 4])
    conv_b = din("conv_b", [DIH, 1])
    w_xp_T = din("w_xp_T", [DIH, 160])
    w_dt_T = din("w_dt_T", [RK, DIH])
    dt_b = din("dt_b", [DIH, 1])
    a_log = din("a_log", [DIH, NST])
    d_vec = din("d_vec", [DIH, 1])
    w_out_T = din("w_out_T", [4096, DMH])
    gu_packed = din("gu_packed", [GU, DM // 8], I32)
    gu_scales_T = din("gu_scales_T", [GU, DM // 128], U16)
    dn_packed = din("dn_packed", [DMH, 2 * IH // 8], I32)
    dn_scales_T = din("dn_scales_T", [DMH, 2 * IH // 128], U16)
    j_in = din("j_in", [1, 1], U32)
    nonce_in = din("nonce_in", [1, 4], U32)
    ident_in = din("ident", [128, 128], BF16)
    constv_in = din("constv", [128, 3])

    out_half = nc.dram_tensor("out_half", [t_tokens, DMH], F32, kind="ExternalOutput").ap()

    def dloc(name, shape, dt):
        return Buf(nc.dram_tensor(name, shape, dt, kind="Internal").ap(), name)

    xn_dr = dloc("xn_dr", [DM, t_tokens], BF16)
    xt_dr = dloc("xt_dr", [DIH, t_tokens], F32)
    zs_dr = dloc("zs_dr", [DIH, t_tokens], F32)
    wout_dr = dloc("wout_dr", [WOBLK, 128, DMH], BF16)
    wgu_dr = dloc("wgu_dr", [MBLK, 128, GU], BF16)
    wd_dr = dloc("wd_dr", [IBLK2, 128, DMH], BF16)
    yffn_dr = dloc("yffn_dr", [t_tokens, DMH], F32)

    def dsh(name, shape, dt):
        return nc.dram_tensor(name, shape, dt, kind="Internal", addr_space="Shared").ap()

    projsh = [dsh(f"projsh{k}", [160, t_tokens], F32) for k in range(2)]
    ysh = [dsh(f"ysh{k}", [DIH, t_tokens], BF16) for k in range(2)]
    hsh = [dsh(f"hsh{k}", [IH, t_tokens], BF16) for k in range(2)]
    sqsh = [dsh(f"sqsh{k}", [t_tokens // P, P], F32) for k in range(2)]
    NFLAG = 2 * nch + 2 * nbig
    flagsh = [dsh(f"flagsh{k}", [NFLAG, 16], U32) for k in range(2)]
    barsh = [dsh(f"barsh{k}", [1, 4], U32) for k in range(2)]
    projsh_b = Buf(None, "projsh")
    ysh_b = Buf(None, "ysh")
    hsh_b = Buf(None, "hsh")
    sqsh_b = Buf(None, "sqsh")

    tap_out = {}
    tap_shapes = {
        "xn": ([DM, t_tokens], BF16), "xt": ([DIH, t_tokens], F32),
        "zs": ([DIH, t_tokens], F32), "proj": ([160, t_tokens], F32),
        "dt": ([DIH, t_tokens], F32), "y": ([DIH, t_tokens], F32),
        "wgu": ([MBLK, 128, GU], BF16), "wd": ([IBLK2, 128, DMH], BF16),
        "s": ([t_tokens, DMH], F32), "yffn": ([t_tokens, DMH], F32),
        "dqx": ([128, DM], BF16), "dqn": ([128, DM], U16),
        "hh": ([2, IH, t_tokens], BF16), "yy": ([2, DIH, t_tokens], BF16),
        "wo": ([WOBLK, 128, DMH], BF16),
    }
    for t in taps:
        sh, dt = tap_shapes[t]
        tap_out[t] = nc.dram_tensor("tap_" + t, sh, dt, kind="ExternalOutput").ap()

    def sbraw(name, shape, dt):
        return nc.alloc_sbuf_tensor(name, shape, dt).ap()

    def sb(name, shape, dt):
        return Buf(sbraw(name, shape, dt), name)

    ident_s = sb("ident_s", [128, 128], BF16)
    j_s = sb("j_s", [1, 1], U32)
    nonce_s = sb("nonce_s", [1, 4], U32)
    bar_s = sb("bar_s", [1, 4], U32)
    zero16 = sb("zero16", [1, 16], U32)
    one16 = sb("one16", [1, 16], U32)
    flag_s = sb("flag_s", [1, 1], U32)
    ones_bf = sb("ones_bf", [1, 128], BF16)
    ones_f = sb("ones_f", [128, 128], F32)

    a_sc = [sb(f"a_sc{i}", [P, NST], F32) for i in range(DBLK)]
    cw_s = [sb(f"cw_s{i}", [P, 4], F32) for i in range(DBLK)]
    cb_s = [sb(f"cb_s{i}", [P, 1], F32) for i in range(DBLK)]
    ncb_s = [sb(f"ncb_s{i}", [P, 1], F32) for i in range(DBLK)]
    dtb_s = [sb(f"dtb_s{i}", [P, 1], F32) for i in range(DBLK)]
    dvec_s = [sb(f"dvec_s{i}", [P, 1], F32) for i in range(DBLK)]
    hstate = [sb(f"hstate{i}", [P, NST], F32) for i in range(DBLK)]
    halo = [sb(f"halo{i}", [P, 3], F32) for i in range(DBLK)]
    wdt_bf = sb("wdt_bf", [RK, DIH], BF16)
    wxp_stg = sb("wxp_stg", [P, 160], F32)
    wxp_bf = [sb(f"wxp_bf{i}", [P, 160], BF16) for i in range(DBLK)]
    wstg = sb("wstg", [P, DIH], F32)
    ln_m = sb("ln_m", [P, 1], F32)
    ln_ssq = sb("ln_ssq", [P, 1], F32)
    ln_rs = sb("ln_rs", [P, 1], F32)
    sq_p = sb("sq_p", [P, 1], F32)
    sq0 = sb("sq0", [P, 1], F32)
    sq1 = sb("sq1", [P, 1], F32)
    rstd3 = sb("rstd3", [P, 1], F32)
    constv = sb("constv_s", [P, 3], F32)

    ARENA_HW = 94000
    arena_ap = sbraw("arena", [P, ARENA_HW], BF16)
    _dt_bytes = {F32: 4, BF16: 2, F16: 2, I32: 4, U16: 2, U32: 4}

    class Arena:
        def __init__(self):
            self.cur = 0

        def reset(self):
            self.cur = 0

        def al(self, name, cols, dt, parts=P):
            b = _dt_bytes[dt]
            off = (self.cur + 3) & ~3
            w_hw = (cols * b + 1) // 2
            assert off + w_hw * 2 <= ARENA_HW * 2, f"arena overflow at {name}"
            view = arena_ap[:parts, off // 2: off // 2 + w_hw]
            self.cur = off + w_hw * 2
            if dt != BF16:
                view = view.bitcast(dt)
                if view.shape[1] != cols:
                    view = view[:, :cols]
            else:
                view = view[:, :cols]
            return Buf(view, name)

    AR = Arena()

    wstg2 = AR.al("wstg2", DMH, F32)
    wo_bf = AR.al("wo_bf", DMH, BF16)
    DQCH_GU = 1024
    DQCH_DN = 1408
    DQCH = 1408
    dq_pk = AR.al("dq_pk", 704, I32)
    dq_nib = AR.al("dq_nib", DQCH, U16)
    dq_t1 = AR.al("dq_t1", DQCH, U16)
    dq_t2 = AR.al("dq_t2", DQCH, U16)
    dq_t3 = AR.al("dq_t3", DQCH, U16)
    dq_sc16 = AR.al("dq_sc16", 44, U16)
    dq_sc = AR.al("dq_sc", 44, F32)
    dq_srep = AR.al("dq_srep", DQCH, F32)
    dq_out = AR.al("dq_out", DQCH, BF16)
    dq_tpo = [AR.al(f"dq_tpo{i}", 128, BF16) for i in range(2)]

    AR.reset()
    win_sb = [AR.al(f"win{i}", DIH, BF16) for i in range(MBLK)]
    xtok = [AR.al(f"xtok{i}", DM, F32) for i in range(2)]
    xc_t = [AR.al(f"xc_t{i}", DM, F32) for i in range(2)]
    xn_tok = [AR.al(f"xn_tok{i}", DM, BF16) for i in range(2)]
    ln_sq = AR.al("ln_sq", DM, F32)
    xn_fm = [AR.al(f"xn_fm{i}", tc, BF16) for i in range(MBLK)]
    xl_t = [AR.al(f"xl_t{i}", tc + 3, F32) for i in range(2)]
    cv_t = [AR.al(f"cv_t{i}", tc, F32) for i in range(2)]
    sil_e = [AR.al(f"sil_e{i}", tc, F32) for i in range(2)]
    sil_r = [AR.al(f"sil_r{i}", tc, F32) for i in range(2)]
    xt_t = [AR.al(f"xt_t{i}", tc, F32) for i in range(2)]
    xt_bf = [AR.al(f"xt_bf{i}", tc, BF16) for i in range(2)]
    z_t = [AR.al(f"z_t{i}", tc, F32) for i in range(2)]
    zs_t = [AR.al(f"zs_t{i}", tc, F32) for i in range(2)]
    pj1_s = AR.al("pj1_s", tc, F32)
    pj2_s = AR.al("pj2_s", tc, F32, parts=32)

    AR.reset()
    pj1a = AR.al("pj1a", tc, F32)
    pj1b = AR.al("pj1b", tc, F32)
    pj1bf = AR.al("pj1bf", tc, BF16)
    pj2a = AR.al("pj2a", tc, F32, parts=32)
    pj2b = AR.al("pj2b", tc, F32, parts=32)
    pjbc_bf = AR.al("pjbc_bf", tc, BF16, parts=32)
    rowstg = [AR.al(f"rowstg{i}", tc, BF16, parts=1) for i in range(2)]
    bb_t = [AR.al(f"bb_t{i}", tc, F32) for i in range(NST)]
    cb_t = [AR.al(f"cb_t{i}", tc, F32) for i in range(NST)]
    dt_s = AR.al("dt_s", tc, F32)
    sp_r = AR.al("sp_r", tc, F32)
    sp_a = AR.al("sp_a", tc, F32)
    w_s = AR.al("w_s", tc, F32)
    da_slab = AR.al("da_slab", 8 * tc, F32)
    wb_slab = AR.al("wb_slab", 8 * tc, F32)
    h_slab = AR.al("h_slab", 8 * tc, F32)
    hc_slab = AR.al("hc_slab", 8 * tc, F32)
    ypart = [AR.al(f"ypart{i}", tc, F32) for i in range(2)]
    y_s = AR.al("y_s", tc, F32)
    yg_slab = AR.al("yg_slab", DBLK * tc, BF16)
    xt_rd = AR.al("xt_rd", tc, F32)
    zs_rd = AR.al("zs_rd", tc, F32)

    AR.reset()
    xn3 = [AR.al(f"xn3_{i}", tb, BF16) for i in range(MBLK)]
    ph3_lo_hw = AR.cur
    AR.reset()
    wd_sb = [AR.al(f"wd{i}", DMH, BF16) for i in range(IBLK2)]
    AR.reset()
    wout_sb = [AR.al(f"wout{i}", DMH, BF16) for i in range(WOBLK)]
    AR.cur = max(ph3_lo_hw, IBLK2 * 2048 + 64, WOBLK * 2048 + 64, AR.cur)
    g_e = AR.al("g_e", 512, F32)
    g_cp = AR.al("g_cp", 512, F32)
    g_r = AR.al("g_r", 512, F32)
    g_sg = AR.al("g_sg", 512, F32)
    u_cp = AR.al("u_cp", 512, F32)
    h_bf = [AR.al(f"h_bf{i}", 512, BF16) for i in range(2)]
    wgu_slab = [AR.al(f"wgu_slab{i}", MBLK * 128, BF16) for i in range(2)]
    h_slab3 = [AR.al(f"h_slab3{i}", IBLK2 * 128, BF16) for i in range(1)]
    yf_t = [AR.al(f"yf_t{i}", DMH, F32) for i in range(2)]
    sqsc = AR.al("sqsc", DMH, F32)
    y_slab = [AR.al(f"y_slab{i}", 32 * 128, BF16) for i in range(2)]
    s_ev = [AR.al(f"s_ev{i}", DMH, F32) for i in range(2)]
    yfr = AR.al("yfr", DMH, F32)
    g3e = AR.al("g3e", DMH, F32)
    g3r = AR.al("g3r", DMH, F32)
    xh_t = AR.al("xh_t", DMH, F32)
    o_t = [AR.al(f"o_t{i}", DMH, F32) for i in range(2)]

    import contextlib
    with contextlib.ExitStack() as _stack:
        ent = _stack.enter_context
        cs_sync = ent(nc.semaphore("cs_sync")); cs_vec = ent(nc.semaphore("cs_vec"))
        cs_sca = ent(nc.semaphore("cs_sca")); cs_ten = ent(nc.semaphore("cs_ten"))
        cs_gps = ent(nc.semaphore("cs_gps"))
        sem_poll = ent(nc.semaphore("s_poll")); sem_gq = ent(nc.semaphore("s_gq"))
        dpool = {}
        for qe in ("sync", "gpsimd"):
            for s in range(K.NSLOT):
                dpool[(qe, s)] = ent(nc.semaphore(f"dq_{qe}_{s}"))
        ps0_h = ent(nc.psum_tensor("ps0", [P, 512], F32))
        ps1_h = ent(nc.psum_tensor("ps1", [P, 512], F32))
        ps2_h = ent(nc.psum_tensor("ps2", [P, 512], F32))
        ps3_h = ent(nc.psum_tensor("ps3", [P, 512], F32))
        ps4_h = ent(nc.psum_tensor("ps4", [P, 512], F32))
        ps5_h = ent(nc.psum_tensor("ps5", [P, 512], F32))
        pst0_h = ent(nc.psum_tensor("pst0", [P, 128], BF16))
        pst1_h = ent(nc.psum_tensor("pst1", [P, 128], BF16))
        block = ent(nc.Block())
        kk = K(nc)
        for en, ch in (("sync", cs_sync), ("vector", cs_vec),
                       ("scalar", cs_sca), ("tensor", cs_ten),
                       ("gpsimd", cs_gps)):
            kk.sems[(en, "c")] = ch
        for (qe, s), h in dpool.items():
            kk.sems[(qe, "d", s)] = h
        psb = [Buf(h.ap(), f"ps{i}") for i, h in enumerate(
            [ps0_h, ps1_h, ps2_h, ps3_h, ps4_h, ps5_h])]
        pstb = [Buf(h.ap(), f"pst{i}") for i, h in enumerate([pst0_h, pst1_h])]

        jr = nc.gpsimd.alloc_register("jr")
        r_fl = nc.gpsimd.alloc_register("r_fl")
        r_nd = nc.gpsimd.alloc_register("r_nd")
        r_a = nc.gpsimd.alloc_register("r_a")
        r_b = nc.gpsimd.alloc_register("r_b")
        r_pc = nc.gpsimd.alloc_register("r_pc")
        gqn = [0]

        def g_branch_write(dst0_ap, dst1_ap, src_ap):
            def fn(g):
                with g.If_eq(jr, 0):
                    g.dma_start(out=dst0_ap, in_=src_ap).then_inc(sem_gq, 16)
                with g.Else():
                    g.dma_start(out=dst1_ap, in_=src_ap).then_inc(sem_gq, 16)
                gqn[0] += 16
                g.wait_ge(sem_gq, gqn[0])
                return g.engine_nop()
            return fn

        def g_flag(fid):
            return g_branch_write(flagsh[0][fid:fid + 1, :],
                                  flagsh[1][fid:fid + 1, :], one16.ap)

        def g_poll(fid):
            def fn(g):
                for k in range(2):
                    g.reg_mov(r_nd, 1)
                    with g.While(r_nd):
                        g.reg_add(r_pc, r_pc, 16)
                        g.dma_start(out=flag_s.ap, in_=flagsh[k][fid:fid + 1, 0:1]).then_inc(sem_poll, 16)
                        g.wait_ge(sem_poll, r_pc)
                        g.reg_load(r_fl, flag_s.ap[:1, :1])
                        g.reg_alu(r_nd, r_fl, 0, ALU.is_equal)
                return g.engine_nop()
            return fn

        nc.const_aps.register(F32, LN_EPS, constv.ap[:, 0:1])
        nc.const_aps.register(F32, RMS_EPS, constv.ap[:, 1:2])
        nc.const_aps.register(F32, 1.0, constv.ap[:, 2:3])
        kk.dma("sync", constv, constv_in)
        kk.dma("sync", ident_s, ident_in)
        kk.dma("sync", j_s, j_in)
        kk.dma("sync", nonce_s, nonce_in)
        for i in range(DBLK):
            kk.dma("sync", a_sc[i], a_log[i * P:(i + 1) * P, :])
            kk.dma("sync", cw_s[i], conv_w[i * P:(i + 1) * P, :])
            kk.dma("sync", cb_s[i], conv_b[i * P:(i + 1) * P, :])
            kk.dma("sync", dtb_s[i], dt_b[i * P:(i + 1) * P, :])
            kk.dma("sync", dvec_s[i], d_vec[i * P:(i + 1) * P, :])
        wdt_f32 = Buf(wstg.ap[:RK, :], "wdt_f32")
        kk.dma("sync", wdt_f32, w_dt_T)

        kk.emit("vector", lambda e: e.memset(zero16.ap, 0), writes=[zero16])
        kk.emit("vector", lambda e: e.memset(one16.ap, 1), writes=[one16])
        kk.emit("vector", lambda e: e.memset(ones_bf.ap, 1.0), writes=[ones_bf])
        kk.emit("vector", lambda e: e.memset(ones_f.ap, 1.0), writes=[ones_f])
        for i in range(DBLK):
            kk.emit("scalar", lambda e, i=i: e.activation(out=a_sc[i].ap, in_=a_sc[i].ap, func=AF.Exp),
                    reads=[a_sc[i]], writes=[a_sc[i]])
            kk.emit("vector", lambda e, i=i: e.tensor_scalar(out=a_sc[i].ap, in0=a_sc[i].ap,
                    scalar1=-1.0, scalar2=None, op0=ALU.mult), reads=[a_sc[i]], writes=[a_sc[i]])
            kk.emit("vector", lambda e, i=i: e.tensor_scalar(out=ncb_s[i].ap, in0=cb_s[i].ap,
                    scalar1=-1.0, scalar2=None, op0=ALU.mult), reads=[cb_s[i]], writes=[ncb_s[i]])
            kk.emit("vector", lambda e, i=i: e.memset(halo[i].ap, 0.0), writes=[halo[i]])
            kk.emit("vector", lambda e, i=i: e.memset(hstate[i].ap, 0.0), writes=[hstate[i]])
        kk.emit("scalar", lambda e: e.copy(out=wdt_bf.ap, in_=wdt_f32.ap),
                reads=[wdt_f32], writes=[wdt_bf])
        for i in range(DBLK):
            kk.dma("sync", wxp_stg, w_xp_T[i * P:(i + 1) * P, :])
            kk.emit("scalar", lambda e, i=i: e.copy(out=wxp_bf[i].ap, in_=wxp_stg.ap),
                    reads=[wxp_stg], writes=[wxp_bf[i]])

        def init_fn(g):
            g.reg_load(jr, j_s.ap[:1, :1])
            g.reg_mov(r_pc, 0)
            with g.If_eq(jr, 0):
                for f in range(NFLAG):
                    g.dma_start(out=flagsh[0][f:f + 1, :], in_=zero16.ap).then_inc(sem_gq, 16)
                g.dma_start(out=barsh[0], in_=nonce_s.ap).then_inc(sem_gq, 16)
            with g.Else():
                for f in range(NFLAG):
                    g.dma_start(out=flagsh[1][f:f + 1, :], in_=zero16.ap).then_inc(sem_gq, 16)
                g.dma_start(out=barsh[1], in_=nonce_s.ap).then_inc(sem_gq, 16)
            gqn[0] += 16 * (NFLAG + 1)
            g.wait_ge(sem_gq, gqn[0])
            for barx in (barsh[0], barsh[1]):
                g.reg_mov(r_nd, 1)
                with g.While(r_nd):
                    g.reg_add(r_pc, r_pc, 16)
                    g.dma_start(out=bar_s.ap, in_=barx).then_inc(sem_poll, 16)
                    g.wait_ge(sem_poll, r_pc)
                    g.reg_mov(r_nd, 0)
                    for kq in range(4):
                        g.reg_load(r_a, bar_s.ap[:1, kq:kq + 1])
                        g.reg_load(r_b, nonce_s.ap[:1, kq:kq + 1])
                        g.reg_alu(r_a, r_a, r_b, ALU.not_equal)
                        g.reg_alu(r_nd, r_nd, r_a, ALU.bitwise_or)
            return g.engine_nop()

        kk.emit("gpsimd", init_fn, reads=[j_s, nonce_s, zero16, one16], writes=[bar_s])

        kk.dma("gpsimd", wout_dr.ap.rearrange("b p c -> (b p) c"), w_out_T,
               wbufs=[wout_dr])

        def emit_dequant(src_packed, src_scales_T, n_rows_blk, n_in, dst_slab, ch_w):
            n_ch = n_in // ch_w
            pkv = Buf(dq_pk.ap[:, :n_in // 8], "pkv")
            scv16 = Buf(dq_sc16.ap[:, :n_in // 128], "scv16")
            scv = Buf(dq_sc.ap[:, :n_in // 128], "scv")
            nibv = Buf(dq_nib.ap[:, :ch_w], "nibv")
            t1 = Buf(dq_t1.ap[:, :ch_w], "t1")
            t2 = Buf(dq_t2.ap[:, :ch_w], "t2")
            t3 = Buf(dq_t3.ap[:, :ch_w], "t3")
            srep = Buf(dq_srep.ap[:, :ch_w], "srep")
            dqo = Buf(dq_out.ap[:, :ch_w], "dqo")
            for r in range(n_rows_blk):
                kk.barrier()
                kk.dma("sync", pkv, src_packed[r * P:(r + 1) * P, :])
                kk.dma("sync", scv16, src_scales_T[r * P:(r + 1) * P, :])
                kk.emit("vector", lambda e, scv=scv, scv16=scv16: e.tensor_copy(
                    out=scv.ap, in_=scv16.ap.bitcast(F16)), reads=[scv16], writes=[scv])
                for chi in range(n_ch):
                    c0 = chi * ch_w
                    pk16 = pkv.ap.bitcast(U16)[:, c0 // 4:(c0 + ch_w) // 4]
                    for kq in range(4):
                        outv = nibv.ap.rearrange("p (j four) -> p j four", four=4)[:, :, kq]
                        kk.emit("vector", lambda e, kq=kq, outv=outv, pk16=pk16: e.tensor_scalar(
                            out=outv, in0=pk16, scalar1=4 * kq, scalar2=15,
                            op0=ALU.logical_shift_right, op1=ALU.bitwise_and),
                            reads=[pkv], writes=[nibv])
                    kk.emit("vector", lambda e, t1=t1, nibv=nibv: e.tensor_scalar(
                        out=t1.ap, in0=nibv.ap, scalar1=7, scalar2=6,
                        op0=ALU.bitwise_and, op1=ALU.logical_shift_left),
                        reads=[nibv], writes=[t1])
                    kk.emit("vector", lambda e, t2=t2, nibv=nibv: e.tensor_scalar(
                        out=t2.ap, in0=nibv.ap, scalar1=8, scalar2=12,
                        op0=ALU.bitwise_and, op1=ALU.logical_shift_left),
                        reads=[nibv], writes=[t2])
                    kk.emit("vector", lambda e, t1=t1, t2=t2: e.scalar_tensor_tensor(
                        out=t1.ap, in0=t1.ap, scalar=16128, in1=t2.ap,
                        op0=ALU.add, op1=ALU.add), reads=[t1, t2], writes=[t1])
                    kk.emit("vector", lambda e, t3=t3, nibv=nibv: e.tensor_scalar(
                        out=t3.ap, in0=nibv.ap, scalar1=1, scalar2=None,
                        op0=ALU.bitwise_and), reads=[nibv], writes=[t3])
                    kk.emit("vector", lambda e, t3=t3: e.tensor_scalar(
                        out=t3.ap, in0=t3.ap, scalar1=16128, scalar2=None,
                        op0=ALU.mult), reads=[t3], writes=[t3])
                    kk.emit("vector", lambda e, t3=t3, t2=t2: e.tensor_tensor(
                        out=t3.ap, in0=t3.ap, in1=t2.ap, op=ALU.add),
                        reads=[t3, t2], writes=[t3])
                    kk.emit("vector", lambda e, nibv=nibv: e.tensor_scalar(
                        out=nibv.ap, in0=nibv.ap, scalar1=6, scalar2=None,
                        op0=ALU.bitwise_and), reads=[nibv], writes=[nibv])
                    kk.emit("vector", lambda e, t2=t2, nibv=nibv, t1=t1, t3=t3: e.select(
                        out=t2.ap, mask=nibv.ap, on_true=t1.ap, on_false=t3.ap),
                        reads=[nibv, t1, t3], writes=[t2])
                    for gi in range(ch_w // 128):
                        gg = c0 // 128 + gi
                        kk.emit("vector", lambda e, gi=gi, gg=gg, srep=srep, scv=scv:
                                e.tensor_scalar(
                            out=srep.ap[:, gi * 128:(gi + 1) * 128], in0=ones_f.ap,
                            scalar1=scv.ap[:, gg:gg + 1], scalar2=None, op0=ALU.mult),
                            reads=[ones_f, scv], writes=[srep])
                    kk.emit("vector", lambda e, dqo=dqo, t2=t2, srep=srep: e.tensor_tensor(
                        out=dqo.ap, in0=t2.ap.bitcast(BF16), in1=srep.ap, op=ALU.mult),
                        reads=[t2, srep], writes=[dqo])
                    if "dqx" in tap_out and dst_slab is wgu_dr and r == 0:
                        kk.dma("gpsimd", tap_out["dqx"][:, c0:c0 + ch_w], dqo)
                        kk.dma("gpsimd", tap_out["dqn"][:, c0:c0 + ch_w], nibv)
                    for ib in range(ch_w // 128):
                        gib = c0 // 128 + ib
                        pt = pstb[ib % 2]
                        po = dq_tpo[ib % 2]
                        kk.emit("tensor", lambda e, ib=ib, pt=pt, dqo=dqo: e.transpose(
                            pt.ap, dqo.ap[:, ib * 128:(ib + 1) * 128], ident_s.ap),
                            reads=[dqo, ident_s], writes=[pt])
                        kk.emit("scalar", lambda e, pt=pt, po=po: e.copy(out=po.ap, in_=pt.ap),
                                reads=[pt], writes=[po])
                        kk.dma("gpsimd", dst_slab.ap[gib, :, r * P:(r + 1) * P], po,
                               rbufs=[po], wbufs=[dst_slab])

        emit_dequant(gu_packed, gu_scales_T, OBLK, DM, wgu_dr, 1024)
        emit_dequant(dn_packed, dn_scales_T, DMH // P, 2 * IH, wd_dr, 1408)
        if "wgu" in tap_out:
            kk.dma("gpsimd", tap_out["wgu"], wgu_dr)
        if "wd" in tap_out:
            kk.dma("gpsimd", tap_out["wd"], wd_dr)

        kk.barrier()
        for part in range(2):
            w_src = w_in_x_T if part == 0 else w_in_z_T
            for i in range(MBLK):
                kk.dma("sync", wstg, w_src[i * P:(i + 1) * P, :])
                kk.emit("scalar", lambda e, i=i: e.copy(out=win_sb[i].ap, in_=wstg.ap),
                        reads=[wstg], writes=[win_sb[i]])

            for c in range(nch):
                t0 = c * tc
                if part == 0:
                    for tt in range(tc // P):
                        xk = xtok[tt % 2]
                        xcv = xc_t[tt % 2]
                        xnv = xn_tok[tt % 2]
                        kk.dma("sync", xk, x_full[t0 + tt * P: t0 + (tt + 1) * P, :])
                        kk.emit("vector", lambda e, xk=xk: e.tensor_reduce(
                            out=ln_m.ap, in_=xk.ap, axis=AX.X, op=ALU.add),
                            reads=[xk], writes=[ln_m])
                        kk.emit("vector", lambda e: e.tensor_scalar(out=ln_m.ap,
                                in0=ln_m.ap, scalar1=1.0 / DM, scalar2=None, op0=ALU.mult),
                                reads=[ln_m], writes=[ln_m])
                        kk.emit("vector", lambda e, xk=xk, xcv=xcv: e.tensor_scalar(
                            out=xcv.ap, in0=xk.ap, scalar1=ln_m.ap, scalar2=None,
                            op0=ALU.subtract), reads=[xk, ln_m], writes=[xcv])
                        kk.emit("scalar", lambda e, xcv=xcv: e.activation(
                            out=ln_sq.ap, in_=xcv.ap, func=AF.Square, accum_out=ln_ssq.ap),
                            reads=[xcv], writes=[ln_sq, ln_ssq])
                        kk.emit("scalar", lambda e: e.activation(out=ln_rs.ap,
                                in_=ln_ssq.ap, func=AF.Ln, scale=1.0 / DM, bias=LN_EPS),
                                reads=[ln_ssq, constv], writes=[ln_rs])
                        kk.emit("scalar", lambda e: e.activation(out=ln_rs.ap,
                                in_=ln_rs.ap, func=AF.Exp, scale=-0.5),
                                reads=[ln_rs], writes=[ln_rs])
                        kk.emit("vector", lambda e, xcv=xcv, xnv=xnv: e.tensor_scalar(
                            out=xnv.ap, in0=xcv.ap, scalar1=ln_rs.ap, scalar2=None,
                            op0=ALU.mult), reads=[xcv, ln_rs], writes=[xnv])
                        for mb in range(MBLK):
                            pt = pstb[mb % 2]
                            kk.emit("tensor", lambda e, mb=mb, pt=pt, xnv=xnv: e.transpose(
                                pt.ap, xnv.ap[:, mb * 128:(mb + 1) * 128], ident_s.ap),
                                reads=[xnv, ident_s], writes=[pt])
                            kk.emit("scalar", lambda e, mb=mb, tt=tt, pt=pt: e.copy(
                                out=xn_fm[mb].ap[:, tt * P:(tt + 1) * P], in_=pt.ap),
                                reads=[pt], writes=[xn_fm[mb]])
                    for mb in range(MBLK):
                        kk.dma("gpsimd", xn_dr.ap[mb * P:(mb + 1) * P, t0:t0 + tc],
                               xn_fm[mb], wbufs=[xn_dr])
                else:
                    for mb in range(MBLK):
                        kk.dma("sync", xn_fm[mb], xn_dr.ap[mb * P:(mb + 1) * P, t0:t0 + tc],
                               rbufs=[xn_dr])

                for eb in range(DBLK):
                    pm = psb[eb % 2]
                    for k in range(MBLK):
                        kk.emit("tensor", lambda e, k=k, eb=eb, pm=pm: e.matmul(
                            pm.ap, win_sb[k].ap[:, eb * 128:(eb + 1) * 128], xn_fm[k].ap,
                            start=(k == 0), stop=(k == MBLK - 1)),
                            reads=[win_sb[k], xn_fm[k]], writes=[pm])
                    if part == 0:
                        xl = xl_t[eb % 2]
                        cv = cv_t[eb % 2]
                        se_ = sil_e[eb % 2]
                        sr_ = sil_r[eb % 2]
                        xt_ = xt_t[eb % 2]
                        xb_ = xt_bf[eb % 2]
                        kk.emit("vector", lambda e, xl=xl, eb=eb: e.tensor_copy(
                            out=xl.ap[:, 0:3], in_=halo[eb].ap), reads=[halo[eb]], writes=[xl])
                        kk.emit("scalar", lambda e, xl=xl, pm=pm: e.copy(
                            out=xl.ap[:, 3:3 + tc], in_=pm.ap), reads=[pm], writes=[xl])
                        kk.emit("vector", lambda e, xl=xl, cv=cv, eb=eb: e.tensor_scalar(
                            out=cv.ap, in0=xl.ap[:, 0:tc], scalar1=cw_s[eb].ap[:, 0:1],
                            scalar2=None, op0=ALU.mult), reads=[xl, cw_s[eb]], writes=[cv])
                        for kq in range(1, 4):
                            kk.emit("vector", lambda e, xl=xl, cv=cv, eb=eb, kq=kq:
                                    e.scalar_tensor_tensor(out=cv.ap, in0=xl.ap[:, kq:kq + tc],
                                    scalar=cw_s[eb].ap[:, kq:kq + 1], in1=cv.ap,
                                    op0=ALU.mult, op1=ALU.add),
                                    reads=[xl, cw_s[eb], cv], writes=[cv])
                        kk.emit("vector", lambda e, xl=xl, eb=eb: e.tensor_copy(
                            out=halo[eb].ap, in_=xl.ap[:, tc:tc + 3]),
                            reads=[xl], writes=[halo[eb]])
                        kk.emit("scalar", lambda e, cv=cv, se_=se_, eb=eb: e.activation(
                            out=se_.ap, in_=cv.ap, func=AF.Exp, scale=-1.0, bias=ncb_s[eb].ap),
                            reads=[cv, ncb_s[eb]], writes=[se_])
                        kk.emit("vector", lambda e, se_=se_, sr_=sr_: e.tensor_scalar(
                            out=sr_.ap, in0=se_.ap, scalar1=1.0, scalar2=None, op0=ALU.add),
                            reads=[se_], writes=[sr_])
                        kk.emit("vector", lambda e, sr_=sr_: e.reciprocal(out=sr_.ap, in_=sr_.ap),
                                reads=[sr_], writes=[sr_])
                        kk.emit("vector", lambda e, cv=cv, sr_=sr_, xt_=xt_, eb=eb:
                                e.scalar_tensor_tensor(out=xt_.ap, in0=cv.ap,
                                scalar=cb_s[eb].ap, in1=sr_.ap, op0=ALU.add, op1=ALU.mult),
                                reads=[cv, cb_s[eb], sr_], writes=[xt_])
                        kk.emit("vector", lambda e, xt_=xt_, xb_=xb_: e.tensor_copy(
                            out=xb_.ap, in_=xt_.ap), reads=[xt_], writes=[xb_])
                        kk.dma("gpsimd", xt_dr.ap[eb * P:(eb + 1) * P, t0:t0 + tc], xt_,
                               wbufs=[xt_dr])
                        kk.emit("tensor", lambda e, eb=eb, xb_=xb_: e.matmul(
                            psb[2].ap, wxp_bf[eb].ap[:, 0:128], xb_.ap,
                            start=(eb == 0), stop=(eb == DBLK - 1)),
                            reads=[wxp_bf[eb], xb_], writes=[psb[2]])
                        kk.emit("tensor", lambda e, eb=eb, xb_=xb_: e.matmul(
                            psb[3].ap[:32, :], wxp_bf[eb].ap[:, 128:160], xb_.ap,
                            start=(eb == 0), stop=(eb == DBLK - 1)),
                            reads=[wxp_bf[eb], xb_], writes=[psb[3]])
                    else:
                        zv = z_t[eb % 2]
                        zs_ = zs_t[eb % 2]
                        se_ = sil_e[eb % 2]
                        sr_ = sil_r[eb % 2]
                        kk.emit("scalar", lambda e, se_=se_, pm=pm: e.activation(
                            out=se_.ap, in_=pm.ap, func=AF.Exp, scale=-1.0),
                            reads=[pm], writes=[se_])
                        kk.emit("scalar", lambda e, zv=zv, pm=pm: e.copy(out=zv.ap, in_=pm.ap),
                                reads=[pm], writes=[zv])
                        kk.emit("vector", lambda e, se_=se_, sr_=sr_: e.tensor_scalar(
                            out=sr_.ap, in0=se_.ap, scalar1=1.0, scalar2=None, op0=ALU.add),
                            reads=[se_], writes=[sr_])
                        kk.emit("vector", lambda e, sr_=sr_: e.reciprocal(out=sr_.ap, in_=sr_.ap),
                                reads=[sr_], writes=[sr_])
                        kk.emit("vector", lambda e, zv=zv, sr_=sr_, zs_=zs_: e.tensor_tensor(
                            out=zs_.ap, in0=zv.ap, in1=sr_.ap, op=ALU.mult),
                            reads=[zv, sr_], writes=[zs_])
                        kk.dma("gpsimd", zs_dr.ap[eb * P:(eb + 1) * P, t0:t0 + tc], zs_,
                               wbufs=[zs_dr])

                if part == 0:
                    kk.emit("scalar", lambda e: e.copy(out=pj1_s.ap, in_=psb[2].ap),
                            reads=[psb[2]], writes=[pj1_s])
                    kk.emit("scalar", lambda e: e.copy(out=pj2_s.ap, in_=psb[3].ap[:32, :]),
                            reads=[psb[3]], writes=[pj2_s])
                    kk.emit("gpsimd", g_branch_write(projsh[0][0:128, t0:t0 + tc],
                                                     projsh[1][0:128, t0:t0 + tc], pj1_s.ap),
                            reads=[pj1_s, j_s, bar_s], writes=[projsh_b])
                    kk.emit("gpsimd", g_branch_write(projsh[0][128:160, t0:t0 + tc],
                                                     projsh[1][128:160, t0:t0 + tc], pj2_s.ap),
                            reads=[pj2_s, j_s], writes=[projsh_b])
                    kk.emit("gpsimd", g_flag(c), reads=[one16, j_s, projsh_b])

        if "xn" in tap_out:
            kk.dma("gpsimd", tap_out["xn"], xn_dr)
        if "xt" in tap_out:
            kk.dma("gpsimd", tap_out["xt"], xt_dr)
        if "zs" in tap_out:
            kk.dma("gpsimd", tap_out["zs"], zs_dr)

        kk.barrier()
        poll_tok = Buf(None, "poll_tok")
        for c in range(nch):
            t0 = c * tc
            kk.emit("gpsimd", g_poll(c), reads=[j_s], writes=[poll_tok, flag_s])
            kk.dma("sync", pj1a, projsh[0][0:128, t0:t0 + tc], rbufs=[poll_tok])
            kk.dma("sync", pj1b, projsh[1][0:128, t0:t0 + tc])
            kk.dma("sync", pj2a, projsh[0][128:160, t0:t0 + tc])
            kk.dma("sync", pj2b, projsh[1][128:160, t0:t0 + tc])
            kk.emit("vector", lambda e: e.tensor_tensor(out=pj1a.ap, in0=pj1a.ap,
                    in1=pj1b.ap, op=ALU.add), reads=[pj1a, pj1b], writes=[pj1a])
            kk.emit("vector", lambda e: e.tensor_tensor(out=pj2a.ap, in0=pj2a.ap,
                    in1=pj2b.ap, op=ALU.add), reads=[pj2a, pj2b], writes=[pj2a])
            kk.emit("vector", lambda e: e.tensor_copy(out=pj1bf.ap, in_=pj1a.ap),
                    reads=[pj1a], writes=[pj1bf])
            kk.emit("vector", lambda e: e.tensor_copy(out=pjbc_bf.ap, in_=pj2a.ap),
                    reads=[pj2a], writes=[pjbc_bf])
            if "proj" in tap_out:
                kk.dma("gpsimd", tap_out["proj"][0:128, t0:t0 + tc], pj1a)
                kk.dma("gpsimd", tap_out["proj"][128:160, t0:t0 + tc], pj2a)

            for n in range(2 * NST):
                dstf = bb_t[n] if n < NST else cb_t[n - NST]
                pbc = psb[4 + (n % 2)]
                rsg = rowstg[n % 2]
                kk.dma("sync", rsg, pjbc_bf.ap[n:n + 1, :], rbufs=[pjbc_bf])
                kk.emit("tensor", lambda e, pbc=pbc, rsg=rsg: e.matmul(
                    pbc.ap, ones_bf.ap, rsg.ap, start=True, stop=True),
                    reads=[rsg, ones_bf], writes=[pbc])
                kk.emit("scalar", lambda e, pbc=pbc, dstf=dstf: e.copy(
                    out=dstf.ap, in_=pbc.ap), reads=[pbc], writes=[dstf])

            for db in range(DBLK):
                pm = psb[db % 2]
                kk.emit("tensor", lambda e, db=db, pm=pm: e.matmul(
                    pm.ap, wdt_bf.ap[:, db * 128:(db + 1) * 128], pj1bf.ap,
                    start=True, stop=True), reads=[wdt_bf, pj1bf], writes=[pm])
                kk.emit("scalar", lambda e, pm=pm, db=db: e.activation(
                    out=sp_r.ap, in_=pm.ap, func=AF.Relu, bias=dtb_s[db].ap),
                    reads=[pm, dtb_s[db]], writes=[sp_r])
                kk.emit("scalar", lambda e, pm=pm, db=db: e.activation(
                    out=sp_a.ap, in_=pm.ap, func=AF.Abs, bias=dtb_s[db].ap),
                    reads=[pm, dtb_s[db]], writes=[sp_a])
                kk.emit("scalar", lambda e: e.activation(out=sp_a.ap, in_=sp_a.ap,
                        func=AF.Exp, scale=-1.0), reads=[sp_a], writes=[sp_a])
                kk.emit("scalar", lambda e: e.activation(out=sp_a.ap, in_=sp_a.ap,
                        func=AF.Ln, bias=1.0), reads=[sp_a, constv], writes=[sp_a])
                kk.dma("sync", xt_rd, xt_dr.ap[db * P:(db + 1) * P, t0:t0 + tc],
                       rbufs=[xt_dr])
                kk.dma("sync", zs_rd, zs_dr.ap[db * P:(db + 1) * P, t0:t0 + tc],
                       rbufs=[zs_dr])
                kk.emit("vector", lambda e: e.tensor_tensor(out=dt_s.ap, in0=sp_r.ap,
                        in1=sp_a.ap, op=ALU.add), reads=[sp_r, sp_a], writes=[dt_s])
                kk.emit("vector", lambda e: e.tensor_tensor(out=w_s.ap, in0=dt_s.ap,
                        in1=xt_rd.ap, op=ALU.mult), reads=[dt_s, xt_rd], writes=[w_s])
                if "dt" in tap_out:
                    kk.dma("gpsimd", tap_out["dt"][db * P:(db + 1) * P, t0:t0 + tc], dt_s)

                for grp in range(2):
                    yp = ypart[grp]
                    for ni in range(8):
                        n = grp * 8 + ni
                        kk.emit("scalar", lambda e, db=db, n=n, ni=ni: e.activation(
                            out=da_slab.ap[:, ni * tc:(ni + 1) * tc], in_=dt_s.ap,
                            func=AF.Exp, scale=a_sc[db].ap[:, n:n + 1]),
                            reads=[dt_s, a_sc[db]], writes=[da_slab])
                        kk.emit("vector", lambda e, n=n, ni=ni: e.tensor_tensor(
                            out=wb_slab.ap[:, ni * tc:(ni + 1) * tc],
                            in0=w_s.ap, in1=bb_t[n].ap, op=ALU.mult),
                            reads=[w_s, bb_t[n]], writes=[wb_slab])
                    for ni in range(8):
                        n = grp * 8 + ni
                        kk.emit("vector", lambda e, db=db, n=n, ni=ni:
                                e.tensor_tensor_scan(
                                    out=h_slab.ap[:, ni * tc:(ni + 1) * tc],
                                    data0=da_slab.ap[:, ni * tc:(ni + 1) * tc],
                                    data1=wb_slab.ap[:, ni * tc:(ni + 1) * tc],
                                    initial=hstate[db].ap[:, n:n + 1],
                                    op0=ALU.mult, op1=ALU.add),
                                reads=[da_slab, wb_slab, hstate[db]], writes=[h_slab])
                    for ni in range(8):
                        n = grp * 8 + ni
                        kk.emit("vector", lambda e, db=db, n=n, ni=ni: e.tensor_copy(
                            out=hstate[db].ap[:, n:n + 1],
                            in_=h_slab.ap[:, (ni + 1) * tc - 1:(ni + 1) * tc]),
                            reads=[h_slab], writes=[hstate[db]])
                    for ni in range(8):
                        n = grp * 8 + ni
                        kk.emit("gpsimd", lambda e, n=n, ni=ni: e.tensor_tensor(
                            out=hc_slab.ap[:, ni * tc:(ni + 1) * tc],
                            in0=h_slab.ap[:, ni * tc:(ni + 1) * tc],
                            in1=cb_t[n].ap, op=ALU.mult),
                            reads=[h_slab, cb_t[n]], writes=[hc_slab])
                    kk.emit("vector", lambda e, yp=yp: e.tensor_reduce(
                        out=yp.ap, in_=hc_slab.ap.rearrange("p (n t) -> p t n", n=8),
                        axis=AX.X, op=ALU.add), reads=[hc_slab], writes=[yp])
                kk.emit("vector", lambda e: e.tensor_tensor(out=y_s.ap, in0=ypart[0].ap,
                        in1=ypart[1].ap, op=ALU.add), reads=[ypart[0], ypart[1]], writes=[y_s])
                if "y" in tap_out:
                    kk.dma("gpsimd", tap_out["y"][db * P:(db + 1) * P, t0:t0 + tc], y_s)
                kk.emit("vector", lambda e, db=db: e.scalar_tensor_tensor(
                    out=y_s.ap, in0=xt_rd.ap, scalar=dvec_s[db].ap, in1=y_s.ap,
                    op0=ALU.mult, op1=ALU.add), reads=[xt_rd, dvec_s[db], y_s], writes=[y_s])
                kk.emit("vector", lambda e: e.tensor_tensor(out=y_s.ap, in0=y_s.ap,
                        in1=zs_rd.ap, op=ALU.mult), reads=[y_s, zs_rd], writes=[y_s])
                kk.emit("vector", lambda e, db=db: e.tensor_copy(
                    out=yg_slab.ap[:, db * tc:(db + 1) * tc], in_=y_s.ap),
                    reads=[y_s], writes=[yg_slab])

            src = yg_slab.ap.rearrange("p (b t) -> p b t", b=DBLK)
            d0 = ysh[0][:, t0:t0 + tc].rearrange("(b p) t -> p b t", p=P)
            d1 = ysh[1][:, t0:t0 + tc].rearrange("(b p) t -> p b t", p=P)
            kk.emit("gpsimd", g_branch_write(d0, d1, src),
                    reads=[yg_slab, j_s], writes=[ysh_b])
            kk.emit("gpsimd", g_flag(nch + c), reads=[one16, j_s, ysh_b])

        kk.barrier()

        for c in range(nch):
            kk.emit("gpsimd", g_poll(nch + c), reads=[j_s], writes=[poll_tok, flag_s])
        kk.barrier()

        for C in range(nbig):
            tb0 = C * tb
            for mb in range(MBLK):
                kk.dma("sync", xn3[mb], xn_dr.ap[mb * P:(mb + 1) * P, tb0:tb0 + tb],
                       rbufs=[xn_dr])
            for ip in range(IBLK2 // 2):
                for half in range(2):
                    ob = ip + half * (OBLK // 2)
                    for k in range(MBLK):
                        kk.dma("sync",
                               Buf(wgu_slab[half].ap[:, k * 128:(k + 1) * 128], "wgu_t"),
                               wgu_dr.ap[k, :, ob * P:(ob + 1) * P],
                               rbufs=[wgu_dr], wbufs=[wgu_slab[half]])
                for sl in range(tb // 512):
                    for half in range(2):
                        pm = psb[half]
                        for k in range(MBLK):
                            kk.emit("tensor", lambda e, k=k, half=half, sl=sl, pm=pm: e.matmul(
                                pm.ap, wgu_slab[half].ap[:, k * 128:(k + 1) * 128],
                                xn3[k].ap[:, sl * 512:(sl + 1) * 512],
                                start=(k == 0), stop=(k == MBLK - 1)),
                                reads=[wgu_slab[half], xn3[k]], writes=[pm])
                    kk.emit("scalar", lambda e: e.activation(
                        out=g_e.ap, in_=psb[0].ap, func=AF.Exp, scale=-1.0),
                        reads=[psb[0]], writes=[g_e])
                    kk.emit("scalar", lambda e: e.copy(out=g_cp.ap, in_=psb[0].ap),
                            reads=[psb[0]], writes=[g_cp])
                    kk.emit("scalar", lambda e: e.copy(out=u_cp.ap, in_=psb[1].ap),
                            reads=[psb[1]], writes=[u_cp])
                    kk.emit("vector", lambda e: e.tensor_scalar(out=g_r.ap,
                            in0=g_e.ap, scalar1=1.0, scalar2=None, op0=ALU.add),
                            reads=[g_e], writes=[g_r])
                    kk.emit("vector", lambda e: e.reciprocal(out=g_r.ap, in_=g_r.ap),
                            reads=[g_r], writes=[g_r])
                    kk.emit("vector", lambda e: e.tensor_tensor(out=g_sg.ap,
                            in0=g_cp.ap, in1=g_r.ap, op=ALU.mult),
                            reads=[g_cp, g_r], writes=[g_sg])
                    kk.emit("vector", lambda e: e.tensor_tensor(out=g_sg.ap,
                            in0=g_sg.ap, in1=u_cp.ap, op=ALU.mult),
                            reads=[g_sg, u_cp], writes=[g_sg])
                    hb = h_bf[sl % 2]
                    kk.emit("vector", lambda e, hb=hb: e.tensor_copy(
                        out=hb.ap, in_=g_sg.ap), reads=[g_sg], writes=[hb])
                    dst0 = hsh[0][ip * P:(ip + 1) * P, tb0 + sl * 512: tb0 + (sl + 1) * 512]
                    dst1 = hsh[1][ip * P:(ip + 1) * P, tb0 + sl * 512: tb0 + (sl + 1) * 512]
                    kk.emit("gpsimd", g_branch_write(dst0, dst1, hb.ap),
                            reads=[hb, j_s], writes=[hsh_b])
            kk.emit("gpsimd", g_flag(2 * nch + C), reads=[one16, j_s, hsh_b])
            kk.emit("gpsimd", g_poll(2 * nch + C), reads=[j_s], writes=[poll_tok, flag_s])
            kk.barrier()

            for ib in range(IBLK2):
                kk.dma("sync", wd_sb[ib], wd_dr.ap[ib], rbufs=[wd_dr])
            for t128 in range(tb // P):
                trow = tb0 + t128 * P
                pm0 = psb[2 * (t128 % 2)]
                pm1 = psb[2 * (t128 % 2) + 1]
                hsl = h_slab3[0]
                for k2 in range(2):
                    kk.dma("sync",
                           hsl.ap.rearrange("p (b t) -> p b t", b=IBLK2)[:, k2 * 22:(k2 + 1) * 22, :],
                           hsh[k2][:, trow:trow + P].rearrange("(b p) t -> p b t", p=P),
                           rbufs=[poll_tok], wbufs=[hsl])
                for ib in range(IBLK2):
                    hl_ap = hsl.ap[:, ib * 128:(ib + 1) * 128]
                    kk.emit("tensor", lambda e, hl_ap=hl_ap, ib=ib, pm0=pm0: e.matmul(
                        pm0.ap, hl_ap, wd_sb[ib].ap[:, 0:512],
                        start=(ib == 0), stop=(ib == IBLK2 - 1)),
                        reads=[hsl, wd_sb[ib]], writes=[pm0])
                    kk.emit("tensor", lambda e, hl_ap=hl_ap, ib=ib, pm1=pm1: e.matmul(
                        pm1.ap, hl_ap, wd_sb[ib].ap[:, 512:1024],
                        start=(ib == 0), stop=(ib == IBLK2 - 1)),
                        reads=[hsl, wd_sb[ib]], writes=[pm1])
                yf = yf_t[t128 % 2]
                kk.emit("scalar", lambda e, pm0=pm0, yf=yf: e.copy(
                    out=yf.ap[:, 0:512], in_=pm0.ap), reads=[pm0], writes=[yf])
                kk.emit("scalar", lambda e, pm1=pm1, yf=yf: e.copy(
                    out=yf.ap[:, 512:1024], in_=pm1.ap), reads=[pm1], writes=[yf])
                kk.dma("gpsimd", yffn_dr.ap[trow:trow + P, :], yf, wbufs=[yffn_dr])
                kk.emit("scalar", lambda e, yf=yf: e.activation(
                    out=sqsc.ap, in_=yf.ap, func=AF.Square, accum_out=sq_p.ap),
                    reads=[yf], writes=[sqsc, sq_p])
                rowi = trow // P
                dst0 = sqsh[0][rowi:rowi + 1, :].rearrange("r p -> p r")
                dst1 = sqsh[1][rowi:rowi + 1, :].rearrange("r p -> p r")
                kk.emit("gpsimd", g_branch_write(dst0, dst1, sq_p.ap),
                        reads=[sq_p, j_s], writes=[sqsh_b])
            kk.emit("gpsimd", g_flag(2 * nch + nbig + C), reads=[one16, j_s, sqsh_b])
            kk.emit("gpsimd", g_poll(2 * nch + nbig + C), reads=[j_s], writes=[poll_tok, flag_s])
            kk.barrier()

            for ib in range(WOBLK):
                kk.dma("sync", wout_sb[ib], wout_dr.ap[ib], rbufs=[wout_dr])
            for t128 in range(tb // P):
                trow = tb0 + t128 * P
                rowi = trow // P
                pm0 = psb[2 + (t128 % 2)]
                pm1 = psb[4 + (t128 % 2)]
                ysl = y_slab[t128 % 2]
                for k2 in range(2):
                    kk.dma("sync",
                           ysl.ap.rearrange("p (b t) -> p b t", b=32)[:, k2 * 16:(k2 + 1) * 16, :],
                           ysh[k2][:, trow:trow + P].rearrange("(b p) t -> p b t", p=P),
                           rbufs=[poll_tok], wbufs=[ysl])
                for ib in range(WOBLK):
                    yl_ap = ysl.ap[:, ib * 128:(ib + 1) * 128]
                    kk.emit("tensor", lambda e, yl_ap=yl_ap, pm0=pm0, ib=ib: e.matmul(
                        pm0.ap, yl_ap, wout_sb[ib].ap[:, 0:512],
                        start=(ib == 0), stop=(ib == WOBLK - 1)),
                        reads=[ysl, wout_sb[ib]], writes=[pm0])
                    kk.emit("tensor", lambda e, yl_ap=yl_ap, pm1=pm1, ib=ib: e.matmul(
                        pm1.ap, yl_ap, wout_sb[ib].ap[:, 512:1024],
                        start=(ib == 0), stop=(ib == WOBLK - 1)),
                        reads=[ysl, wout_sb[ib]], writes=[pm1])
                sv = s_ev[t128 % 2]
                kk.emit("scalar", lambda e, pm0=pm0, sv=sv: e.copy(
                    out=sv.ap[:, 0:512], in_=pm0.ap), reads=[pm0], writes=[sv])
                kk.emit("scalar", lambda e, pm1=pm1, sv=sv: e.copy(
                    out=sv.ap[:, 512:1024], in_=pm1.ap), reads=[pm1], writes=[sv])
                if "s" in tap_out:
                    kk.dma("gpsimd", tap_out["s"][trow:trow + P, :], sv)
                kk.dma("sync", sq0, sqsh[0][rowi:rowi + 1, :].rearrange("r p -> p r"),
                       rbufs=[poll_tok])
                kk.dma("sync", sq1, sqsh[1][rowi:rowi + 1, :].rearrange("r p -> p r"),
                       rbufs=[poll_tok])
                kk.emit("vector", lambda e: e.tensor_tensor(out=sq0.ap, in0=sq0.ap,
                        in1=sq1.ap, op=ALU.add), reads=[sq0, sq1], writes=[sq0])
                kk.emit("scalar", lambda e: e.activation(out=rstd3.ap, in_=sq0.ap,
                        func=AF.Ln, scale=1.0 / DM, bias=RMS_EPS), reads=[sq0, constv], writes=[rstd3])
                kk.emit("scalar", lambda e: e.activation(out=rstd3.ap, in_=rstd3.ap,
                        func=AF.Exp, scale=-0.5), reads=[rstd3], writes=[rstd3])
                kk.dma("sync", yfr, yffn_dr.ap[trow:trow + P, :], rbufs=[yffn_dr])
                kk.emit("vector", lambda e: e.tensor_scalar(out=yfr.ap, in0=yfr.ap,
                        scalar1=rstd3.ap, scalar2=None, op0=ALU.mult),
                        reads=[yfr, rstd3], writes=[yfr])
                kk.emit("scalar", lambda e: e.activation(out=g3e.ap, in_=yfr.ap,
                        func=AF.Exp, scale=-1.0), reads=[yfr], writes=[g3e])
                kk.emit("vector", lambda e: e.tensor_scalar(out=g3r.ap, in0=g3e.ap,
                        scalar1=1.0, scalar2=None, op0=ALU.add), reads=[g3e], writes=[g3r])
                kk.emit("vector", lambda e: e.reciprocal(out=g3r.ap, in_=g3r.ap),
                        reads=[g3r], writes=[g3r])
                kk.emit("vector", lambda e: e.tensor_tensor(out=yfr.ap, in0=yfr.ap,
                        in1=g3r.ap, op=ALU.mult), reads=[yfr, g3r], writes=[yfr])
                ov = o_t[t128 % 2]
                kk.dma("sync", xh_t, x_halfi[trow:trow + P, :])
                kk.emit("vector", lambda e, sv=sv, ov=ov: e.tensor_tensor(
                    out=ov.ap, in0=sv.ap, in1=yfr.ap, op=ALU.mult),
                    reads=[sv, yfr], writes=[ov])
                kk.emit("vector", lambda e, ov=ov: e.tensor_tensor(
                    out=ov.ap, in0=ov.ap, in1=xh_t.ap, op=ALU.add),
                    reads=[ov, xh_t], writes=[ov])
                kk.dma("sync", out_half[trow:trow + P, :], ov)
            kk.barrier()

        if "yffn" in tap_out:
            kk.dma("gpsimd", tap_out["yffn"], yffn_dr)
        if "hh" in tap_out:
            kk.dma("gpsimd", tap_out["hh"][0], hsh[0], rbufs=[hsh_b])
            kk.dma("gpsimd", tap_out["hh"][1], hsh[1], rbufs=[hsh_b])
        if "wo" in tap_out:
            kk.dma("gpsimd", tap_out["wo"], wout_dr)
        if "yy" in tap_out:
            kk.dma("gpsimd", tap_out["yy"][0], ysh[0], rbufs=[ysh_b])
            kk.dma("gpsimd", tap_out["yy"][1], ysh[1], rbufs=[ysh_b])

        kk.barrier()
        for en in ("sync", "gpsimd"):
            for s in range(K.NSLOT):
                if kk.slot_cnt[en][s]:
                    kk._wait(en, (en, "d", s), kk.slot_cnt[en][s])

        kk.flush(block)

    nc._kk = kk
    return nc



_IDENT = np.eye(128, dtype=ml_dtypes.bfloat16)


def _core_inputs(core, inp):
    b, j = core // 2, core & 1
    x = np.ascontiguousarray(inp["L0_attn_hidden_states"][b])
    wi = inp["L0_attn_in_proj_w"]
    di0, di1 = j * DIH, (j + 1) * DIH
    return {
        "x_full": x,
        "x_halfi": np.ascontiguousarray(x[:, j * DMH:(j + 1) * DMH]),
        "w_in_x_T": np.ascontiguousarray(wi[di0:di1, :].T),
        "w_in_z_T": np.ascontiguousarray(wi[4096 + di0:4096 + di1, :].T),
        "conv_w": np.ascontiguousarray(inp["L0_attn_conv_w"][di0:di1]),
        "conv_b": np.ascontiguousarray(inp["L0_attn_conv_b"][di0:di1, None]),
        "w_xp_T": np.ascontiguousarray(inp["L0_attn_x_proj_w"][:, di0:di1].T),
        "w_dt_T": np.ascontiguousarray(inp["L0_attn_dt_proj_w"][di0:di1, :].T),
        "dt_b": np.ascontiguousarray(inp["L0_attn_dt_proj_b"][di0:di1, None]),
        "a_log": np.ascontiguousarray(inp["L0_attn_A_log"][di0:di1]),
        "d_vec": np.ascontiguousarray(inp["L0_attn_D"][di0:di1, None]),
        "w_out_T": np.ascontiguousarray(
            inp["L0_attn_out_proj_w"].T[:, j * DMH:(j + 1) * DMH]),
        "gu_packed": np.ascontiguousarray(np.concatenate([
            inp["L0_ffn_gate_up_packed"][j * IH:(j + 1) * IH],
            inp["L0_ffn_gate_up_packed"][5632 + j * IH:5632 + (j + 1) * IH]], axis=0)),
        "gu_scales_T": np.ascontiguousarray(np.concatenate([
            inp["L0_ffn_gate_up_scales"][:, j * IH:(j + 1) * IH],
            inp["L0_ffn_gate_up_scales"][:, 5632 + j * IH:5632 + (j + 1) * IH]],
            axis=1).T.copy()).view(np.uint16),
        "dn_packed": np.ascontiguousarray(
            inp["L0_ffn_down_packed"][j * DMH:(j + 1) * DMH]),
        "dn_scales_T": np.ascontiguousarray(
            inp["L0_ffn_down_scales"][:, j * DMH:(j + 1) * DMH].T).view(np.uint16),
        "j_in": np.array([[j]], dtype=np.uint32),
        "ident": _IDENT,
        "constv": np.broadcast_to(
            np.array([LN_EPS, RMS_EPS, 1.0], np.float32), (128, 3)).copy(),
    }


_prog_cache = {}


def kernel(**inputs):
    t_tokens = inputs["L0_attn_hidden_states"].shape[1]
    if t_tokens not in _prog_cache:
        _prog_cache[t_tokens] = build_program(t_tokens=t_tokens)
    nc = _prog_cache[t_tokens]
    nonce = np.random.randint(1, 2**31 - 1, size=(1, 4)).astype(np.uint32)
    in_maps = []
    for core in range(8):
        m = _core_inputs(core, inputs)
        m["nonce_in"] = nonce
        in_maps.append(m)
    res = run_bass_kernel_spmd(nc, in_maps, core_ids=list(range(8)))
    out = np.empty((4, t_tokens, DM), dtype=np.float32)
    for core in range(8):
        b, j = core // 2, core & 1
        out[b][:, j * DMH:(j + 1) * DMH] = res.results[core]["out_half"]
    return out


# version 05 (final)
# speedup vs baseline: 1.2316x; 1.0285x over previous
import numpy as np
import ml_dtypes
import concourse.bass as bass
import concourse.mybir as mybir
from concourse.bass_utils import run_bass_kernel_spmd

F32 = mybir.dt.float32
BF16 = mybir.dt.bfloat16
F16 = mybir.dt.float16
I32 = mybir.dt.int32
U32 = mybir.dt.uint32
U16 = mybir.dt.uint16
AF = mybir.ActivationFunctionType
ALU = mybir.AluOpType
AX = mybir.AxisListType

P = 128
DM = 2048
DIH = 2048
NST = 16
RK = 128
IH = 2816
GU = 2 * IH
DMH = 1024
LN_EPS = 1e-5
RMS_EPS = 1e-6
DBLK = DIH // P
MBLK = DM // P
OBLK = GU // P
IBLK2 = 2 * IH // P
WOBLK = 4096 // P

ENGS = ("sync", "vector", "scalar", "tensor", "gpsimd")


class Op:
    __slots__ = ("eng", "idx", "sem_val", "dma", "pend_inc", "label", "slot", "slot_val")

    def __init__(self, eng, idx, dma):
        self.eng = eng
        self.idx = idx
        self.dma = dma
        self.sem_val = None
        self.pend_inc = 0


class Buf:
    __slots__ = ("ap", "wop", "rops", "name")

    def __init__(self, ap, name=""):
        self.ap = ap
        self.wop = None
        self.rops = []
        self.name = name


class K:

    NSLOT = 16

    def __init__(self, nc):
        self.nc = nc
        self.ops = {e: [] for e in ENGS}
        self.clos = {e: [] for e in ENGS}
        self.milestones = {e: 0 for e in ENGS}
        self.dma_emitted = {e: 0 for e in ENGS}
        self.slot_cnt = {e: [0] * self.NSLOT for e in ENGS}
        self.waited = {e: {} for e in ENGS}
        self.sems = {}

    def _milestone(self, op):
        if op.dma:
            return (op.eng, "d", op.slot), op.slot_val
        if op.sem_val is None:
            last = self.ops[op.eng][-1]
            while last.dma:
                for o in reversed(self.ops[op.eng]):
                    if not o.dma and o.idx >= op.idx:
                        last = o
                        break
                else:
                    last = None
                break
            if last is None:
                lastd = self.ops[op.eng][-1]
                return (op.eng, "d", lastd.slot), lastd.slot_val
            if last.sem_val is None:
                self.milestones[op.eng] += 1
                last.sem_val = self.milestones[op.eng]
                last.pend_inc += 1
            return (op.eng, "c"), last.sem_val
        return (op.eng, "c"), op.sem_val

    def _wait(self, eng, key, val):
        if self.waited[eng].get(key, 0) >= val:
            return
        self.waited[eng][key] = val
        self.clos[eng].append(("wait", key, val))

    def _dep(self, eng, op):
        key, val = self._milestone(op)
        self._wait(eng, key, val)

    def emit(self, eng, fn, reads=(), writes=(), dma=False, label=""):
        for b in reads:
            if b.wop is not None:
                self._dep(eng, b.wop)
        for b in writes:
            if b.wop is not None and b.wop.eng != eng:
                self._dep(eng, b.wop)
            for r in b.rops:
                if r.eng != eng:
                    self._dep(eng, r)
        op = Op(eng, len(self.ops[eng]), dma)
        op.label = label
        if dma:
            i = self.dma_emitted[eng]
            slot = i % self.NSLOT
            if self.slot_cnt[eng][slot] > 0:
                self._wait(eng, (eng, "d", slot), self.slot_cnt[eng][slot])
            self.slot_cnt[eng][slot] += 16
            op.slot = slot
            op.slot_val = self.slot_cnt[eng][slot]
            self.dma_emitted[eng] += 1
        self.ops[eng].append(op)
        self.clos[eng].append(("op", op, fn))
        for b in reads:
            b.rops.append(op)
        for b in writes:
            b.wop = op
            b.rops = []
        return op

    def dma(self, eng, out, in_, rbufs=(), wbufs=(), **kw):
        reads = list(rbufs)
        writes = list(wbufs)
        if isinstance(in_, Buf):
            reads.append(in_)
            in_ap = in_.ap
        else:
            in_ap = in_
        if isinstance(out, Buf):
            writes.append(out)
            out_ap = out.ap
        else:
            out_ap = out
        return self.emit(eng, lambda e: e.dma_start(out=out_ap, in_=in_ap, **kw),
                         reads=reads, writes=writes, dma=True)

    def barrier(self):
        marks = []
        for en in ENGS:
            for s in range(self.NSLOT):
                if self.slot_cnt[en][s]:
                    marks.append(((en, "d", s), self.slot_cnt[en][s]))
            for o in reversed(self.ops[en]):
                if not o.dma:
                    key, val = self._milestone(o)
                    marks.append((key, val))
                    break
        for en in ENGS:
            for key, val in marks:
                if key[0] != en:
                    self._wait(en, key, val)
                elif len(key) == 3:
                    self._wait(en, key, val)

    def flush(self, block):
        makers = {"sync": block.sync, "vector": block.vector,
                  "scalar": block.scalar, "tensor": block.tensor,
                  "gpsimd": block.gpsimd}
        for en in ENGS:
            items = self.clos[en]
            if not items:
                continue

            def body(e, items=items, en=en):
                for it in items:
                    if it[0] == "wait":
                        _, key, val = it
                        e.wait_ge(self.sems[key], val)
                    else:
                        _, op, fn = it
                        inst = fn(e)
                        if op.dma:
                            inst.then_inc(self.sems[(en, "d", op.slot)], 16)
                        if op.pend_inc:
                            inst.then_inc(self.sems[(en, "c")], op.pend_inc)
            makers[en](body)


def build_program(t_tokens=8192, tc=512, taps=()):
    nch = t_tokens // tc
    nbig = 4
    cpb = nch // nbig
    tb = cpb * tc
    nc = bass.Bass("TRN2", target_bir_lowering=False)

    def din(name, shape, dt=F32):
        return nc.dram_tensor(name, shape, dt, kind="ExternalInput").ap()

    x_full = din("x_full", [t_tokens, DM])
    x_halfi = din("x_halfi", [t_tokens, DMH])
    w_in_x_T = din("w_in_x_T", [DM, DIH])
    w_in_z_T = din("w_in_z_T", [DM, DIH])
    conv_w = din("conv_w", [DIH, 4])
    conv_b = din("conv_b", [DIH, 1])
    w_xp_T = din("w_xp_T", [DIH, 160])
    w_dt_T = din("w_dt_T", [RK, DIH])
    dt_b = din("dt_b", [DIH, 1])
    a_log = din("a_log", [DIH, NST])
    d_vec = din("d_vec", [DIH, 1])
    w_out_T = din("w_out_T", [4096, DMH])
    gu_packed = din("gu_packed", [GU, DM // 8], I32)
    gu_scales_T = din("gu_scales_T", [GU, DM // 128], U16)
    dn_packed = din("dn_packed", [DMH, 2 * IH // 8], I32)
    dn_scales_T = din("dn_scales_T", [DMH, 2 * IH // 128], U16)
    j_in = din("j_in", [1, 1], U32)
    nonce_in = din("nonce_in", [1, 4], U32)
    ident_in = din("ident", [128, 128], BF16)
    constv_in = din("constv", [128, 3])

    out_half = nc.dram_tensor("out_half", [t_tokens, DMH], F32, kind="ExternalOutput").ap()

    def dloc(name, shape, dt):
        return Buf(nc.dram_tensor(name, shape, dt, kind="Internal").ap(), name)

    xn_dr = dloc("xn_dr", [DM, t_tokens], BF16)
    xt_dr = dloc("xt_dr", [DIH, t_tokens], F32)
    zs_dr = dloc("zs_dr", [DIH, t_tokens], F32)
    wout_dr = dloc("wout_dr", [WOBLK, 128, DMH], BF16)
    wgu_dr = dloc("wgu_dr", [MBLK, 128, GU], BF16)
    wd_dr = dloc("wd_dr", [IBLK2, 128, DMH], BF16)
    yffn_dr = dloc("yffn_dr", [t_tokens, DMH], F32)

    def dsh(name, shape, dt):
        return nc.dram_tensor(name, shape, dt, kind="Internal", addr_space="Shared").ap()

    projsh = [dsh(f"projsh{k}", [160, t_tokens], F32) for k in range(2)]
    ysh = [dsh(f"ysh{k}", [DIH, t_tokens], BF16) for k in range(2)]
    hsh = [dsh(f"hsh{k}", [IH, t_tokens], BF16) for k in range(2)]
    sqsh = [dsh(f"sqsh{k}", [t_tokens // P, P], F32) for k in range(2)]
    NFLAG = 2 * nch + 2 * nbig
    flagsh = [dsh(f"flagsh{k}", [NFLAG, 16], U32) for k in range(2)]
    barsh = [dsh(f"barsh{k}", [1, 4], U32) for k in range(2)]
    projsh_b = Buf(None, "projsh")
    ysh_b = Buf(None, "ysh")
    hsh_b = Buf(None, "hsh")
    sqsh_b = Buf(None, "sqsh")

    tap_out = {}
    tap_shapes = {
        "xn": ([DM, t_tokens], BF16), "xt": ([DIH, t_tokens], F32),
        "zs": ([DIH, t_tokens], F32), "proj": ([160, t_tokens], F32),
        "dt": ([DIH, t_tokens], F32), "y": ([DIH, t_tokens], F32),
        "wgu": ([MBLK, 128, GU], BF16), "wd": ([IBLK2, 128, DMH], BF16),
        "s": ([t_tokens, DMH], F32), "yffn": ([t_tokens, DMH], F32),
        "dqx": ([128, DM], BF16), "dqn": ([128, DM], U16),
        "hh": ([2, IH, t_tokens], BF16), "yy": ([2, DIH, t_tokens], BF16),
        "wo": ([WOBLK, 128, DMH], BF16),
    }
    for t in taps:
        sh, dt = tap_shapes[t]
        tap_out[t] = nc.dram_tensor("tap_" + t, sh, dt, kind="ExternalOutput").ap()

    def sbraw(name, shape, dt):
        return nc.alloc_sbuf_tensor(name, shape, dt).ap()

    def sb(name, shape, dt):
        return Buf(sbraw(name, shape, dt), name)

    ident_s = sb("ident_s", [128, 128], BF16)
    j_s = sb("j_s", [1, 1], U32)
    nonce_s = sb("nonce_s", [1, 4], U32)
    bar_s = sb("bar_s", [1, 4], U32)
    zero16 = sb("zero16", [1, 16], U32)
    one16 = sb("one16", [1, 16], U32)
    flag_s = sb("flag_s", [1, 1], U32)
    ones_bf = sb("ones_bf", [1, 128], BF16)
    ones_f = sb("ones_f", [128, 128], F32)

    a_sc = [sb(f"a_sc{i}", [P, NST], F32) for i in range(DBLK)]
    cw_s = [sb(f"cw_s{i}", [P, 4], F32) for i in range(DBLK)]
    cb_s = [sb(f"cb_s{i}", [P, 1], F32) for i in range(DBLK)]
    ncb_s = [sb(f"ncb_s{i}", [P, 1], F32) for i in range(DBLK)]
    dtb_s = [sb(f"dtb_s{i}", [P, 1], F32) for i in range(DBLK)]
    dvec_s = [sb(f"dvec_s{i}", [P, 1], F32) for i in range(DBLK)]
    hstate = [sb(f"hstate{i}", [P, NST], F32) for i in range(DBLK)]
    halo = [sb(f"halo{i}", [P, 3], F32) for i in range(DBLK)]
    wdt_bf = sb("wdt_bf", [RK, DIH], BF16)
    wxp_stg = sb("wxp_stg", [P, 160], F32)
    wxp_bf = [sb(f"wxp_bf{i}", [P, 160], BF16) for i in range(DBLK)]
    wstg = sb("wstg", [P, DIH], F32)
    ln_m = sb("ln_m", [P, 1], F32)
    ln_ssq = sb("ln_ssq", [P, 1], F32)
    ln_rs = sb("ln_rs", [P, 1], F32)
    sq_p = sb("sq_p", [P, 1], F32)
    sq0 = sb("sq0", [P, 1], F32)
    sq1 = sb("sq1", [P, 1], F32)
    rstd3 = sb("rstd3", [P, 1], F32)
    constv = sb("constv_s", [P, 3], F32)

    ARENA_HW = 94000
    arena_ap = sbraw("arena", [P, ARENA_HW], BF16)
    _dt_bytes = {F32: 4, BF16: 2, F16: 2, I32: 4, U16: 2, U32: 4}

    class Arena:
        def __init__(self):
            self.cur = 0

        def reset(self):
            self.cur = 0

        def al(self, name, cols, dt, parts=P):
            b = _dt_bytes[dt]
            off = (self.cur + 3) & ~3
            w_hw = (cols * b + 1) // 2
            assert off + w_hw * 2 <= ARENA_HW * 2, f"arena overflow at {name}"
            view = arena_ap[:parts, off // 2: off // 2 + w_hw]
            self.cur = off + w_hw * 2
            if dt != BF16:
                view = view.bitcast(dt)
                if view.shape[1] != cols:
                    view = view[:, :cols]
            else:
                view = view[:, :cols]
            return Buf(view, name)

    AR = Arena()

    wstg2 = AR.al("wstg2", DMH, F32)
    wo_bf = AR.al("wo_bf", DMH, BF16)
    DQCH_GU = 1024
    DQCH_DN = 1408
    DQCH = 1408
    dq_pk = AR.al("dq_pk", 704, I32)
    dq_nib = AR.al("dq_nib", DQCH, U16)
    dq_t1 = AR.al("dq_t1", DQCH, U16)
    dq_t2 = AR.al("dq_t2", DQCH, U16)
    dq_t3 = AR.al("dq_t3", DQCH, U16)
    dq_sc16 = AR.al("dq_sc16", 44, U16)
    dq_sc = AR.al("dq_sc", 44, F32)
    dq_srep = AR.al("dq_srep", DQCH, F32)
    dq_out = AR.al("dq_out", DQCH, BF16)
    dq_tpo = [AR.al(f"dq_tpo{i}", 128, BF16) for i in range(2)]

    AR.reset()
    win_sb = [AR.al(f"win{i}", DIH, BF16) for i in range(MBLK)]
    xtok = [AR.al(f"xtok{i}", DM, F32) for i in range(2)]
    xc_t = [AR.al(f"xc_t{i}", DM, F32) for i in range(2)]
    xn_tok = [AR.al(f"xn_tok{i}", DM, BF16) for i in range(2)]
    ln_sq = AR.al("ln_sq", DM, F32)
    xn_fm = [AR.al(f"xn_fm{i}", tc, BF16) for i in range(MBLK)]
    xl_t = [AR.al(f"xl_t{i}", tc + 3, F32) for i in range(2)]
    cv_t = [AR.al(f"cv_t{i}", tc, F32) for i in range(2)]
    sil_e = [AR.al(f"sil_e{i}", tc, F32) for i in range(2)]
    sil_r = [AR.al(f"sil_r{i}", tc, F32) for i in range(2)]
    xt_t = [AR.al(f"xt_t{i}", tc, F32) for i in range(2)]
    xt_bf = [AR.al(f"xt_bf{i}", tc, BF16) for i in range(2)]
    z_t = [AR.al(f"z_t{i}", tc, F32) for i in range(2)]
    zs_t = [AR.al(f"zs_t{i}", tc, F32) for i in range(2)]
    pj1_s = AR.al("pj1_s", tc, F32)
    pj2_s = AR.al("pj2_s", tc, F32, parts=32)

    AR.reset()
    pj1a = AR.al("pj1a", tc, F32)
    pj1b = AR.al("pj1b", tc, F32)
    pj1bf = AR.al("pj1bf", tc, BF16)
    pj2a = AR.al("pj2a", tc, F32, parts=32)
    pj2b = AR.al("pj2b", tc, F32, parts=32)
    pjbc_bf = AR.al("pjbc_bf", tc, BF16, parts=32)
    rowstg = [AR.al(f"rowstg{i}", tc, BF16, parts=1) for i in range(2)]
    bb_t = [AR.al(f"bb_t{i}", tc, BF16) for i in range(NST)]
    cb_t = [AR.al(f"cb_t{i}", tc, F32) for i in range(NST)]
    dt_s = [AR.al(f"dt_s{i}", tc, F32) for i in range(2)]
    sp_r = AR.al("sp_r", tc, F32)
    sp_a = AR.al("sp_a", tc, F32)
    w_s = [AR.al(f"w_s{i}", tc, F32) for i in range(2)]
    da_slab = AR.al("da_slab", 8 * tc, F32)
    wb_slab = AR.al("wb_slab", 8 * tc, F32)
    h_slab = AR.al("h_slab", 8 * tc, F32)
    hc_slab = [AR.al(f"hc_slab{i}", 8 * tc, F32) for i in range(2)]
    y_s = [AR.al(f"y_s{i}", tc, F32) for i in range(2)]
    yg_slab = AR.al("yg_slab", DBLK * tc, BF16)
    xt_rd = [AR.al(f"xt_rd{i}", tc, F32) for i in range(2)]
    zs_rd = [AR.al(f"zs_rd{i}", tc, F32) for i in range(2)]
    da_sl = [Buf(da_slab.ap[:, i * tc:(i + 1) * tc], f"da{i}") for i in range(8)]
    wb_sl = [Buf(wb_slab.ap[:, i * tc:(i + 1) * tc], f"wb{i}") for i in range(8)]
    h_sl = [Buf(h_slab.ap[:, i * tc:(i + 1) * tc], f"h{i}") for i in range(8)]
    hc_sl = [[Buf(hc_slab[r].ap[:, i * tc:(i + 1) * tc], f"hc{r}_{i}")
              for i in range(8)] for r in range(2)]
    hst_g = [[Buf(hstate[d].ap[:, g * 8:(g + 1) * 8], f"hst{d}_{g}")
              for g in range(2)] for d in range(DBLK)]

    AR.reset()
    xn3 = [AR.al(f"xn3_{i}", tb, BF16) for i in range(MBLK)]
    ph3_lo_hw = AR.cur
    AR.reset()
    wd_sb = [AR.al(f"wd{i}", DMH, BF16) for i in range(IBLK2)]
    AR.reset()
    wout_sb = [AR.al(f"wout{i}", DMH, BF16) for i in range(WOBLK)]
    AR.cur = max(ph3_lo_hw, IBLK2 * 2048 + 64, WOBLK * 2048 + 64, AR.cur)
    g_e = AR.al("g_e", 512, F32)
    g_cp = AR.al("g_cp", 512, F32)
    g_r = AR.al("g_r", 512, F32)
    g_sg = AR.al("g_sg", 512, F32)
    u_cp = AR.al("u_cp", 512, F32)
    h_bf = [AR.al(f"h_bf{i}", 512, BF16) for i in range(2)]
    wgu_slab = [AR.al(f"wgu_slab{i}", MBLK * 128, BF16) for i in range(2)]
    h_slab3 = [AR.al(f"h_slab3{i}", IBLK2 * 128, BF16) for i in range(1)]
    yf_t = [AR.al(f"yf_t{i}", DMH, F32) for i in range(2)]
    sqsc = AR.al("sqsc", DMH, F32)
    y_slab = [AR.al(f"y_slab{i}", 32 * 128, BF16) for i in range(2)]
    s_ev = [AR.al(f"s_ev{i}", DMH, F32) for i in range(2)]
    yfr = AR.al("yfr", DMH, F32)
    g3e = AR.al("g3e", DMH, F32)
    g3r = AR.al("g3r", DMH, F32)
    xh_t = AR.al("xh_t", DMH, F32)
    o_t = [AR.al(f"o_t{i}", DMH, F32) for i in range(2)]

    import contextlib
    with contextlib.ExitStack() as _stack:
        ent = _stack.enter_context
        cs_sync = ent(nc.semaphore("cs_sync")); cs_vec = ent(nc.semaphore("cs_vec"))
        cs_sca = ent(nc.semaphore("cs_sca")); cs_ten = ent(nc.semaphore("cs_ten"))
        cs_gps = ent(nc.semaphore("cs_gps"))
        sem_poll = ent(nc.semaphore("s_poll")); sem_gq = ent(nc.semaphore("s_gq"))
        dpool = {}
        for qe in ("sync", "gpsimd"):
            for s in range(K.NSLOT):
                dpool[(qe, s)] = ent(nc.semaphore(f"dq_{qe}_{s}"))
        ps0_h = ent(nc.psum_tensor("ps0", [P, 512], F32))
        ps1_h = ent(nc.psum_tensor("ps1", [P, 512], F32))
        ps2_h = ent(nc.psum_tensor("ps2", [P, 512], F32))
        ps3_h = ent(nc.psum_tensor("ps3", [P, 512], F32))
        ps4_h = ent(nc.psum_tensor("ps4", [P, 512], F32))
        ps5_h = ent(nc.psum_tensor("ps5", [P, 512], F32))
        pst0_h = ent(nc.psum_tensor("pst0", [P, 128], BF16))
        pst1_h = ent(nc.psum_tensor("pst1", [P, 128], BF16))
        block = ent(nc.Block())
        kk = K(nc)
        for en, ch in (("sync", cs_sync), ("vector", cs_vec),
                       ("scalar", cs_sca), ("tensor", cs_ten),
                       ("gpsimd", cs_gps)):
            kk.sems[(en, "c")] = ch
        for (qe, s), h in dpool.items():
            kk.sems[(qe, "d", s)] = h
        psb = [Buf(h.ap(), f"ps{i}") for i, h in enumerate(
            [ps0_h, ps1_h, ps2_h, ps3_h, ps4_h, ps5_h])]
        pstb = [Buf(h.ap(), f"pst{i}") for i, h in enumerate([pst0_h, pst1_h])]

        jr = nc.gpsimd.alloc_register("jr")
        r_fl = nc.gpsimd.alloc_register("r_fl")
        r_nd = nc.gpsimd.alloc_register("r_nd")
        r_a = nc.gpsimd.alloc_register("r_a")
        r_b = nc.gpsimd.alloc_register("r_b")
        r_pc = nc.gpsimd.alloc_register("r_pc")
        gqn = [0]

        def g_branch_write(dst0_ap, dst1_ap, src_ap):
            def fn(g):
                with g.If_eq(jr, 0):
                    g.dma_start(out=dst0_ap, in_=src_ap).then_inc(sem_gq, 16)
                with g.Else():
                    g.dma_start(out=dst1_ap, in_=src_ap).then_inc(sem_gq, 16)
                gqn[0] += 16
                g.wait_ge(sem_gq, gqn[0])
                return g.engine_nop()
            return fn

        def g_flag(fid):
            return g_branch_write(flagsh[0][fid:fid + 1, :],
                                  flagsh[1][fid:fid + 1, :], one16.ap)

        def g_poll(fid):
            def fn(g):
                for k in range(2):
                    g.reg_mov(r_nd, 1)
                    with g.While(r_nd):
                        g.reg_add(r_pc, r_pc, 16)
                        g.dma_start(out=flag_s.ap, in_=flagsh[k][fid:fid + 1, 0:1]).then_inc(sem_poll, 16)
                        g.wait_ge(sem_poll, r_pc)
                        g.reg_load(r_fl, flag_s.ap[:1, :1])
                        g.reg_alu(r_nd, r_fl, 0, ALU.is_equal)
                return g.engine_nop()
            return fn

        nc.const_aps.register(F32, LN_EPS, constv.ap[:, 0:1])
        nc.const_aps.register(F32, RMS_EPS, constv.ap[:, 1:2])
        nc.const_aps.register(F32, 1.0, constv.ap[:, 2:3])
        kk.dma("sync", constv, constv_in)
        kk.dma("sync", ident_s, ident_in)
        kk.dma("sync", j_s, j_in)
        kk.dma("sync", nonce_s, nonce_in)
        for i in range(DBLK):
            kk.dma("sync", a_sc[i], a_log[i * P:(i + 1) * P, :])
            kk.dma("sync", cw_s[i], conv_w[i * P:(i + 1) * P, :])
            kk.dma("sync", cb_s[i], conv_b[i * P:(i + 1) * P, :])
            kk.dma("sync", dtb_s[i], dt_b[i * P:(i + 1) * P, :])
            kk.dma("sync", dvec_s[i], d_vec[i * P:(i + 1) * P, :])
        wdt_f32 = Buf(wstg.ap[:RK, :], "wdt_f32")
        kk.dma("sync", wdt_f32, w_dt_T)

        kk.emit("vector", lambda e: e.memset(zero16.ap, 0), writes=[zero16])
        kk.emit("vector", lambda e: e.memset(one16.ap, 1), writes=[one16])
        kk.emit("vector", lambda e: e.memset(ones_bf.ap, 1.0), writes=[ones_bf])
        kk.emit("vector", lambda e: e.memset(ones_f.ap, 1.0), writes=[ones_f])
        for i in range(DBLK):
            kk.emit("scalar", lambda e, i=i: e.activation(out=a_sc[i].ap, in_=a_sc[i].ap, func=AF.Exp),
                    reads=[a_sc[i]], writes=[a_sc[i]])
            kk.emit("vector", lambda e, i=i: e.tensor_scalar(out=a_sc[i].ap, in0=a_sc[i].ap,
                    scalar1=-1.0, scalar2=None, op0=ALU.mult), reads=[a_sc[i]], writes=[a_sc[i]])
            kk.emit("vector", lambda e, i=i: e.tensor_scalar(out=ncb_s[i].ap, in0=cb_s[i].ap,
                    scalar1=-1.0, scalar2=None, op0=ALU.mult), reads=[cb_s[i]], writes=[ncb_s[i]])
            kk.emit("vector", lambda e, i=i: e.memset(halo[i].ap, 0.0), writes=[halo[i]])
            kk.emit("vector", lambda e, i=i: e.memset(hstate[i].ap, 0.0),
                    writes=[hstate[i], hst_g[i][0], hst_g[i][1]])
        kk.emit("scalar", lambda e: e.copy(out=wdt_bf.ap, in_=wdt_f32.ap),
                reads=[wdt_f32], writes=[wdt_bf])
        for i in range(DBLK):
            kk.dma("sync", wxp_stg, w_xp_T[i * P:(i + 1) * P, :])
            kk.emit("scalar", lambda e, i=i: e.copy(out=wxp_bf[i].ap, in_=wxp_stg.ap),
                    reads=[wxp_stg], writes=[wxp_bf[i]])

        def init_fn(g):
            g.reg_load(jr, j_s.ap[:1, :1])
            g.reg_mov(r_pc, 0)
            with g.If_eq(jr, 0):
                for f in range(NFLAG):
                    g.dma_start(out=flagsh[0][f:f + 1, :], in_=zero16.ap).then_inc(sem_gq, 16)
                g.dma_start(out=barsh[0], in_=nonce_s.ap).then_inc(sem_gq, 16)
            with g.Else():
                for f in range(NFLAG):
                    g.dma_start(out=flagsh[1][f:f + 1, :], in_=zero16.ap).then_inc(sem_gq, 16)
                g.dma_start(out=barsh[1], in_=nonce_s.ap).then_inc(sem_gq, 16)
            gqn[0] += 16 * (NFLAG + 1)
            g.wait_ge(sem_gq, gqn[0])
            for barx in (barsh[0], barsh[1]):
                g.reg_mov(r_nd, 1)
                with g.While(r_nd):
                    g.reg_add(r_pc, r_pc, 16)
                    g.dma_start(out=bar_s.ap, in_=barx).then_inc(sem_poll, 16)
                    g.wait_ge(sem_poll, r_pc)
                    g.reg_mov(r_nd, 0)
                    for kq in range(4):
                        g.reg_load(r_a, bar_s.ap[:1, kq:kq + 1])
                        g.reg_load(r_b, nonce_s.ap[:1, kq:kq + 1])
                        g.reg_alu(r_a, r_a, r_b, ALU.not_equal)
                        g.reg_alu(r_nd, r_nd, r_a, ALU.bitwise_or)
            return g.engine_nop()

        kk.dma("gpsimd", wout_dr.ap.rearrange("b p c -> (b p) c"), w_out_T,
               wbufs=[wout_dr])

        def emit_dequant(src_packed, src_scales_T, n_rows_blk, n_in, dst_slab, ch_w):
            n_ch = n_in // ch_w
            pkv = Buf(dq_pk.ap[:, :n_in // 8], "pkv")
            scv16 = Buf(dq_sc16.ap[:, :n_in // 128], "scv16")
            scv = Buf(dq_sc.ap[:, :n_in // 128], "scv")
            nibv = Buf(dq_nib.ap[:, :ch_w], "nibv")
            t1 = Buf(dq_t1.ap[:, :ch_w], "t1")
            t2 = Buf(dq_t2.ap[:, :ch_w], "t2")
            t3 = Buf(dq_t3.ap[:, :ch_w], "t3")
            srep = Buf(dq_srep.ap[:, :ch_w], "srep")
            dqo = Buf(dq_out.ap[:, :ch_w], "dqo")
            for r in range(n_rows_blk):
                kk.barrier()
                kk.dma("sync", pkv, src_packed[r * P:(r + 1) * P, :])
                kk.dma("sync", scv16, src_scales_T[r * P:(r + 1) * P, :])
                kk.emit("vector", lambda e, scv=scv, scv16=scv16: e.tensor_copy(
                    out=scv.ap, in_=scv16.ap.bitcast(F16)), reads=[scv16], writes=[scv])
                for chi in range(n_ch):
                    c0 = chi * ch_w
                    pk16 = pkv.ap.bitcast(U16)[:, c0 // 4:(c0 + ch_w) // 4]
                    for kq in range(4):
                        outv = nibv.ap.rearrange("p (j four) -> p j four", four=4)[:, :, kq]
                        kk.emit("vector", lambda e, kq=kq, outv=outv, pk16=pk16: e.tensor_scalar(
                            out=outv, in0=pk16, scalar1=4 * kq, scalar2=15,
                            op0=ALU.logical_shift_right, op1=ALU.bitwise_and),
                            reads=[pkv], writes=[nibv])
                    kk.emit("vector", lambda e, t1=t1, nibv=nibv: e.tensor_scalar(
                        out=t1.ap, in0=nibv.ap, scalar1=7, scalar2=6,
                        op0=ALU.bitwise_and, op1=ALU.logical_shift_left),
                        reads=[nibv], writes=[t1])
                    kk.emit("vector", lambda e, t2=t2, nibv=nibv: e.tensor_scalar(
                        out=t2.ap, in0=nibv.ap, scalar1=8, scalar2=12,
                        op0=ALU.bitwise_and, op1=ALU.logical_shift_left),
                        reads=[nibv], writes=[t2])
                    kk.emit("vector", lambda e, t1=t1, t2=t2: e.scalar_tensor_tensor(
                        out=t1.ap, in0=t1.ap, scalar=16128, in1=t2.ap,
                        op0=ALU.add, op1=ALU.add), reads=[t1, t2], writes=[t1])
                    kk.emit("vector", lambda e, t3=t3, nibv=nibv: e.tensor_scalar(
                        out=t3.ap, in0=nibv.ap, scalar1=1, scalar2=None,
                        op0=ALU.bitwise_and), reads=[nibv], writes=[t3])
                    kk.emit("vector", lambda e, t3=t3: e.tensor_scalar(
                        out=t3.ap, in0=t3.ap, scalar1=16128, scalar2=None,
                        op0=ALU.mult), reads=[t3], writes=[t3])
                    kk.emit("vector", lambda e, t3=t3, t2=t2: e.tensor_tensor(
                        out=t3.ap, in0=t3.ap, in1=t2.ap, op=ALU.add),
                        reads=[t3, t2], writes=[t3])
                    kk.emit("vector", lambda e, nibv=nibv: e.tensor_scalar(
                        out=nibv.ap, in0=nibv.ap, scalar1=6, scalar2=None,
                        op0=ALU.bitwise_and), reads=[nibv], writes=[nibv])
                    kk.emit("vector", lambda e, t2=t2, nibv=nibv, t1=t1, t3=t3: e.select(
                        out=t2.ap, mask=nibv.ap, on_true=t1.ap, on_false=t3.ap),
                        reads=[nibv, t1, t3], writes=[t2])
                    for gi in range(ch_w // 128):
                        gg = c0 // 128 + gi
                        kk.emit("vector", lambda e, gi=gi, gg=gg, srep=srep, scv=scv:
                                e.tensor_scalar(
                            out=srep.ap[:, gi * 128:(gi + 1) * 128], in0=ones_f.ap,
                            scalar1=scv.ap[:, gg:gg + 1], scalar2=None, op0=ALU.mult),
                            reads=[ones_f, scv], writes=[srep])
                    kk.emit("vector", lambda e, dqo=dqo, t2=t2, srep=srep: e.tensor_tensor(
                        out=dqo.ap, in0=t2.ap.bitcast(BF16), in1=srep.ap, op=ALU.mult),
                        reads=[t2, srep], writes=[dqo])
                    if "dqx" in tap_out and dst_slab is wgu_dr and r == 0:
                        kk.dma("gpsimd", tap_out["dqx"][:, c0:c0 + ch_w], dqo)
                        kk.dma("gpsimd", tap_out["dqn"][:, c0:c0 + ch_w], nibv)
                    for ib in range(ch_w // 128):
                        gib = c0 // 128 + ib
                        pt = pstb[ib % 2]
                        po = dq_tpo[ib % 2]
                        kk.emit("tensor", lambda e, ib=ib, pt=pt, dqo=dqo: e.transpose(
                            pt.ap, dqo.ap[:, ib * 128:(ib + 1) * 128], ident_s.ap),
                            reads=[dqo, ident_s], writes=[pt])
                        kk.emit("scalar", lambda e, pt=pt, po=po: e.copy(out=po.ap, in_=pt.ap),
                                reads=[pt], writes=[po])
                        kk.dma("gpsimd", dst_slab.ap[gib, :, r * P:(r + 1) * P], po,
                               rbufs=[po], wbufs=[dst_slab])

        emit_dequant(gu_packed, gu_scales_T, OBLK, DM, wgu_dr, 1024)
        emit_dequant(dn_packed, dn_scales_T, DMH // P, 2 * IH, wd_dr, 1408)
        kk.emit("gpsimd", init_fn, reads=[j_s, nonce_s, zero16, one16], writes=[bar_s])
        if "wgu" in tap_out:
            kk.dma("gpsimd", tap_out["wgu"], wgu_dr)
        if "wd" in tap_out:
            kk.dma("gpsimd", tap_out["wd"], wd_dr)

        kk.barrier()
        for part in range(2):
            w_src = w_in_x_T if part == 0 else w_in_z_T
            for i in range(MBLK):
                kk.dma("sync", wstg, w_src[i * P:(i + 1) * P, :])
                kk.emit("scalar", lambda e, i=i: e.copy(out=win_sb[i].ap, in_=wstg.ap),
                        reads=[wstg], writes=[win_sb[i]])

            for c in range(nch):
                t0 = c * tc
                if part == 0:
                    for tt in range(tc // P):
                        xk = xtok[tt % 2]
                        xcv = xc_t[tt % 2]
                        xnv = xn_tok[tt % 2]
                        kk.dma("sync", xk, x_full[t0 + tt * P: t0 + (tt + 1) * P, :])
                        kk.emit("vector", lambda e, xk=xk: e.tensor_reduce(
                            out=ln_m.ap, in_=xk.ap, axis=AX.X, op=ALU.add),
                            reads=[xk], writes=[ln_m])
                        kk.emit("vector", lambda e: e.tensor_scalar(out=ln_m.ap,
                                in0=ln_m.ap, scalar1=1.0 / DM, scalar2=None, op0=ALU.mult),
                                reads=[ln_m], writes=[ln_m])
                        kk.emit("vector", lambda e, xk=xk, xcv=xcv: e.tensor_scalar(
                            out=xcv.ap, in0=xk.ap, scalar1=ln_m.ap, scalar2=None,
                            op0=ALU.subtract), reads=[xk, ln_m], writes=[xcv])
                        kk.emit("scalar", lambda e, xcv=xcv: e.activation(
                            out=ln_sq.ap, in_=xcv.ap, func=AF.Square, accum_out=ln_ssq.ap),
                            reads=[xcv], writes=[ln_sq, ln_ssq])
                        kk.emit("scalar", lambda e: e.activation(out=ln_rs.ap,
                                in_=ln_ssq.ap, func=AF.Ln, scale=1.0 / DM, bias=LN_EPS),
                                reads=[ln_ssq, constv], writes=[ln_rs])
                        kk.emit("scalar", lambda e: e.activation(out=ln_rs.ap,
                                in_=ln_rs.ap, func=AF.Exp, scale=-0.5),
                                reads=[ln_rs], writes=[ln_rs])
                        kk.emit("vector", lambda e, xcv=xcv, xnv=xnv: e.tensor_scalar(
                            out=xnv.ap, in0=xcv.ap, scalar1=ln_rs.ap, scalar2=None,
                            op0=ALU.mult), reads=[xcv, ln_rs], writes=[xnv])
                        for mb in range(MBLK):
                            pt = pstb[mb % 2]
                            kk.emit("tensor", lambda e, mb=mb, pt=pt, xnv=xnv: e.transpose(
                                pt.ap, xnv.ap[:, mb * 128:(mb + 1) * 128], ident_s.ap),
                                reads=[xnv, ident_s], writes=[pt])
                            kk.emit("scalar", lambda e, mb=mb, tt=tt, pt=pt: e.copy(
                                out=xn_fm[mb].ap[:, tt * P:(tt + 1) * P], in_=pt.ap),
                                reads=[pt], writes=[xn_fm[mb]])
                    for mb in range(MBLK):
                        kk.dma("gpsimd", xn_dr.ap[mb * P:(mb + 1) * P, t0:t0 + tc],
                               xn_fm[mb], wbufs=[xn_dr])
                else:
                    for mb in range(MBLK):
                        kk.dma("sync", xn_fm[mb], xn_dr.ap[mb * P:(mb + 1) * P, t0:t0 + tc],
                               rbufs=[xn_dr])

                for eb in range(DBLK):
                    pm = psb[eb % 2]
                    for k in range(MBLK):
                        kk.emit("tensor", lambda e, k=k, eb=eb, pm=pm: e.matmul(
                            pm.ap, win_sb[k].ap[:, eb * 128:(eb + 1) * 128], xn_fm[k].ap,
                            start=(k == 0), stop=(k == MBLK - 1)),
                            reads=[win_sb[k], xn_fm[k]], writes=[pm])
                    if part == 0:
                        xl = xl_t[eb % 2]
                        cv = cv_t[eb % 2]
                        se_ = sil_e[eb % 2]
                        sr_ = sil_r[eb % 2]
                        xt_ = xt_t[eb % 2]
                        xb_ = xt_bf[eb % 2]
                        kk.emit("vector", lambda e, xl=xl, eb=eb: e.tensor_copy(
                            out=xl.ap[:, 0:3], in_=halo[eb].ap), reads=[halo[eb]], writes=[xl])
                        kk.emit("scalar", lambda e, xl=xl, pm=pm: e.copy(
                            out=xl.ap[:, 3:3 + tc], in_=pm.ap), reads=[pm], writes=[xl])
                        kk.emit("vector", lambda e, xl=xl, cv=cv, eb=eb: e.tensor_scalar(
                            out=cv.ap, in0=xl.ap[:, 0:tc], scalar1=cw_s[eb].ap[:, 0:1],
                            scalar2=None, op0=ALU.mult), reads=[xl, cw_s[eb]], writes=[cv])
                        for kq in range(1, 4):
                            kk.emit("vector", lambda e, xl=xl, cv=cv, eb=eb, kq=kq:
                                    e.scalar_tensor_tensor(out=cv.ap, in0=xl.ap[:, kq:kq + tc],
                                    scalar=cw_s[eb].ap[:, kq:kq + 1], in1=cv.ap,
                                    op0=ALU.mult, op1=ALU.add),
                                    reads=[xl, cw_s[eb], cv], writes=[cv])
                        kk.emit("vector", lambda e, xl=xl, eb=eb: e.tensor_copy(
                            out=halo[eb].ap, in_=xl.ap[:, tc:tc + 3]),
                            reads=[xl], writes=[halo[eb]])
                        kk.emit("scalar", lambda e, cv=cv, se_=se_, eb=eb: e.activation(
                            out=se_.ap, in_=cv.ap, func=AF.Exp, scale=-1.0, bias=ncb_s[eb].ap),
                            reads=[cv, ncb_s[eb]], writes=[se_])
                        kk.emit("vector", lambda e, se_=se_, sr_=sr_: e.tensor_scalar(
                            out=sr_.ap, in0=se_.ap, scalar1=1.0, scalar2=None, op0=ALU.add),
                            reads=[se_], writes=[sr_])
                        kk.emit("vector", lambda e, sr_=sr_: e.reciprocal(out=sr_.ap, in_=sr_.ap),
                                reads=[sr_], writes=[sr_])
                        kk.emit("vector", lambda e, cv=cv, sr_=sr_, xt_=xt_, eb=eb:
                                e.scalar_tensor_tensor(out=xt_.ap, in0=cv.ap,
                                scalar=cb_s[eb].ap, in1=sr_.ap, op0=ALU.add, op1=ALU.mult),
                                reads=[cv, cb_s[eb], sr_], writes=[xt_])
                        kk.emit("vector", lambda e, xt_=xt_, xb_=xb_: e.tensor_copy(
                            out=xb_.ap, in_=xt_.ap), reads=[xt_], writes=[xb_])
                        kk.dma("gpsimd", xt_dr.ap[eb * P:(eb + 1) * P, t0:t0 + tc], xt_,
                               wbufs=[xt_dr])
                        kk.emit("tensor", lambda e, eb=eb, xb_=xb_: e.matmul(
                            psb[2].ap, wxp_bf[eb].ap[:, 0:128], xb_.ap,
                            start=(eb == 0), stop=(eb == DBLK - 1)),
                            reads=[wxp_bf[eb], xb_], writes=[psb[2]])
                        kk.emit("tensor", lambda e, eb=eb, xb_=xb_: e.matmul(
                            psb[3].ap[:32, :], wxp_bf[eb].ap[:, 128:160], xb_.ap,
                            start=(eb == 0), stop=(eb == DBLK - 1)),
                            reads=[wxp_bf[eb], xb_], writes=[psb[3]])
                    else:
                        zv = z_t[eb % 2]
                        zs_ = zs_t[eb % 2]
                        se_ = sil_e[eb % 2]
                        sr_ = sil_r[eb % 2]
                        kk.emit("scalar", lambda e, se_=se_, pm=pm: e.activation(
                            out=se_.ap, in_=pm.ap, func=AF.Exp, scale=-1.0),
                            reads=[pm], writes=[se_])
                        kk.emit("scalar", lambda e, zv=zv, pm=pm: e.copy(out=zv.ap, in_=pm.ap),
                                reads=[pm], writes=[zv])
                        kk.emit("vector", lambda e, se_=se_, sr_=sr_: e.tensor_scalar(
                            out=sr_.ap, in0=se_.ap, scalar1=1.0, scalar2=None, op0=ALU.add),
                            reads=[se_], writes=[sr_])
                        kk.emit("vector", lambda e, sr_=sr_: e.reciprocal(out=sr_.ap, in_=sr_.ap),
                                reads=[sr_], writes=[sr_])
                        kk.emit("vector", lambda e, zv=zv, sr_=sr_, zs_=zs_: e.tensor_tensor(
                            out=zs_.ap, in0=zv.ap, in1=sr_.ap, op=ALU.mult),
                            reads=[zv, sr_], writes=[zs_])
                        kk.dma("gpsimd", zs_dr.ap[eb * P:(eb + 1) * P, t0:t0 + tc], zs_,
                               wbufs=[zs_dr])

                if part == 0:
                    kk.emit("scalar", lambda e: e.copy(out=pj1_s.ap, in_=psb[2].ap),
                            reads=[psb[2]], writes=[pj1_s])
                    kk.emit("scalar", lambda e: e.copy(out=pj2_s.ap, in_=psb[3].ap[:32, :]),
                            reads=[psb[3]], writes=[pj2_s])
                    kk.emit("gpsimd", g_branch_write(projsh[0][0:128, t0:t0 + tc],
                                                     projsh[1][0:128, t0:t0 + tc], pj1_s.ap),
                            reads=[pj1_s, j_s, bar_s], writes=[projsh_b])
                    kk.emit("gpsimd", g_branch_write(projsh[0][128:160, t0:t0 + tc],
                                                     projsh[1][128:160, t0:t0 + tc], pj2_s.ap),
                            reads=[pj2_s, j_s], writes=[projsh_b])
                    kk.emit("gpsimd", g_flag(c), reads=[one16, j_s, projsh_b])

        if "xn" in tap_out:
            kk.dma("gpsimd", tap_out["xn"], xn_dr)
        if "xt" in tap_out:
            kk.dma("gpsimd", tap_out["xt"], xt_dr)
        if "zs" in tap_out:
            kk.dma("gpsimd", tap_out["zs"], zs_dr)

        kk.barrier()
        poll_tok = Buf(None, "poll_tok")
        for c in range(nch):
            t0 = c * tc
            kk.emit("gpsimd", g_poll(c), reads=[j_s], writes=[poll_tok, flag_s])
            kk.dma("sync", pj1a, projsh[0][0:128, t0:t0 + tc], rbufs=[poll_tok])
            kk.dma("sync", pj1b, projsh[1][0:128, t0:t0 + tc])
            kk.dma("sync", pj2a, projsh[0][128:160, t0:t0 + tc])
            kk.dma("sync", pj2b, projsh[1][128:160, t0:t0 + tc])
            kk.emit("vector", lambda e: e.tensor_tensor(out=pj1a.ap, in0=pj1a.ap,
                    in1=pj1b.ap, op=ALU.add), reads=[pj1a, pj1b], writes=[pj1a])
            kk.emit("vector", lambda e: e.tensor_tensor(out=pj2a.ap, in0=pj2a.ap,
                    in1=pj2b.ap, op=ALU.add), reads=[pj2a, pj2b], writes=[pj2a])
            kk.emit("vector", lambda e: e.tensor_copy(out=pj1bf.ap, in_=pj1a.ap),
                    reads=[pj1a], writes=[pj1bf])
            kk.emit("vector", lambda e: e.tensor_copy(out=pjbc_bf.ap, in_=pj2a.ap),
                    reads=[pj2a], writes=[pjbc_bf])
            if "proj" in tap_out:
                kk.dma("gpsimd", tap_out["proj"][0:128, t0:t0 + tc], pj1a)
                kk.dma("gpsimd", tap_out["proj"][128:160, t0:t0 + tc], pj2a)

            for n in range(2 * NST):
                dstf = bb_t[n] if n < NST else cb_t[n - NST]
                pbc = psb[4 + (n % 2)]
                rsg = rowstg[n % 2]
                kk.dma("sync", rsg, pjbc_bf.ap[n:n + 1, :], rbufs=[pjbc_bf])
                kk.emit("tensor", lambda e, pbc=pbc, rsg=rsg: e.matmul(
                    pbc.ap, ones_bf.ap, rsg.ap, start=True, stop=True),
                    reads=[rsg, ones_bf], writes=[pbc])
                kk.emit("scalar", lambda e, pbc=pbc, dstf=dstf: e.copy(
                    out=dstf.ap, in_=pbc.ap), reads=[pbc], writes=[dstf])

            def flush_pend(pend):
                dbp, grpp, rp = pend
                sl = hc_sl[rp]
                for ni in range(4):
                    kk.emit("vector", lambda e, sl=sl, ni=ni: e.tensor_tensor(
                        out=sl[ni].ap, in0=sl[ni].ap, in1=sl[ni + 4].ap,
                        op=ALU.add), reads=[sl[ni], sl[ni + 4]], writes=[sl[ni]])
                for ni in range(2):
                    kk.emit("vector", lambda e, sl=sl, ni=ni: e.tensor_tensor(
                        out=sl[ni].ap, in0=sl[ni].ap, in1=sl[ni + 2].ap,
                        op=ALU.add), reads=[sl[ni], sl[ni + 2]], writes=[sl[ni]])
                kk.emit("vector", lambda e, sl=sl: e.tensor_tensor(
                    out=sl[0].ap, in0=sl[0].ap, in1=sl[1].ap, op=ALU.add),
                    reads=[sl[0], sl[1]], writes=[sl[0]])
                if grpp == 1:
                    curp = dbp % 2
                    ys = y_s[curp]
                    kk.emit("vector", lambda e, ys=ys: e.tensor_tensor(out=ys.ap,
                            in0=hc_sl[0][0].ap, in1=hc_sl[1][0].ap, op=ALU.add),
                            reads=[hc_sl[0][0], hc_sl[1][0]], writes=[ys])
                    if "y" in tap_out:
                        kk.dma("gpsimd",
                               tap_out["y"][dbp * P:(dbp + 1) * P, t0:t0 + tc], ys)
                    kk.emit("vector", lambda e, ys=ys, dbp=dbp, curp=curp:
                            e.scalar_tensor_tensor(
                                out=ys.ap, in0=xt_rd[curp].ap, scalar=dvec_s[dbp].ap,
                                in1=ys.ap, op0=ALU.mult, op1=ALU.add),
                            reads=[xt_rd[curp], dvec_s[dbp], ys], writes=[ys])
                    kk.emit("vector", lambda e, ys=ys, curp=curp: e.tensor_tensor(
                        out=ys.ap, in0=ys.ap, in1=zs_rd[curp].ap, op=ALU.mult),
                        reads=[ys, zs_rd[curp]], writes=[ys])
                    kk.emit("scalar", lambda e, ys=ys, dbp=dbp: e.copy(
                        out=yg_slab.ap[:, dbp * tc:(dbp + 1) * tc], in_=ys.ap),
                        reads=[ys], writes=[yg_slab])

            def emit_front(db):
                cur = db % 2
                pm = psb[db % 2]
                dts, ws = dt_s[cur], w_s[cur]
                xtr, zsr = xt_rd[cur], zs_rd[cur]
                kk.emit("tensor", lambda e, db=db, pm=pm: e.matmul(
                    pm.ap, wdt_bf.ap[:, db * 128:(db + 1) * 128], pj1bf.ap,
                    start=True, stop=True), reads=[wdt_bf, pj1bf], writes=[pm])
                kk.emit("scalar", lambda e, pm=pm, db=db: e.activation(
                    out=sp_r.ap, in_=pm.ap, func=AF.Relu, bias=dtb_s[db].ap),
                    reads=[pm, dtb_s[db]], writes=[sp_r])
                kk.emit("scalar", lambda e, pm=pm, db=db: e.activation(
                    out=sp_a.ap, in_=pm.ap, func=AF.Abs, bias=dtb_s[db].ap),
                    reads=[pm, dtb_s[db]], writes=[sp_a])
                kk.emit("scalar", lambda e: e.activation(out=sp_a.ap, in_=sp_a.ap,
                        func=AF.Exp, scale=-1.0), reads=[sp_a], writes=[sp_a])
                kk.emit("scalar", lambda e: e.activation(out=sp_a.ap, in_=sp_a.ap,
                        func=AF.Ln, bias=1.0), reads=[sp_a, constv], writes=[sp_a])
                kk.dma("sync", xtr, xt_dr.ap[db * P:(db + 1) * P, t0:t0 + tc],
                       rbufs=[xt_dr])
                kk.dma("sync", zsr, zs_dr.ap[db * P:(db + 1) * P, t0:t0 + tc],
                       rbufs=[zs_dr])
                kk.emit("gpsimd", lambda e, dts=dts: e.tensor_tensor(
                    out=dts.ap, in0=sp_r.ap, in1=sp_a.ap, op=ALU.add),
                    reads=[sp_r, sp_a], writes=[dts])
                kk.emit("gpsimd", lambda e, dts=dts, ws=ws, xtr=xtr: e.tensor_tensor(
                    out=ws.ap, in0=dts.ap, in1=xtr.ap, op=ALU.mult),
                    reads=[dts, xtr], writes=[ws])
                if "dt" in tap_out:
                    kk.dma("gpsimd", tap_out["dt"][db * P:(db + 1) * P, t0:t0 + tc],
                           dts)

            def emit_wb(db, grp):
                ws = w_s[db % 2]
                for ni in range(8):
                    n = grp * 8 + ni
                    weng = "gpsimd" if ni < 5 else "vector"
                    kk.emit(weng, lambda e, n=n, ni=ni, ws=ws: e.tensor_tensor(
                        out=wb_sl[ni].ap, in0=ws.ap, in1=bb_t[n].ap,
                        op=ALU.mult), reads=[ws, bb_t[n]], writes=[wb_sl[ni]])

            pend = None
            emit_front(0)
            emit_wb(0, 0)
            for db in range(DBLK):
                dts = dt_s[db % 2]
                for grp in range(2):
                    for ni in range(8):
                        n = grp * 8 + ni
                        kk.emit("scalar", lambda e, db=db, n=n, ni=ni, dts=dts:
                                e.activation(
                                    out=da_sl[ni].ap, in_=dts.ap, func=AF.Exp,
                                    scale=a_sc[db].ap[:, n:n + 1]),
                                reads=[dts, a_sc[db]], writes=[da_sl[ni]])
                    for ni in range(8):
                        n = grp * 8 + ni
                        kk.emit("vector", lambda e, db=db, n=n, ni=ni, grp=grp:
                                e.tensor_tensor_scan(
                                    out=h_sl[ni].ap,
                                    data0=da_sl[ni].ap, data1=wb_sl[ni].ap,
                                    initial=hstate[db].ap[:, n:n + 1],
                                    op0=ALU.mult, op1=ALU.add),
                                reads=[da_sl[ni], wb_sl[ni], hst_g[db][grp]],
                                writes=[h_sl[ni]])
                    if pend is not None:
                        flush_pend(pend)
                        pend = None
                    kk.emit("gpsimd", lambda e, db=db, grp=grp: e.tensor_copy(
                        out=hstate[db].ap[:, grp * 8:(grp + 1) * 8],
                        in_=h_slab.ap.rearrange(
                            "p (n t) -> p n t", n=8)[:, :, tc - 1:tc]),
                        reads=list(h_sl), writes=[hst_g[db][grp]])
                    if grp == 0:
                        emit_wb(db, 1)
                    elif db + 1 < DBLK:
                        emit_front(db + 1)
                        emit_wb(db + 1, 0)
                    for ni in range(8):
                        n = grp * 8 + ni
                        kk.emit("gpsimd", lambda e, n=n, ni=ni, grp=grp:
                                e.tensor_tensor(
                                    out=hc_sl[grp][ni].ap, in0=h_sl[ni].ap,
                                    in1=cb_t[n].ap, op=ALU.mult),
                                reads=[h_sl[ni], cb_t[n]],
                                writes=[hc_sl[grp][ni]])
                    pend = (db, grp, grp)
            flush_pend(pend)
            pend = None

            src = yg_slab.ap.rearrange("p (b t) -> p b t", b=DBLK)
            d0 = ysh[0][:, t0:t0 + tc].rearrange("(b p) t -> p b t", p=P)
            d1 = ysh[1][:, t0:t0 + tc].rearrange("(b p) t -> p b t", p=P)
            kk.emit("gpsimd", g_branch_write(d0, d1, src),
                    reads=[yg_slab, j_s], writes=[ysh_b])
            kk.emit("gpsimd", g_flag(nch + c), reads=[one16, j_s, ysh_b])

        kk.barrier()

        for c in range(nch):
            kk.emit("gpsimd", g_poll(nch + c), reads=[j_s], writes=[poll_tok, flag_s])
        kk.barrier()

        for C in range(nbig):
            tb0 = C * tb
            for mb in range(MBLK):
                kk.dma("sync", xn3[mb], xn_dr.ap[mb * P:(mb + 1) * P, tb0:tb0 + tb],
                       rbufs=[xn_dr])
            for ip in range(IBLK2 // 2):
                for half in range(2):
                    ob = ip + half * (OBLK // 2)
                    for k in range(MBLK):
                        kk.dma("sync",
                               Buf(wgu_slab[half].ap[:, k * 128:(k + 1) * 128], "wgu_t"),
                               wgu_dr.ap[k, :, ob * P:(ob + 1) * P],
                               rbufs=[wgu_dr], wbufs=[wgu_slab[half]])
                for sl in range(tb // 512):
                    for half in range(2):
                        pm = psb[half]
                        for k in range(MBLK):
                            kk.emit("tensor", lambda e, k=k, half=half, sl=sl, pm=pm: e.matmul(
                                pm.ap, wgu_slab[half].ap[:, k * 128:(k + 1) * 128],
                                xn3[k].ap[:, sl * 512:(sl + 1) * 512],
                                start=(k == 0), stop=(k == MBLK - 1)),
                                reads=[wgu_slab[half], xn3[k]], writes=[pm])
                    kk.emit("scalar", lambda e: e.activation(
                        out=g_e.ap, in_=psb[0].ap, func=AF.Exp, scale=-1.0),
                        reads=[psb[0]], writes=[g_e])
                    kk.emit("scalar", lambda e: e.copy(out=g_cp.ap, in_=psb[0].ap),
                            reads=[psb[0]], writes=[g_cp])
                    kk.emit("scalar", lambda e: e.copy(out=u_cp.ap, in_=psb[1].ap),
                            reads=[psb[1]], writes=[u_cp])
                    kk.emit("vector", lambda e: e.tensor_scalar(out=g_r.ap,
                            in0=g_e.ap, scalar1=1.0, scalar2=None, op0=ALU.add),
                            reads=[g_e], writes=[g_r])
                    kk.emit("vector", lambda e: e.reciprocal(out=g_r.ap, in_=g_r.ap),
                            reads=[g_r], writes=[g_r])
                    kk.emit("vector", lambda e: e.tensor_tensor(out=g_sg.ap,
                            in0=g_cp.ap, in1=g_r.ap, op=ALU.mult),
                            reads=[g_cp, g_r], writes=[g_sg])
                    kk.emit("vector", lambda e: e.tensor_tensor(out=g_sg.ap,
                            in0=g_sg.ap, in1=u_cp.ap, op=ALU.mult),
                            reads=[g_sg, u_cp], writes=[g_sg])
                    hb = h_bf[sl % 2]
                    kk.emit("vector", lambda e, hb=hb: e.tensor_copy(
                        out=hb.ap, in_=g_sg.ap), reads=[g_sg], writes=[hb])
                    dst0 = hsh[0][ip * P:(ip + 1) * P, tb0 + sl * 512: tb0 + (sl + 1) * 512]
                    dst1 = hsh[1][ip * P:(ip + 1) * P, tb0 + sl * 512: tb0 + (sl + 1) * 512]
                    kk.emit("gpsimd", g_branch_write(dst0, dst1, hb.ap),
                            reads=[hb, j_s], writes=[hsh_b])
            kk.emit("gpsimd", g_flag(2 * nch + C), reads=[one16, j_s, hsh_b])
            kk.emit("gpsimd", g_poll(2 * nch + C), reads=[j_s], writes=[poll_tok, flag_s])
            kk.barrier()

            for ib in range(IBLK2):
                kk.dma("sync", wd_sb[ib], wd_dr.ap[ib], rbufs=[wd_dr])
            for t128 in range(tb // P):
                trow = tb0 + t128 * P
                pm0 = psb[2 * (t128 % 2)]
                pm1 = psb[2 * (t128 % 2) + 1]
                hsl = h_slab3[0]
                for k2 in range(2):
                    kk.dma("sync",
                           hsl.ap.rearrange("p (b t) -> p b t", b=IBLK2)[:, k2 * 22:(k2 + 1) * 22, :],
                           hsh[k2][:, trow:trow + P].rearrange("(b p) t -> p b t", p=P),
                           rbufs=[poll_tok], wbufs=[hsl])
                for ib in range(IBLK2):
                    hl_ap = hsl.ap[:, ib * 128:(ib + 1) * 128]
                    kk.emit("tensor", lambda e, hl_ap=hl_ap, ib=ib, pm0=pm0: e.matmul(
                        pm0.ap, hl_ap, wd_sb[ib].ap[:, 0:512],
                        start=(ib == 0), stop=(ib == IBLK2 - 1)),
                        reads=[hsl, wd_sb[ib]], writes=[pm0])
                    kk.emit("tensor", lambda e, hl_ap=hl_ap, ib=ib, pm1=pm1: e.matmul(
                        pm1.ap, hl_ap, wd_sb[ib].ap[:, 512:1024],
                        start=(ib == 0), stop=(ib == IBLK2 - 1)),
                        reads=[hsl, wd_sb[ib]], writes=[pm1])
                yf = yf_t[t128 % 2]
                kk.emit("scalar", lambda e, pm0=pm0, yf=yf: e.copy(
                    out=yf.ap[:, 0:512], in_=pm0.ap), reads=[pm0], writes=[yf])
                kk.emit("scalar", lambda e, pm1=pm1, yf=yf: e.copy(
                    out=yf.ap[:, 512:1024], in_=pm1.ap), reads=[pm1], writes=[yf])
                kk.dma("gpsimd", yffn_dr.ap[trow:trow + P, :], yf, wbufs=[yffn_dr])
                kk.emit("scalar", lambda e, yf=yf: e.activation(
                    out=sqsc.ap, in_=yf.ap, func=AF.Square, accum_out=sq_p.ap),
                    reads=[yf], writes=[sqsc, sq_p])
                rowi = trow // P
                dst0 = sqsh[0][rowi:rowi + 1, :].rearrange("r p -> p r")
                dst1 = sqsh[1][rowi:rowi + 1, :].rearrange("r p -> p r")
                kk.emit("gpsimd", g_branch_write(dst0, dst1, sq_p.ap),
                        reads=[sq_p, j_s], writes=[sqsh_b])
            kk.emit("gpsimd", g_flag(2 * nch + nbig + C), reads=[one16, j_s, sqsh_b])
            kk.emit("gpsimd", g_poll(2 * nch + nbig + C), reads=[j_s], writes=[poll_tok, flag_s])
            kk.barrier()

            for ib in range(WOBLK):
                kk.dma("sync", wout_sb[ib], wout_dr.ap[ib], rbufs=[wout_dr])
            for t128 in range(tb // P):
                trow = tb0 + t128 * P
                rowi = trow // P
                pm0 = psb[2 + (t128 % 2)]
                pm1 = psb[4 + (t128 % 2)]
                ysl = y_slab[t128 % 2]
                for k2 in range(2):
                    kk.dma("sync",
                           ysl.ap.rearrange("p (b t) -> p b t", b=32)[:, k2 * 16:(k2 + 1) * 16, :],
                           ysh[k2][:, trow:trow + P].rearrange("(b p) t -> p b t", p=P),
                           rbufs=[poll_tok], wbufs=[ysl])
                for ib in range(WOBLK):
                    yl_ap = ysl.ap[:, ib * 128:(ib + 1) * 128]
                    kk.emit("tensor", lambda e, yl_ap=yl_ap, pm0=pm0, ib=ib: e.matmul(
                        pm0.ap, yl_ap, wout_sb[ib].ap[:, 0:512],
                        start=(ib == 0), stop=(ib == WOBLK - 1)),
                        reads=[ysl, wout_sb[ib]], writes=[pm0])
                    kk.emit("tensor", lambda e, yl_ap=yl_ap, pm1=pm1, ib=ib: e.matmul(
                        pm1.ap, yl_ap, wout_sb[ib].ap[:, 512:1024],
                        start=(ib == 0), stop=(ib == WOBLK - 1)),
                        reads=[ysl, wout_sb[ib]], writes=[pm1])
                sv = s_ev[t128 % 2]
                kk.emit("scalar", lambda e, pm0=pm0, sv=sv: e.copy(
                    out=sv.ap[:, 0:512], in_=pm0.ap), reads=[pm0], writes=[sv])
                kk.emit("scalar", lambda e, pm1=pm1, sv=sv: e.copy(
                    out=sv.ap[:, 512:1024], in_=pm1.ap), reads=[pm1], writes=[sv])
                if "s" in tap_out:
                    kk.dma("gpsimd", tap_out["s"][trow:trow + P, :], sv)
                kk.dma("sync", sq0, sqsh[0][rowi:rowi + 1, :].rearrange("r p -> p r"),
                       rbufs=[poll_tok])
                kk.dma("sync", sq1, sqsh[1][rowi:rowi + 1, :].rearrange("r p -> p r"),
                       rbufs=[poll_tok])
                kk.emit("vector", lambda e: e.tensor_tensor(out=sq0.ap, in0=sq0.ap,
                        in1=sq1.ap, op=ALU.add), reads=[sq0, sq1], writes=[sq0])
                kk.emit("scalar", lambda e: e.activation(out=rstd3.ap, in_=sq0.ap,
                        func=AF.Ln, scale=1.0 / DM, bias=RMS_EPS), reads=[sq0, constv], writes=[rstd3])
                kk.emit("scalar", lambda e: e.activation(out=rstd3.ap, in_=rstd3.ap,
                        func=AF.Exp, scale=-0.5), reads=[rstd3], writes=[rstd3])
                kk.dma("sync", yfr, yffn_dr.ap[trow:trow + P, :], rbufs=[yffn_dr])
                kk.emit("vector", lambda e: e.tensor_scalar(out=yfr.ap, in0=yfr.ap,
                        scalar1=rstd3.ap, scalar2=None, op0=ALU.mult),
                        reads=[yfr, rstd3], writes=[yfr])
                kk.emit("scalar", lambda e: e.activation(out=g3e.ap, in_=yfr.ap,
                        func=AF.Exp, scale=-1.0), reads=[yfr], writes=[g3e])
                kk.emit("vector", lambda e: e.tensor_scalar(out=g3r.ap, in0=g3e.ap,
                        scalar1=1.0, scalar2=None, op0=ALU.add), reads=[g3e], writes=[g3r])
                kk.emit("vector", lambda e: e.reciprocal(out=g3r.ap, in_=g3r.ap),
                        reads=[g3r], writes=[g3r])
                kk.emit("vector", lambda e: e.tensor_tensor(out=yfr.ap, in0=yfr.ap,
                        in1=g3r.ap, op=ALU.mult), reads=[yfr, g3r], writes=[yfr])
                ov = o_t[t128 % 2]
                kk.dma("sync", xh_t, x_halfi[trow:trow + P, :])
                kk.emit("vector", lambda e, sv=sv, ov=ov: e.tensor_tensor(
                    out=ov.ap, in0=sv.ap, in1=yfr.ap, op=ALU.mult),
                    reads=[sv, yfr], writes=[ov])
                kk.emit("vector", lambda e, ov=ov: e.tensor_tensor(
                    out=ov.ap, in0=ov.ap, in1=xh_t.ap, op=ALU.add),
                    reads=[ov, xh_t], writes=[ov])
                kk.dma("sync", out_half[trow:trow + P, :], ov)
            kk.barrier()

        if "yffn" in tap_out:
            kk.dma("gpsimd", tap_out["yffn"], yffn_dr)
        if "hh" in tap_out:
            kk.dma("gpsimd", tap_out["hh"][0], hsh[0], rbufs=[hsh_b])
            kk.dma("gpsimd", tap_out["hh"][1], hsh[1], rbufs=[hsh_b])
        if "wo" in tap_out:
            kk.dma("gpsimd", tap_out["wo"], wout_dr)
        if "yy" in tap_out:
            kk.dma("gpsimd", tap_out["yy"][0], ysh[0], rbufs=[ysh_b])
            kk.dma("gpsimd", tap_out["yy"][1], ysh[1], rbufs=[ysh_b])

        kk.barrier()
        for en in ("sync", "gpsimd"):
            for s in range(K.NSLOT):
                if kk.slot_cnt[en][s]:
                    kk._wait(en, (en, "d", s), kk.slot_cnt[en][s])

        kk.flush(block)

    nc._kk = kk
    return nc



_IDENT = np.eye(128, dtype=ml_dtypes.bfloat16)


def _core_inputs(core, inp):
    b, j = core // 2, core & 1
    x = np.ascontiguousarray(inp["L0_attn_hidden_states"][b])
    wi = inp["L0_attn_in_proj_w"]
    di0, di1 = j * DIH, (j + 1) * DIH
    return {
        "x_full": x,
        "x_halfi": np.ascontiguousarray(x[:, j * DMH:(j + 1) * DMH]),
        "w_in_x_T": np.ascontiguousarray(wi[di0:di1, :].T),
        "w_in_z_T": np.ascontiguousarray(wi[4096 + di0:4096 + di1, :].T),
        "conv_w": np.ascontiguousarray(inp["L0_attn_conv_w"][di0:di1]),
        "conv_b": np.ascontiguousarray(inp["L0_attn_conv_b"][di0:di1, None]),
        "w_xp_T": np.ascontiguousarray(inp["L0_attn_x_proj_w"][:, di0:di1].T),
        "w_dt_T": np.ascontiguousarray(inp["L0_attn_dt_proj_w"][di0:di1, :].T),
        "dt_b": np.ascontiguousarray(inp["L0_attn_dt_proj_b"][di0:di1, None]),
        "a_log": np.ascontiguousarray(inp["L0_attn_A_log"][di0:di1]),
        "d_vec": np.ascontiguousarray(inp["L0_attn_D"][di0:di1, None]),
        "w_out_T": np.ascontiguousarray(
            inp["L0_attn_out_proj_w"].T[:, j * DMH:(j + 1) * DMH]),
        "gu_packed": np.ascontiguousarray(np.concatenate([
            inp["L0_ffn_gate_up_packed"][j * IH:(j + 1) * IH],
            inp["L0_ffn_gate_up_packed"][5632 + j * IH:5632 + (j + 1) * IH]], axis=0)),
        "gu_scales_T": np.ascontiguousarray(np.concatenate([
            inp["L0_ffn_gate_up_scales"][:, j * IH:(j + 1) * IH],
            inp["L0_ffn_gate_up_scales"][:, 5632 + j * IH:5632 + (j + 1) * IH]],
            axis=1).T.copy()).view(np.uint16),
        "dn_packed": np.ascontiguousarray(
            inp["L0_ffn_down_packed"][j * DMH:(j + 1) * DMH]),
        "dn_scales_T": np.ascontiguousarray(
            inp["L0_ffn_down_scales"][:, j * DMH:(j + 1) * DMH].T).view(np.uint16),
        "j_in": np.array([[j]], dtype=np.uint32),
        "ident": _IDENT,
        "constv": np.broadcast_to(
            np.array([LN_EPS, RMS_EPS, 1.0], np.float32), (128, 3)).copy(),
    }


_prog_cache = {}


def kernel(**inputs):
    t_tokens = inputs["L0_attn_hidden_states"].shape[1]
    if t_tokens not in _prog_cache:
        _prog_cache[t_tokens] = build_program(t_tokens=t_tokens)
    nc = _prog_cache[t_tokens]
    nonce = np.random.randint(1, 2**31 - 1, size=(1, 4)).astype(np.uint32)
    in_maps = []
    for core in range(8):
        m = _core_inputs(core, inputs)
        m["nonce_in"] = nonce
        in_maps.append(m)
    res = run_bass_kernel_spmd(nc, in_maps, core_ids=list(range(8)))
    out = np.empty((4, t_tokens, DM), dtype=np.float32)
    for core in range(8):
        b, j = core // 2, core & 1
        out[b][:, j * DMH:(j + 1) * DMH] = res.results[core]["out_half"]
    return out


# version 06
# speedup vs baseline: 1.5075x; 1.2241x over previous
import bisect
import numpy as np
import ml_dtypes
import concourse.bass as bass
import concourse.mybir as mybir
from concourse.bass_utils import run_bass_kernel_spmd

F32 = mybir.dt.float32
BF16 = mybir.dt.bfloat16
F16 = mybir.dt.float16
I32 = mybir.dt.int32
U32 = mybir.dt.uint32
U16 = mybir.dt.uint16
AF = mybir.ActivationFunctionType
ALU = mybir.AluOpType
AX = mybir.AxisListType

P = 128
DM = 2048
DIH = 2048
NST = 16
RK = 128
IH = 2816
GU = 2 * IH
DMH = 1024
LN_EPS = 1e-5
RMS_EPS = 1e-6
DBLK = DIH // P
MBLK = DM // P
OBLK = GU // P
IBLK2 = 2 * IH // P
WOBLK = 4096 // P

ENGS = ("sync", "vector", "scalar", "tensor", "gpsimd")


class Op:
    __slots__ = ("eng", "idx", "sem_val", "dma", "pend_inc", "label", "slot", "slot_val")

    def __init__(self, eng, idx, dma):
        self.eng = eng
        self.idx = idx
        self.dma = dma
        self.sem_val = None
        self.pend_inc = 0


class Buf:
    __slots__ = ("ap", "wop", "rops", "name")

    def __init__(self, ap, name=""):
        self.ap = ap
        self.wop = None
        self.rops = []
        self.name = name


class K:

    NSLOT = 16

    def __init__(self, nc):
        self.nc = nc
        self.ops = {e: [] for e in ENGS}
        self.clos = {e: [] for e in ENGS}
        self.milestones = {e: 0 for e in ENGS}
        self.dma_emitted = {e: 0 for e in ENGS}
        self.slot_cnt = {e: [0] * self.NSLOT for e in ENGS}
        self.waited = {e: {} for e in ENGS}
        self.sems = {}
        self.semd = {e: [] for e in ENGS}

    def _milestone(self, op):
        if op.dma:
            return (op.eng, "d", op.slot), op.slot_val
        if op.sem_val is not None:
            return (op.eng, "c"), op.sem_val
        lst = self.semd[op.eng]
        j = bisect.bisect_left(lst, op.idx)
        if j < len(lst):
            o = self.ops[op.eng][lst[j]]
            return (op.eng, "c"), o.sem_val
        self.milestones[op.eng] += 1
        op.sem_val = self.milestones[op.eng]
        op.pend_inc += 1
        lst.append(op.idx)
        return (op.eng, "c"), op.sem_val

    def _wait(self, eng, key, val):
        if self.waited[eng].get(key, 0) >= val:
            return
        self.waited[eng][key] = val
        self.clos[eng].append(("wait", key, val))

    def _dep(self, eng, op):
        key, val = self._milestone(op)
        self._wait(eng, key, val)

    def emit(self, eng, fn, reads=(), writes=(), dma=False, label=""):
        for b in reads:
            if b.wop is not None:
                self._dep(eng, b.wop)
        for b in writes:
            if b.wop is not None and b.wop.eng != eng:
                self._dep(eng, b.wop)
            for r in b.rops:
                if r.eng != eng:
                    self._dep(eng, r)
        op = Op(eng, len(self.ops[eng]), dma)
        op.label = label
        if dma:
            i = self.dma_emitted[eng]
            slot = i % self.NSLOT
            if self.slot_cnt[eng][slot] > 0:
                self._wait(eng, (eng, "d", slot), self.slot_cnt[eng][slot])
            self.slot_cnt[eng][slot] += 16
            op.slot = slot
            op.slot_val = self.slot_cnt[eng][slot]
            self.dma_emitted[eng] += 1
        self.ops[eng].append(op)
        self.clos[eng].append(("op", op, fn))
        for b in reads:
            b.rops.append(op)
        for b in writes:
            b.wop = op
            b.rops = []
        return op

    def dma(self, eng, out, in_, rbufs=(), wbufs=(), **kw):
        reads = list(rbufs)
        writes = list(wbufs)
        if isinstance(in_, Buf):
            reads.append(in_)
            in_ap = in_.ap
        else:
            in_ap = in_
        if isinstance(out, Buf):
            writes.append(out)
            out_ap = out.ap
        else:
            out_ap = out
        return self.emit(eng, lambda e: e.dma_start(out=out_ap, in_=in_ap, **kw),
                         reads=reads, writes=writes, dma=True)

    def barrier(self):
        marks = []
        for en in ENGS:
            for s in range(self.NSLOT):
                if self.slot_cnt[en][s]:
                    marks.append(((en, "d", s), self.slot_cnt[en][s]))
            for o in reversed(self.ops[en]):
                if not o.dma:
                    key, val = self._milestone(o)
                    marks.append((key, val))
                    break
        for en in ENGS:
            for key, val in marks:
                if key[0] != en:
                    self._wait(en, key, val)
                elif len(key) == 3:
                    self._wait(en, key, val)

    def flush(self, block):
        makers = {"sync": block.sync, "vector": block.vector,
                  "scalar": block.scalar, "tensor": block.tensor,
                  "gpsimd": block.gpsimd}
        for en in ENGS:
            items = self.clos[en]
            if not items:
                continue

            def body(e, items=items, en=en):
                for it in items:
                    if it[0] == "wait":
                        _, key, val = it
                        e.wait_ge(self.sems[key], val)
                    else:
                        _, op, fn = it
                        inst = fn(e)
                        if op.dma:
                            inst.then_inc(self.sems[(en, "d", op.slot)], 16)
                        if op.pend_inc:
                            inst.then_inc(self.sems[(en, "c")], op.pend_inc)
            makers[en](body)


def build_program(t_tokens=8192, tc=512, taps=()):
    nch = t_tokens // tc
    nbig = 4
    cpb = nch // nbig
    tb = cpb * tc
    nc = bass.Bass("TRN2", target_bir_lowering=False)

    def din(name, shape, dt=F32):
        return nc.dram_tensor(name, shape, dt, kind="ExternalInput").ap()

    x_full = din("x_full", [t_tokens, DM])
    x_halfi = din("x_halfi", [t_tokens, DMH])
    w_in_x_T = din("w_in_x_T", [DM, DIH])
    w_in_z_T = din("w_in_z_T", [DM, DIH])
    conv_w = din("conv_w", [DIH, 4])
    conv_b = din("conv_b", [DIH, 1])
    w_xp_T = din("w_xp_T", [DIH, 160])
    w_dt_T = din("w_dt_T", [RK, DIH])
    dt_b = din("dt_b", [DIH, 1])
    a_log = din("a_log", [DIH, NST])
    d_vec = din("d_vec", [DIH, 1])
    w_out_T = din("w_out_T", [4096, DMH])
    gu_packed = din("gu_packed", [GU, DM // 8], I32)
    gu_scales_T = din("gu_scales_T", [GU, DM // 128], U16)
    dn_packed = din("dn_packed", [DMH, 2 * IH // 8], I32)
    dn_scales_T = din("dn_scales_T", [DMH, 2 * IH // 128], U16)
    j_in = din("j_in", [1, 1], U32)
    nonce_in = din("nonce_in", [1, 4], U32)
    ident_in = din("ident", [128, 128], BF16)
    constv_in = din("constv", [128, 3])

    out_half = nc.dram_tensor("out_half", [t_tokens, DMH], F32, kind="ExternalOutput").ap()

    def dloc(name, shape, dt):
        return Buf(nc.dram_tensor(name, shape, dt, kind="Internal").ap(), name)

    xn_dr = dloc("xn_dr", [DM, t_tokens], BF16)
    xt_dr = dloc("xt_dr", [DIH, t_tokens], F32)
    zs_dr = dloc("zs_dr", [DIH, t_tokens], F32)
    wout_dr = dloc("wout_dr", [WOBLK, 128, DMH], BF16)
    wgu_dr = dloc("wgu_dr", [MBLK, 128, GU], BF16)
    wd_dr = dloc("wd_dr", [IBLK2, 128, DMH], BF16)
    yffn_dr = dloc("yffn_dr", [t_tokens, DMH], F32)

    def dsh(name, shape, dt):
        return nc.dram_tensor(name, shape, dt, kind="Internal", addr_space="Shared").ap()

    projsh = [dsh(f"projsh{k}", [160, t_tokens], F32) for k in range(2)]
    ysh = [dsh(f"ysh{k}", [DIH, t_tokens], BF16) for k in range(2)]
    hsh = [dsh(f"hsh{k}", [IH, t_tokens], BF16) for k in range(2)]
    sqsh = [dsh(f"sqsh{k}", [t_tokens // P, P], F32) for k in range(2)]
    NFLAG = 2 * nch + 2 * nbig
    flagsh = [dsh(f"flagsh{k}", [NFLAG, 16], U32) for k in range(2)]
    barsh = [dsh(f"barsh{k}", [1, 4], U32) for k in range(2)]
    projsh_b = Buf(None, "projsh")
    ysh_b = Buf(None, "ysh")
    hsh_b = Buf(None, "hsh")
    sqsh_b = Buf(None, "sqsh")

    tap_out = {}
    tap_shapes = {
        "xn": ([DM, t_tokens], BF16), "xt": ([DIH, t_tokens], F32),
        "zs": ([DIH, t_tokens], F32), "proj": ([160, t_tokens], F32),
        "dt": ([DIH, t_tokens], F32), "y": ([DIH, t_tokens], F32),
        "wgu": ([MBLK, 128, GU], BF16), "wd": ([IBLK2, 128, DMH], BF16),
        "s": ([t_tokens, DMH], F32), "yffn": ([t_tokens, DMH], F32),
        "dqx": ([128, DM], BF16), "dqn": ([128, DM], U16),
        "hh": ([2, IH, t_tokens], BF16), "yy": ([2, DIH, t_tokens], BF16),
        "wo": ([WOBLK, 128, DMH], BF16),
    }
    for t in taps:
        sh, dt = tap_shapes[t]
        tap_out[t] = nc.dram_tensor("tap_" + t, sh, dt, kind="ExternalOutput").ap()

    def sbraw(name, shape, dt):
        return nc.alloc_sbuf_tensor(name, shape, dt).ap()

    def sb(name, shape, dt):
        return Buf(sbraw(name, shape, dt), name)

    ident_s = sb("ident_s", [128, 128], BF16)
    j_s = sb("j_s", [1, 1], U32)
    nonce_s = sb("nonce_s", [1, 4], U32)
    bar_s = sb("bar_s", [1, 4], U32)
    zero16 = sb("zero16", [1, 16], U32)
    one16 = sb("one16", [1, 16], U32)
    flag_s = sb("flag_s", [1, 1], U32)
    ones_bf = sb("ones_bf", [1, 128], BF16)
    ones_f = sb("ones_f", [128, 128], F32)

    a_sc = [sb(f"a_sc{i}", [P, NST], F32) for i in range(DBLK)]
    cw_s = [sb(f"cw_s{i}", [P, 4], F32) for i in range(DBLK)]
    cb_s = [sb(f"cb_s{i}", [P, 1], F32) for i in range(DBLK)]
    ncb_s = [sb(f"ncb_s{i}", [P, 1], F32) for i in range(DBLK)]
    dtb_s = [sb(f"dtb_s{i}", [P, 1], F32) for i in range(DBLK)]
    dvec_s = [sb(f"dvec_s{i}", [P, 1], F32) for i in range(DBLK)]
    hstate = [sb(f"hstate{i}", [P, NST], F32) for i in range(DBLK)]
    halo = [sb(f"halo{i}", [P, 3], F32) for i in range(DBLK)]
    wdt_bf = sb("wdt_bf", [RK, DIH], BF16)
    wxp_stg = sb("wxp_stg", [P, 160], F32)
    wxp_bf = [sb(f"wxp_bf{i}", [P, 160], BF16) for i in range(DBLK)]
    wstg = sb("wstg", [P, DIH], F32)
    ln_m = sb("ln_m", [P, 1], F32)
    ln_ssq = sb("ln_ssq", [P, 1], F32)
    ln_rs = sb("ln_rs", [P, 1], F32)
    sq_p = sb("sq_p", [P, 1], F32)
    sq0 = sb("sq0", [P, 1], F32)
    sq1 = sb("sq1", [P, 1], F32)
    rstd3 = sb("rstd3", [P, 1], F32)
    constv = sb("constv_s", [P, 3], F32)

    ARENA_HW = 94000
    arena_ap = sbraw("arena", [P, ARENA_HW], BF16)
    _dt_bytes = {F32: 4, BF16: 2, F16: 2, I32: 4, U16: 2, U32: 4}

    class Arena:
        def __init__(self):
            self.cur = 0

        def reset(self):
            self.cur = 0

        def al(self, name, cols, dt, parts=P):
            b = _dt_bytes[dt]
            off = (self.cur + 3) & ~3
            w_hw = (cols * b + 1) // 2
            assert off + w_hw * 2 <= ARENA_HW * 2, f"arena overflow at {name}"
            view = arena_ap[:parts, off // 2: off // 2 + w_hw]
            self.cur = off + w_hw * 2
            if dt != BF16:
                view = view.bitcast(dt)
                if view.shape[1] != cols:
                    view = view[:, :cols]
            else:
                view = view[:, :cols]
            return Buf(view, name)

    AR = Arena()

    wstg2 = AR.al("wstg2", DMH, F32)
    wo_bf = AR.al("wo_bf", DMH, BF16)
    DQCH_GU = 1024
    DQCH_DN = 1408
    DQCH = 1408
    dq_pk = AR.al("dq_pk", 704, I32)
    dq_nib = AR.al("dq_nib", DQCH, U16)
    dq_t1 = AR.al("dq_t1", DQCH, U16)
    dq_t2 = AR.al("dq_t2", DQCH, U16)
    dq_t3 = AR.al("dq_t3", DQCH, U16)
    dq_sc16 = AR.al("dq_sc16", 44, U16)
    dq_sc = AR.al("dq_sc", 44, F32)
    dq_srep = AR.al("dq_srep", DQCH, F32)
    dq_out = AR.al("dq_out", DQCH, BF16)
    dq_tpo = [AR.al(f"dq_tpo{i}", 128, BF16) for i in range(2)]

    AR.reset()
    win_sb = [AR.al(f"win{i}", DIH, BF16) for i in range(MBLK)]
    xtok = [AR.al(f"xtok{i}", DM, F32) for i in range(2)]
    xc_t = [AR.al(f"xc_t{i}", DM, F32) for i in range(2)]
    xn_tok = [AR.al(f"xn_tok{i}", DM, BF16) for i in range(2)]
    ln_sq = AR.al("ln_sq", DM, F32)
    xn_fm = [AR.al(f"xn_fm{i}", tc, BF16) for i in range(MBLK)]
    xl_t = [AR.al(f"xl_t{i}", tc + 3, F32) for i in range(2)]
    cv_t = [AR.al(f"cv_t{i}", tc, F32) for i in range(2)]
    sil_e = [AR.al(f"sil_e{i}", tc, F32) for i in range(2)]
    sil_r = [AR.al(f"sil_r{i}", tc, F32) for i in range(2)]
    xt_t = [AR.al(f"xt_t{i}", tc, F32) for i in range(2)]
    xt_bf = [AR.al(f"xt_bf{i}", tc, BF16) for i in range(2)]
    z_t = [AR.al(f"z_t{i}", tc, F32) for i in range(2)]
    zs_t = [AR.al(f"zs_t{i}", tc, F32) for i in range(2)]
    pj1_s = AR.al("pj1_s", tc, F32)
    pj2_s = AR.al("pj2_s", tc, F32, parts=32)

    AR.reset()
    pj1a = AR.al("pj1a", tc, F32)
    pj1b = AR.al("pj1b", tc, F32)
    pj1bf = AR.al("pj1bf", tc, BF16)
    pj2a = AR.al("pj2a", tc, F32, parts=32)
    pj2b = AR.al("pj2b", tc, F32, parts=32)
    pjbc_bf = AR.al("pjbc_bf", tc, BF16, parts=32)
    rowstg = [AR.al(f"rowstg{i}", tc, BF16, parts=1) for i in range(2)]
    bb_t = [AR.al(f"bb_t{i}", tc, BF16) for i in range(NST)]
    cb_t = [AR.al(f"cb_t{i}", tc, F32) for i in range(NST)]
    dt_s = [AR.al(f"dt_s{i}", tc, F32) for i in range(2)]
    sp_r = AR.al("sp_r", tc, F32)
    sp_a = AR.al("sp_a", tc, F32)
    w_s = [AR.al(f"w_s{i}", tc, F32) for i in range(2)]
    da_slab = AR.al("da_slab", 8 * tc, F32)
    wb_slab = AR.al("wb_slab", 8 * tc, F32)
    h_slab = AR.al("h_slab", 8 * tc, F32)
    hc_slab = [AR.al(f"hc_slab{i}", 8 * tc, F32) for i in range(2)]
    y_s = [AR.al(f"y_s{i}", tc, F32) for i in range(2)]
    yg_slab = AR.al("yg_slab", DBLK * tc, BF16)
    xt_rd = [AR.al(f"xt_rd{i}", tc, F32) for i in range(2)]
    zs_rd = [AR.al(f"zs_rd{i}", tc, F32) for i in range(2)]
    da_sl = [Buf(da_slab.ap[:, i * tc:(i + 1) * tc], f"da{i}") for i in range(8)]
    wb_sl = [Buf(wb_slab.ap[:, i * tc:(i + 1) * tc], f"wb{i}") for i in range(8)]
    h_sl = [Buf(h_slab.ap[:, i * tc:(i + 1) * tc], f"h{i}") for i in range(8)]
    hc_sl = [[Buf(hc_slab[r].ap[:, i * tc:(i + 1) * tc], f"hc{r}_{i}")
              for i in range(8)] for r in range(2)]
    hst_g = [[Buf(hstate[d].ap[:, g * 8:(g + 1) * 8], f"hst{d}_{g}")
              for g in range(2)] for d in range(DBLK)]

    AR.reset()
    xn3 = [AR.al(f"xn3_{i}", tb, BF16) for i in range(MBLK)]
    ph3_lo_hw = AR.cur
    AR.reset()
    wd_sb = [AR.al(f"wd{i}", DMH, BF16) for i in range(IBLK2)]
    AR.reset()
    wout_sb = [AR.al(f"wout{i}", DMH, BF16) for i in range(WOBLK)]
    AR.cur = max(ph3_lo_hw, IBLK2 * 2048 + 64, WOBLK * 2048 + 64, AR.cur)
    g_e = AR.al("g_e", 512, F32)
    g_cp = AR.al("g_cp", 512, F32)
    g_r = AR.al("g_r", 512, F32)
    g_sg = AR.al("g_sg", 512, F32)
    u_cp = AR.al("u_cp", 512, F32)
    h_bf = [AR.al(f"h_bf{i}", 512, BF16) for i in range(2)]
    wgu_slab = [AR.al(f"wgu_slab{i}", MBLK * 128, BF16) for i in range(2)]
    h_slab3 = [AR.al(f"h_slab3{i}", IBLK2 * 128, BF16) for i in range(1)]
    yf_t = [AR.al(f"yf_t{i}", DMH, F32) for i in range(2)]
    sqsc = AR.al("sqsc", DMH, F32)
    y_slab = [AR.al(f"y_slab{i}", 32 * 128, BF16) for i in range(2)]
    s_ev = [AR.al(f"s_ev{i}", DMH, F32) for i in range(2)]
    yfr = AR.al("yfr", DMH, F32)
    g3e = AR.al("g3e", DMH, F32)
    g3r = AR.al("g3r", DMH, F32)
    xh_t = AR.al("xh_t", DMH, F32)
    o_t = [AR.al(f"o_t{i}", DMH, F32) for i in range(2)]

    import contextlib
    with contextlib.ExitStack() as _stack:
        ent = _stack.enter_context
        cs_sync = ent(nc.semaphore("cs_sync")); cs_vec = ent(nc.semaphore("cs_vec"))
        cs_sca = ent(nc.semaphore("cs_sca")); cs_ten = ent(nc.semaphore("cs_ten"))
        cs_gps = ent(nc.semaphore("cs_gps"))
        sem_poll = ent(nc.semaphore("s_poll")); sem_gq = ent(nc.semaphore("s_gq"))
        dpool = {}
        for qe in ("sync", "gpsimd"):
            for s in range(K.NSLOT):
                dpool[(qe, s)] = ent(nc.semaphore(f"dq_{qe}_{s}"))
        ps0_h = ent(nc.psum_tensor("ps0", [P, 512], F32))
        ps1_h = ent(nc.psum_tensor("ps1", [P, 512], F32))
        ps2_h = ent(nc.psum_tensor("ps2", [P, 512], F32))
        ps3_h = ent(nc.psum_tensor("ps3", [P, 512], F32))
        ps4_h = ent(nc.psum_tensor("ps4", [P, 512], F32))
        ps5_h = ent(nc.psum_tensor("ps5", [P, 512], F32))
        pst0_h = ent(nc.psum_tensor("pst0", [P, 128], BF16))
        pst1_h = ent(nc.psum_tensor("pst1", [P, 128], BF16))
        block = ent(nc.Block())
        kk = K(nc)
        for en, ch in (("sync", cs_sync), ("vector", cs_vec),
                       ("scalar", cs_sca), ("tensor", cs_ten),
                       ("gpsimd", cs_gps)):
            kk.sems[(en, "c")] = ch
        for (qe, s), h in dpool.items():
            kk.sems[(qe, "d", s)] = h
        psb = [Buf(h.ap(), f"ps{i}") for i, h in enumerate(
            [ps0_h, ps1_h, ps2_h, ps3_h, ps4_h, ps5_h])]
        pstb = [Buf(h.ap(), f"pst{i}") for i, h in enumerate([pst0_h, pst1_h])]

        jr = nc.gpsimd.alloc_register("jr")
        r_fl = nc.gpsimd.alloc_register("r_fl")
        r_nd = nc.gpsimd.alloc_register("r_nd")
        r_a = nc.gpsimd.alloc_register("r_a")
        r_b = nc.gpsimd.alloc_register("r_b")
        r_pc = nc.gpsimd.alloc_register("r_pc")
        gqn = [0]

        def g_branch_write(dst0_ap, dst1_ap, src_ap):
            def fn(g):
                with g.If_eq(jr, 0):
                    g.dma_start(out=dst0_ap, in_=src_ap).then_inc(sem_gq, 16)
                with g.Else():
                    g.dma_start(out=dst1_ap, in_=src_ap).then_inc(sem_gq, 16)
                gqn[0] += 16
                g.wait_ge(sem_gq, gqn[0])
                return g.engine_nop()
            return fn

        def g_flag(fid):
            return g_branch_write(flagsh[0][fid:fid + 1, :],
                                  flagsh[1][fid:fid + 1, :], one16.ap)

        def g_poll(fid):
            def fn(g):
                for k in range(2):
                    g.reg_mov(r_nd, 1)
                    with g.While(r_nd):
                        g.reg_add(r_pc, r_pc, 16)
                        g.dma_start(out=flag_s.ap, in_=flagsh[k][fid:fid + 1, 0:1]).then_inc(sem_poll, 16)
                        g.wait_ge(sem_poll, r_pc)
                        g.reg_load(r_fl, flag_s.ap[:1, :1])
                        g.reg_alu(r_nd, r_fl, 0, ALU.is_equal)
                return g.engine_nop()
            return fn

        nc.const_aps.register(F32, LN_EPS, constv.ap[:, 0:1])
        nc.const_aps.register(F32, RMS_EPS, constv.ap[:, 1:2])
        nc.const_aps.register(F32, 1.0, constv.ap[:, 2:3])
        kk.dma("sync", constv, constv_in)
        kk.dma("sync", ident_s, ident_in)
        kk.dma("sync", j_s, j_in)
        kk.dma("sync", nonce_s, nonce_in)
        for i in range(DBLK):
            kk.dma("sync", a_sc[i], a_log[i * P:(i + 1) * P, :])
            kk.dma("sync", cw_s[i], conv_w[i * P:(i + 1) * P, :])
            kk.dma("sync", cb_s[i], conv_b[i * P:(i + 1) * P, :])
            kk.dma("sync", dtb_s[i], dt_b[i * P:(i + 1) * P, :])
            kk.dma("sync", dvec_s[i], d_vec[i * P:(i + 1) * P, :])
        wdt_f32 = Buf(wstg.ap[:RK, :], "wdt_f32")
        kk.dma("sync", wdt_f32, w_dt_T)

        kk.emit("vector", lambda e: e.memset(zero16.ap, 0), writes=[zero16])
        kk.emit("vector", lambda e: e.memset(one16.ap, 1), writes=[one16])
        kk.emit("vector", lambda e: e.memset(ones_bf.ap, 1.0), writes=[ones_bf])
        kk.emit("vector", lambda e: e.memset(ones_f.ap, 1.0), writes=[ones_f])
        for i in range(DBLK):
            kk.emit("scalar", lambda e, i=i: e.activation(out=a_sc[i].ap, in_=a_sc[i].ap, func=AF.Exp),
                    reads=[a_sc[i]], writes=[a_sc[i]])
            kk.emit("vector", lambda e, i=i: e.tensor_scalar(out=a_sc[i].ap, in0=a_sc[i].ap,
                    scalar1=-1.0, scalar2=None, op0=ALU.mult), reads=[a_sc[i]], writes=[a_sc[i]])
            kk.emit("vector", lambda e, i=i: e.tensor_scalar(out=ncb_s[i].ap, in0=cb_s[i].ap,
                    scalar1=-1.0, scalar2=None, op0=ALU.mult), reads=[cb_s[i]], writes=[ncb_s[i]])
            kk.emit("vector", lambda e, i=i: e.memset(halo[i].ap, 0.0), writes=[halo[i]])
            kk.emit("vector", lambda e, i=i: e.memset(hstate[i].ap, 0.0),
                    writes=[hstate[i], hst_g[i][0], hst_g[i][1]])
        kk.emit("scalar", lambda e: e.copy(out=wdt_bf.ap, in_=wdt_f32.ap),
                reads=[wdt_f32], writes=[wdt_bf])
        for i in range(DBLK):
            kk.dma("sync", wxp_stg, w_xp_T[i * P:(i + 1) * P, :])
            kk.emit("scalar", lambda e, i=i: e.copy(out=wxp_bf[i].ap, in_=wxp_stg.ap),
                    reads=[wxp_stg], writes=[wxp_bf[i]])

        def init_fn(g):
            g.reg_load(jr, j_s.ap[:1, :1])
            g.reg_mov(r_pc, 0)
            with g.If_eq(jr, 0):
                for f in range(NFLAG):
                    g.dma_start(out=flagsh[0][f:f + 1, :], in_=zero16.ap).then_inc(sem_gq, 16)
                g.dma_start(out=barsh[0], in_=nonce_s.ap).then_inc(sem_gq, 16)
            with g.Else():
                for f in range(NFLAG):
                    g.dma_start(out=flagsh[1][f:f + 1, :], in_=zero16.ap).then_inc(sem_gq, 16)
                g.dma_start(out=barsh[1], in_=nonce_s.ap).then_inc(sem_gq, 16)
            gqn[0] += 16 * (NFLAG + 1)
            g.wait_ge(sem_gq, gqn[0])
            for barx in (barsh[0], barsh[1]):
                g.reg_mov(r_nd, 1)
                with g.While(r_nd):
                    g.reg_add(r_pc, r_pc, 16)
                    g.dma_start(out=bar_s.ap, in_=barx).then_inc(sem_poll, 16)
                    g.wait_ge(sem_poll, r_pc)
                    g.reg_mov(r_nd, 0)
                    for kq in range(4):
                        g.reg_load(r_a, bar_s.ap[:1, kq:kq + 1])
                        g.reg_load(r_b, nonce_s.ap[:1, kq:kq + 1])
                        g.reg_alu(r_a, r_a, r_b, ALU.not_equal)
                        g.reg_alu(r_nd, r_nd, r_a, ALU.bitwise_or)
            return g.engine_nop()

        kk.dma("gpsimd", wout_dr.ap.rearrange("b p c -> (b p) c"), w_out_T,
               wbufs=[wout_dr])

        def emit_dequant(src_packed, src_scales_T, n_rows_blk, n_in, dst_slab, ch_w):
            n_ch = n_in // ch_w
            pkv = Buf(dq_pk.ap[:, :n_in // 8], "pkv")
            scv16 = Buf(dq_sc16.ap[:, :n_in // 128], "scv16")
            scv = Buf(dq_sc.ap[:, :n_in // 128], "scv")
            nibv = Buf(dq_nib.ap[:, :ch_w], "nibv")
            t1 = Buf(dq_t1.ap[:, :ch_w], "t1")
            t2 = Buf(dq_t2.ap[:, :ch_w], "t2")
            t3 = Buf(dq_t3.ap[:, :ch_w], "t3")
            srep = Buf(dq_srep.ap[:, :ch_w], "srep")
            dqo = Buf(dq_out.ap[:, :ch_w], "dqo")
            for r in range(n_rows_blk):
                kk.barrier()
                kk.dma("sync", pkv, src_packed[r * P:(r + 1) * P, :])
                kk.dma("sync", scv16, src_scales_T[r * P:(r + 1) * P, :])
                kk.emit("vector", lambda e, scv=scv, scv16=scv16: e.tensor_copy(
                    out=scv.ap, in_=scv16.ap.bitcast(F16)), reads=[scv16], writes=[scv])
                for chi in range(n_ch):
                    c0 = chi * ch_w
                    pk16 = pkv.ap.bitcast(U16)[:, c0 // 4:(c0 + ch_w) // 4]
                    for kq in range(4):
                        outv = nibv.ap.rearrange("p (j four) -> p j four", four=4)[:, :, kq]
                        kk.emit("vector", lambda e, kq=kq, outv=outv, pk16=pk16: e.tensor_scalar(
                            out=outv, in0=pk16, scalar1=4 * kq, scalar2=15,
                            op0=ALU.logical_shift_right, op1=ALU.bitwise_and),
                            reads=[pkv], writes=[nibv])
                    kk.emit("vector", lambda e, t1=t1, nibv=nibv: e.tensor_scalar(
                        out=t1.ap, in0=nibv.ap, scalar1=7, scalar2=6,
                        op0=ALU.bitwise_and, op1=ALU.logical_shift_left),
                        reads=[nibv], writes=[t1])
                    kk.emit("vector", lambda e, t2=t2, nibv=nibv: e.tensor_scalar(
                        out=t2.ap, in0=nibv.ap, scalar1=8, scalar2=12,
                        op0=ALU.bitwise_and, op1=ALU.logical_shift_left),
                        reads=[nibv], writes=[t2])
                    kk.emit("vector", lambda e, t1=t1, t2=t2: e.scalar_tensor_tensor(
                        out=t1.ap, in0=t1.ap, scalar=16128, in1=t2.ap,
                        op0=ALU.add, op1=ALU.add), reads=[t1, t2], writes=[t1])
                    kk.emit("vector", lambda e, t3=t3, nibv=nibv: e.tensor_scalar(
                        out=t3.ap, in0=nibv.ap, scalar1=1, scalar2=None,
                        op0=ALU.bitwise_and), reads=[nibv], writes=[t3])
                    kk.emit("vector", lambda e, t3=t3: e.tensor_scalar(
                        out=t3.ap, in0=t3.ap, scalar1=16128, scalar2=None,
                        op0=ALU.mult), reads=[t3], writes=[t3])
                    kk.emit("vector", lambda e, t3=t3, t2=t2: e.tensor_tensor(
                        out=t3.ap, in0=t3.ap, in1=t2.ap, op=ALU.add),
                        reads=[t3, t2], writes=[t3])
                    kk.emit("vector", lambda e, nibv=nibv: e.tensor_scalar(
                        out=nibv.ap, in0=nibv.ap, scalar1=6, scalar2=None,
                        op0=ALU.bitwise_and), reads=[nibv], writes=[nibv])
                    kk.emit("vector", lambda e, t2=t2, nibv=nibv, t1=t1, t3=t3: e.select(
                        out=t2.ap, mask=nibv.ap, on_true=t1.ap, on_false=t3.ap),
                        reads=[nibv, t1, t3], writes=[t2])
                    for gi in range(ch_w // 128):
                        gg = c0 // 128 + gi
                        kk.emit("vector", lambda e, gi=gi, gg=gg, srep=srep, scv=scv:
                                e.tensor_scalar(
                            out=srep.ap[:, gi * 128:(gi + 1) * 128], in0=ones_f.ap,
                            scalar1=scv.ap[:, gg:gg + 1], scalar2=None, op0=ALU.mult),
                            reads=[ones_f, scv], writes=[srep])
                    kk.emit("vector", lambda e, dqo=dqo, t2=t2, srep=srep: e.tensor_tensor(
                        out=dqo.ap, in0=t2.ap.bitcast(BF16), in1=srep.ap, op=ALU.mult),
                        reads=[t2, srep], writes=[dqo])
                    if "dqx" in tap_out and dst_slab is wgu_dr and r == 0:
                        kk.dma("gpsimd", tap_out["dqx"][:, c0:c0 + ch_w], dqo)
                        kk.dma("gpsimd", tap_out["dqn"][:, c0:c0 + ch_w], nibv)
                    for ib in range(ch_w // 128):
                        gib = c0 // 128 + ib
                        pt = pstb[ib % 2]
                        po = dq_tpo[ib % 2]
                        kk.emit("tensor", lambda e, ib=ib, pt=pt, dqo=dqo: e.transpose(
                            pt.ap, dqo.ap[:, ib * 128:(ib + 1) * 128], ident_s.ap),
                            reads=[dqo, ident_s], writes=[pt])
                        kk.emit("scalar", lambda e, pt=pt, po=po: e.copy(out=po.ap, in_=pt.ap),
                                reads=[pt], writes=[po])
                        kk.dma("gpsimd", dst_slab.ap[gib, :, r * P:(r + 1) * P], po,
                               rbufs=[po], wbufs=[dst_slab])

        emit_dequant(gu_packed, gu_scales_T, OBLK, DM, wgu_dr, 1024)
        emit_dequant(dn_packed, dn_scales_T, DMH // P, 2 * IH, wd_dr, 1408)
        kk.emit("gpsimd", init_fn, reads=[j_s, nonce_s, zero16, one16], writes=[bar_s])
        if "wgu" in tap_out:
            kk.dma("gpsimd", tap_out["wgu"], wgu_dr)
        if "wd" in tap_out:
            kk.dma("gpsimd", tap_out["wd"], wd_dr)

        kk.barrier()
        for part in range(2):
            w_src = w_in_x_T if part == 0 else w_in_z_T
            for i in range(MBLK):
                kk.dma("sync", wstg, w_src[i * P:(i + 1) * P, :])
                kk.emit("scalar", lambda e, i=i: e.copy(out=win_sb[i].ap, in_=wstg.ap),
                        reads=[wstg], writes=[win_sb[i]])

            for c in range(nch):
                t0 = c * tc
                if part == 0:
                    for tt in range(tc // P):
                        xk = xtok[tt % 2]
                        xcv = xc_t[tt % 2]
                        xnv = xn_tok[tt % 2]
                        kk.dma("sync", xk, x_full[t0 + tt * P: t0 + (tt + 1) * P, :])
                        kk.emit("vector", lambda e, xk=xk: e.tensor_reduce(
                            out=ln_m.ap, in_=xk.ap, axis=AX.X, op=ALU.add),
                            reads=[xk], writes=[ln_m])
                        kk.emit("vector", lambda e: e.tensor_scalar(out=ln_m.ap,
                                in0=ln_m.ap, scalar1=1.0 / DM, scalar2=None, op0=ALU.mult),
                                reads=[ln_m], writes=[ln_m])
                        kk.emit("vector", lambda e, xk=xk, xcv=xcv: e.tensor_scalar(
                            out=xcv.ap, in0=xk.ap, scalar1=ln_m.ap, scalar2=None,
                            op0=ALU.subtract), reads=[xk, ln_m], writes=[xcv])
                        kk.emit("scalar", lambda e, xcv=xcv: e.activation(
                            out=ln_sq.ap, in_=xcv.ap, func=AF.Square, accum_out=ln_ssq.ap),
                            reads=[xcv], writes=[ln_sq, ln_ssq])
                        kk.emit("scalar", lambda e: e.activation(out=ln_rs.ap,
                                in_=ln_ssq.ap, func=AF.Ln, scale=1.0 / DM, bias=LN_EPS),
                                reads=[ln_ssq, constv], writes=[ln_rs])
                        kk.emit("scalar", lambda e: e.activation(out=ln_rs.ap,
                                in_=ln_rs.ap, func=AF.Exp, scale=-0.5),
                                reads=[ln_rs], writes=[ln_rs])
                        kk.emit("vector", lambda e, xcv=xcv, xnv=xnv: e.tensor_scalar(
                            out=xnv.ap, in0=xcv.ap, scalar1=ln_rs.ap, scalar2=None,
                            op0=ALU.mult), reads=[xcv, ln_rs], writes=[xnv])
                        for mb in range(MBLK):
                            pt = pstb[mb % 2]
                            kk.emit("tensor", lambda e, mb=mb, pt=pt, xnv=xnv: e.transpose(
                                pt.ap, xnv.ap[:, mb * 128:(mb + 1) * 128], ident_s.ap),
                                reads=[xnv, ident_s], writes=[pt])
                            kk.emit("scalar", lambda e, mb=mb, tt=tt, pt=pt: e.copy(
                                out=xn_fm[mb].ap[:, tt * P:(tt + 1) * P], in_=pt.ap),
                                reads=[pt], writes=[xn_fm[mb]])
                    for mb in range(MBLK):
                        kk.dma("gpsimd", xn_dr.ap[mb * P:(mb + 1) * P, t0:t0 + tc],
                               xn_fm[mb], wbufs=[xn_dr])
                else:
                    for mb in range(MBLK):
                        kk.dma("sync", xn_fm[mb], xn_dr.ap[mb * P:(mb + 1) * P, t0:t0 + tc],
                               rbufs=[xn_dr])

                for eb in range(DBLK):
                    pm = psb[eb % 2]
                    for k in range(MBLK):
                        kk.emit("tensor", lambda e, k=k, eb=eb, pm=pm: e.matmul(
                            pm.ap, win_sb[k].ap[:, eb * 128:(eb + 1) * 128], xn_fm[k].ap,
                            start=(k == 0), stop=(k == MBLK - 1)),
                            reads=[win_sb[k], xn_fm[k]], writes=[pm])
                    if part == 0:
                        xl = xl_t[eb % 2]
                        cv = cv_t[eb % 2]
                        se_ = sil_e[eb % 2]
                        sr_ = sil_r[eb % 2]
                        xt_ = xt_t[eb % 2]
                        xb_ = xt_bf[eb % 2]
                        kk.emit("vector", lambda e, xl=xl, eb=eb: e.tensor_copy(
                            out=xl.ap[:, 0:3], in_=halo[eb].ap), reads=[halo[eb]], writes=[xl])
                        kk.emit("scalar", lambda e, xl=xl, pm=pm: e.copy(
                            out=xl.ap[:, 3:3 + tc], in_=pm.ap), reads=[pm], writes=[xl])
                        kk.emit("vector", lambda e, xl=xl, cv=cv, eb=eb: e.tensor_scalar(
                            out=cv.ap, in0=xl.ap[:, 0:tc], scalar1=cw_s[eb].ap[:, 0:1],
                            scalar2=None, op0=ALU.mult), reads=[xl, cw_s[eb]], writes=[cv])
                        for kq in range(1, 4):
                            kk.emit("vector", lambda e, xl=xl, cv=cv, eb=eb, kq=kq:
                                    e.scalar_tensor_tensor(out=cv.ap, in0=xl.ap[:, kq:kq + tc],
                                    scalar=cw_s[eb].ap[:, kq:kq + 1], in1=cv.ap,
                                    op0=ALU.mult, op1=ALU.add),
                                    reads=[xl, cw_s[eb], cv], writes=[cv])
                        kk.emit("vector", lambda e, xl=xl, eb=eb: e.tensor_copy(
                            out=halo[eb].ap, in_=xl.ap[:, tc:tc + 3]),
                            reads=[xl], writes=[halo[eb]])
                        kk.emit("scalar", lambda e, cv=cv, se_=se_, eb=eb: e.activation(
                            out=se_.ap, in_=cv.ap, func=AF.Exp, scale=-1.0, bias=ncb_s[eb].ap),
                            reads=[cv, ncb_s[eb]], writes=[se_])
                        kk.emit("vector", lambda e, se_=se_, sr_=sr_: e.tensor_scalar(
                            out=sr_.ap, in0=se_.ap, scalar1=1.0, scalar2=None, op0=ALU.add),
                            reads=[se_], writes=[sr_])
                        kk.emit("vector", lambda e, sr_=sr_: e.reciprocal(out=sr_.ap, in_=sr_.ap),
                                reads=[sr_], writes=[sr_])
                        kk.emit("vector", lambda e, cv=cv, sr_=sr_, xt_=xt_, eb=eb:
                                e.scalar_tensor_tensor(out=xt_.ap, in0=cv.ap,
                                scalar=cb_s[eb].ap, in1=sr_.ap, op0=ALU.add, op1=ALU.mult),
                                reads=[cv, cb_s[eb], sr_], writes=[xt_])
                        kk.emit("vector", lambda e, xt_=xt_, xb_=xb_: e.tensor_copy(
                            out=xb_.ap, in_=xt_.ap), reads=[xt_], writes=[xb_])
                        kk.dma("gpsimd", xt_dr.ap[eb * P:(eb + 1) * P, t0:t0 + tc], xt_,
                               wbufs=[xt_dr])
                        kk.emit("tensor", lambda e, eb=eb, xb_=xb_: e.matmul(
                            psb[2].ap, wxp_bf[eb].ap[:, 0:128], xb_.ap,
                            start=(eb == 0), stop=(eb == DBLK - 1)),
                            reads=[wxp_bf[eb], xb_], writes=[psb[2]])
                        kk.emit("tensor", lambda e, eb=eb, xb_=xb_: e.matmul(
                            psb[3].ap[:32, :], wxp_bf[eb].ap[:, 128:160], xb_.ap,
                            start=(eb == 0), stop=(eb == DBLK - 1)),
                            reads=[wxp_bf[eb], xb_], writes=[psb[3]])
                    else:
                        zv = z_t[eb % 2]
                        zs_ = zs_t[eb % 2]
                        se_ = sil_e[eb % 2]
                        sr_ = sil_r[eb % 2]
                        kk.emit("scalar", lambda e, se_=se_, pm=pm: e.activation(
                            out=se_.ap, in_=pm.ap, func=AF.Exp, scale=-1.0),
                            reads=[pm], writes=[se_])
                        kk.emit("scalar", lambda e, zv=zv, pm=pm: e.copy(out=zv.ap, in_=pm.ap),
                                reads=[pm], writes=[zv])
                        kk.emit("vector", lambda e, se_=se_, sr_=sr_: e.tensor_scalar(
                            out=sr_.ap, in0=se_.ap, scalar1=1.0, scalar2=None, op0=ALU.add),
                            reads=[se_], writes=[sr_])
                        kk.emit("vector", lambda e, sr_=sr_: e.reciprocal(out=sr_.ap, in_=sr_.ap),
                                reads=[sr_], writes=[sr_])
                        kk.emit("vector", lambda e, zv=zv, sr_=sr_, zs_=zs_: e.tensor_tensor(
                            out=zs_.ap, in0=zv.ap, in1=sr_.ap, op=ALU.mult),
                            reads=[zv, sr_], writes=[zs_])
                        kk.dma("gpsimd", zs_dr.ap[eb * P:(eb + 1) * P, t0:t0 + tc], zs_,
                               wbufs=[zs_dr])

                if part == 0:
                    kk.emit("scalar", lambda e: e.copy(out=pj1_s.ap, in_=psb[2].ap),
                            reads=[psb[2]], writes=[pj1_s])
                    kk.emit("scalar", lambda e: e.copy(out=pj2_s.ap, in_=psb[3].ap[:32, :]),
                            reads=[psb[3]], writes=[pj2_s])
                    kk.emit("gpsimd", g_branch_write(projsh[0][0:128, t0:t0 + tc],
                                                     projsh[1][0:128, t0:t0 + tc], pj1_s.ap),
                            reads=[pj1_s, j_s, bar_s], writes=[projsh_b])
                    kk.emit("gpsimd", g_branch_write(projsh[0][128:160, t0:t0 + tc],
                                                     projsh[1][128:160, t0:t0 + tc], pj2_s.ap),
                            reads=[pj2_s, j_s], writes=[projsh_b])
                    kk.emit("gpsimd", g_flag(c), reads=[one16, j_s, projsh_b])

        if "xn" in tap_out:
            kk.dma("gpsimd", tap_out["xn"], xn_dr)
        if "xt" in tap_out:
            kk.dma("gpsimd", tap_out["xt"], xt_dr)
        if "zs" in tap_out:
            kk.dma("gpsimd", tap_out["zs"], zs_dr)

        kk.barrier()
        poll_tok = Buf(None, "poll_tok")
        for c in range(nch):
            t0 = c * tc
            kk.emit("gpsimd", g_poll(c), reads=[j_s], writes=[poll_tok, flag_s])
            kk.dma("sync", pj1a, projsh[0][0:128, t0:t0 + tc], rbufs=[poll_tok])
            kk.dma("sync", pj1b, projsh[1][0:128, t0:t0 + tc])
            kk.dma("sync", pj2a, projsh[0][128:160, t0:t0 + tc])
            kk.dma("sync", pj2b, projsh[1][128:160, t0:t0 + tc])
            kk.emit("vector", lambda e: e.tensor_tensor(out=pj1a.ap, in0=pj1a.ap,
                    in1=pj1b.ap, op=ALU.add), reads=[pj1a, pj1b], writes=[pj1a])
            kk.emit("vector", lambda e: e.tensor_tensor(out=pj2a.ap, in0=pj2a.ap,
                    in1=pj2b.ap, op=ALU.add), reads=[pj2a, pj2b], writes=[pj2a])
            kk.emit("vector", lambda e: e.tensor_copy(out=pj1bf.ap, in_=pj1a.ap),
                    reads=[pj1a], writes=[pj1bf])
            kk.emit("vector", lambda e: e.tensor_copy(out=pjbc_bf.ap, in_=pj2a.ap),
                    reads=[pj2a], writes=[pjbc_bf])
            if "proj" in tap_out:
                kk.dma("gpsimd", tap_out["proj"][0:128, t0:t0 + tc], pj1a)
                kk.dma("gpsimd", tap_out["proj"][128:160, t0:t0 + tc], pj2a)

            for n in range(2 * NST):
                dstf = bb_t[n] if n < NST else cb_t[n - NST]
                pbc = psb[4 + (n % 2)]
                rsg = rowstg[n % 2]
                kk.dma("sync", rsg, pjbc_bf.ap[n:n + 1, :], rbufs=[pjbc_bf])
                kk.emit("tensor", lambda e, pbc=pbc, rsg=rsg: e.matmul(
                    pbc.ap, ones_bf.ap, rsg.ap, start=True, stop=True),
                    reads=[rsg, ones_bf], writes=[pbc])
                kk.emit("scalar", lambda e, pbc=pbc, dstf=dstf: e.copy(
                    out=dstf.ap, in_=pbc.ap), reads=[pbc], writes=[dstf])

            def flush_pend(pend):
                dbp, grpp, rp = pend
                sl = hc_sl[rp]
                for ni in range(4):
                    kk.emit("vector", lambda e, sl=sl, ni=ni: e.tensor_tensor(
                        out=sl[ni].ap, in0=sl[ni].ap, in1=sl[ni + 4].ap,
                        op=ALU.add), reads=[sl[ni], sl[ni + 4]], writes=[sl[ni]])
                for ni in range(2):
                    kk.emit("vector", lambda e, sl=sl, ni=ni: e.tensor_tensor(
                        out=sl[ni].ap, in0=sl[ni].ap, in1=sl[ni + 2].ap,
                        op=ALU.add), reads=[sl[ni], sl[ni + 2]], writes=[sl[ni]])
                kk.emit("vector", lambda e, sl=sl: e.tensor_tensor(
                    out=sl[0].ap, in0=sl[0].ap, in1=sl[1].ap, op=ALU.add),
                    reads=[sl[0], sl[1]], writes=[sl[0]])
                if grpp == 1:
                    curp = dbp % 2
                    ys = y_s[curp]
                    kk.emit("vector", lambda e, ys=ys: e.tensor_tensor(out=ys.ap,
                            in0=hc_sl[0][0].ap, in1=hc_sl[1][0].ap, op=ALU.add),
                            reads=[hc_sl[0][0], hc_sl[1][0]], writes=[ys])
                    if "y" in tap_out:
                        kk.dma("gpsimd",
                               tap_out["y"][dbp * P:(dbp + 1) * P, t0:t0 + tc], ys)
                    kk.emit("vector", lambda e, ys=ys, dbp=dbp, curp=curp:
                            e.scalar_tensor_tensor(
                                out=ys.ap, in0=xt_rd[curp].ap, scalar=dvec_s[dbp].ap,
                                in1=ys.ap, op0=ALU.mult, op1=ALU.add),
                            reads=[xt_rd[curp], dvec_s[dbp], ys], writes=[ys])
                    kk.emit("vector", lambda e, ys=ys, curp=curp: e.tensor_tensor(
                        out=ys.ap, in0=ys.ap, in1=zs_rd[curp].ap, op=ALU.mult),
                        reads=[ys, zs_rd[curp]], writes=[ys])
                    kk.emit("scalar", lambda e, ys=ys, dbp=dbp: e.copy(
                        out=yg_slab.ap[:, dbp * tc:(dbp + 1) * tc], in_=ys.ap),
                        reads=[ys], writes=[yg_slab])

            def emit_front(db):
                cur = db % 2
                pm = psb[db % 2]
                dts, ws = dt_s[cur], w_s[cur]
                xtr, zsr = xt_rd[cur], zs_rd[cur]
                kk.emit("tensor", lambda e, db=db, pm=pm: e.matmul(
                    pm.ap, wdt_bf.ap[:, db * 128:(db + 1) * 128], pj1bf.ap,
                    start=True, stop=True), reads=[wdt_bf, pj1bf], writes=[pm])
                kk.emit("scalar", lambda e, pm=pm, db=db: e.activation(
                    out=sp_r.ap, in_=pm.ap, func=AF.Relu, bias=dtb_s[db].ap),
                    reads=[pm, dtb_s[db]], writes=[sp_r])
                kk.emit("scalar", lambda e, pm=pm, db=db: e.activation(
                    out=sp_a.ap, in_=pm.ap, func=AF.Abs, bias=dtb_s[db].ap),
                    reads=[pm, dtb_s[db]], writes=[sp_a])
                kk.emit("scalar", lambda e: e.activation(out=sp_a.ap, in_=sp_a.ap,
                        func=AF.Exp, scale=-1.0), reads=[sp_a], writes=[sp_a])
                kk.emit("scalar", lambda e: e.activation(out=sp_a.ap, in_=sp_a.ap,
                        func=AF.Ln, bias=1.0), reads=[sp_a, constv], writes=[sp_a])
                kk.dma("sync", xtr, xt_dr.ap[db * P:(db + 1) * P, t0:t0 + tc],
                       rbufs=[xt_dr])
                kk.dma("sync", zsr, zs_dr.ap[db * P:(db + 1) * P, t0:t0 + tc],
                       rbufs=[zs_dr])
                kk.emit("gpsimd", lambda e, dts=dts: e.tensor_tensor(
                    out=dts.ap, in0=sp_r.ap, in1=sp_a.ap, op=ALU.add),
                    reads=[sp_r, sp_a], writes=[dts])
                kk.emit("gpsimd", lambda e, dts=dts, ws=ws, xtr=xtr: e.tensor_tensor(
                    out=ws.ap, in0=dts.ap, in1=xtr.ap, op=ALU.mult),
                    reads=[dts, xtr], writes=[ws])
                if "dt" in tap_out:
                    kk.dma("gpsimd", tap_out["dt"][db * P:(db + 1) * P, t0:t0 + tc],
                           dts)

            def emit_wb(db, grp):
                ws = w_s[db % 2]
                for ni in range(8):
                    n = grp * 8 + ni
                    weng = "gpsimd" if ni < 5 else "vector"
                    kk.emit(weng, lambda e, n=n, ni=ni, ws=ws: e.tensor_tensor(
                        out=wb_sl[ni].ap, in0=ws.ap, in1=bb_t[n].ap,
                        op=ALU.mult), reads=[ws, bb_t[n]], writes=[wb_sl[ni]])

            pend = None
            emit_front(0)
            emit_wb(0, 0)
            for db in range(DBLK):
                dts = dt_s[db % 2]
                for grp in range(2):
                    for ni in range(8):
                        n = grp * 8 + ni
                        kk.emit("scalar", lambda e, db=db, n=n, ni=ni, dts=dts:
                                e.activation(
                                    out=da_sl[ni].ap, in_=dts.ap, func=AF.Exp,
                                    scale=a_sc[db].ap[:, n:n + 1]),
                                reads=[dts, a_sc[db]], writes=[da_sl[ni]])
                    for ni in range(8):
                        n = grp * 8 + ni
                        kk.emit("vector", lambda e, db=db, n=n, ni=ni, grp=grp:
                                e.tensor_tensor_scan(
                                    out=h_sl[ni].ap,
                                    data0=da_sl[ni].ap, data1=wb_sl[ni].ap,
                                    initial=hstate[db].ap[:, n:n + 1],
                                    op0=ALU.mult, op1=ALU.add),
                                reads=[da_sl[ni], wb_sl[ni], hst_g[db][grp]],
                                writes=[h_sl[ni]])
                    if pend is not None:
                        flush_pend(pend)
                        pend = None
                    kk.emit("gpsimd", lambda e, db=db, grp=grp: e.tensor_copy(
                        out=hstate[db].ap[:, grp * 8:(grp + 1) * 8],
                        in_=h_slab.ap.rearrange(
                            "p (n t) -> p n t", n=8)[:, :, tc - 1:tc]),
                        reads=list(h_sl), writes=[hst_g[db][grp]])
                    if grp == 0:
                        emit_wb(db, 1)
                    elif db + 1 < DBLK:
                        emit_front(db + 1)
                        emit_wb(db + 1, 0)
                    for ni in range(8):
                        n = grp * 8 + ni
                        kk.emit("gpsimd", lambda e, n=n, ni=ni, grp=grp:
                                e.tensor_tensor(
                                    out=hc_sl[grp][ni].ap, in0=h_sl[ni].ap,
                                    in1=cb_t[n].ap, op=ALU.mult),
                                reads=[h_sl[ni], cb_t[n]],
                                writes=[hc_sl[grp][ni]])
                    pend = (db, grp, grp)
            flush_pend(pend)
            pend = None

            src = yg_slab.ap.rearrange("p (b t) -> p b t", b=DBLK)
            d0 = ysh[0][:, t0:t0 + tc].rearrange("(b p) t -> p b t", p=P)
            d1 = ysh[1][:, t0:t0 + tc].rearrange("(b p) t -> p b t", p=P)
            kk.emit("gpsimd", g_branch_write(d0, d1, src),
                    reads=[yg_slab, j_s], writes=[ysh_b])
            kk.emit("gpsimd", g_flag(nch + c), reads=[one16, j_s, ysh_b])

        kk.barrier()

        for c in range(nch):
            kk.emit("gpsimd", g_poll(nch + c), reads=[j_s], writes=[poll_tok, flag_s])
        kk.barrier()

        for C in range(nbig):
            tb0 = C * tb
            for mb in range(MBLK):
                kk.dma("sync", xn3[mb], xn_dr.ap[mb * P:(mb + 1) * P, tb0:tb0 + tb],
                       rbufs=[xn_dr])
            for ip in range(IBLK2 // 2):
                for half in range(2):
                    ob = ip + half * (OBLK // 2)
                    for k in range(MBLK):
                        kk.dma("sync",
                               Buf(wgu_slab[half].ap[:, k * 128:(k + 1) * 128], "wgu_t"),
                               wgu_dr.ap[k, :, ob * P:(ob + 1) * P],
                               rbufs=[wgu_dr], wbufs=[wgu_slab[half]])
                for sl in range(tb // 512):
                    for half in range(2):
                        pm = psb[half]
                        for k in range(MBLK):
                            kk.emit("tensor", lambda e, k=k, half=half, sl=sl, pm=pm: e.matmul(
                                pm.ap, wgu_slab[half].ap[:, k * 128:(k + 1) * 128],
                                xn3[k].ap[:, sl * 512:(sl + 1) * 512],
                                start=(k == 0), stop=(k == MBLK - 1)),
                                reads=[wgu_slab[half], xn3[k]], writes=[pm])
                    kk.emit("scalar", lambda e: e.activation(
                        out=g_e.ap, in_=psb[0].ap, func=AF.Exp, scale=-1.0),
                        reads=[psb[0]], writes=[g_e])
                    kk.emit("scalar", lambda e: e.copy(out=g_cp.ap, in_=psb[0].ap),
                            reads=[psb[0]], writes=[g_cp])
                    kk.emit("scalar", lambda e: e.copy(out=u_cp.ap, in_=psb[1].ap),
                            reads=[psb[1]], writes=[u_cp])
                    kk.emit("vector", lambda e: e.tensor_scalar(out=g_r.ap,
                            in0=g_e.ap, scalar1=1.0, scalar2=None, op0=ALU.add),
                            reads=[g_e], writes=[g_r])
                    kk.emit("vector", lambda e: e.reciprocal(out=g_r.ap, in_=g_r.ap),
                            reads=[g_r], writes=[g_r])
                    kk.emit("vector", lambda e: e.tensor_tensor(out=g_sg.ap,
                            in0=g_cp.ap, in1=g_r.ap, op=ALU.mult),
                            reads=[g_cp, g_r], writes=[g_sg])
                    kk.emit("vector", lambda e: e.tensor_tensor(out=g_sg.ap,
                            in0=g_sg.ap, in1=u_cp.ap, op=ALU.mult),
                            reads=[g_sg, u_cp], writes=[g_sg])
                    hb = h_bf[sl % 2]
                    kk.emit("vector", lambda e, hb=hb: e.tensor_copy(
                        out=hb.ap, in_=g_sg.ap), reads=[g_sg], writes=[hb])
                    dst0 = hsh[0][ip * P:(ip + 1) * P, tb0 + sl * 512: tb0 + (sl + 1) * 512]
                    dst1 = hsh[1][ip * P:(ip + 1) * P, tb0 + sl * 512: tb0 + (sl + 1) * 512]
                    kk.emit("gpsimd", g_branch_write(dst0, dst1, hb.ap),
                            reads=[hb, j_s], writes=[hsh_b])
            kk.emit("gpsimd", g_flag(2 * nch + C), reads=[one16, j_s, hsh_b])
            kk.emit("gpsimd", g_poll(2 * nch + C), reads=[j_s], writes=[poll_tok, flag_s])
            kk.barrier()

            for ib in range(IBLK2):
                kk.dma("sync", wd_sb[ib], wd_dr.ap[ib], rbufs=[wd_dr])
            for t128 in range(tb // P):
                trow = tb0 + t128 * P
                pm0 = psb[2 * (t128 % 2)]
                pm1 = psb[2 * (t128 % 2) + 1]
                hsl = h_slab3[0]
                for k2 in range(2):
                    kk.dma("sync",
                           hsl.ap.rearrange("p (b t) -> p b t", b=IBLK2)[:, k2 * 22:(k2 + 1) * 22, :],
                           hsh[k2][:, trow:trow + P].rearrange("(b p) t -> p b t", p=P),
                           rbufs=[poll_tok], wbufs=[hsl])
                for ib in range(IBLK2):
                    hl_ap = hsl.ap[:, ib * 128:(ib + 1) * 128]
                    kk.emit("tensor", lambda e, hl_ap=hl_ap, ib=ib, pm0=pm0: e.matmul(
                        pm0.ap, hl_ap, wd_sb[ib].ap[:, 0:512],
                        start=(ib == 0), stop=(ib == IBLK2 - 1)),
                        reads=[hsl, wd_sb[ib]], writes=[pm0])
                    kk.emit("tensor", lambda e, hl_ap=hl_ap, ib=ib, pm1=pm1: e.matmul(
                        pm1.ap, hl_ap, wd_sb[ib].ap[:, 512:1024],
                        start=(ib == 0), stop=(ib == IBLK2 - 1)),
                        reads=[hsl, wd_sb[ib]], writes=[pm1])
                yf = yf_t[t128 % 2]
                kk.emit("scalar", lambda e, pm0=pm0, yf=yf: e.copy(
                    out=yf.ap[:, 0:512], in_=pm0.ap), reads=[pm0], writes=[yf])
                kk.emit("scalar", lambda e, pm1=pm1, yf=yf: e.copy(
                    out=yf.ap[:, 512:1024], in_=pm1.ap), reads=[pm1], writes=[yf])
                kk.dma("gpsimd", yffn_dr.ap[trow:trow + P, :], yf, wbufs=[yffn_dr])
                kk.emit("scalar", lambda e, yf=yf: e.activation(
                    out=sqsc.ap, in_=yf.ap, func=AF.Square, accum_out=sq_p.ap),
                    reads=[yf], writes=[sqsc, sq_p])
                rowi = trow // P
                dst0 = sqsh[0][rowi:rowi + 1, :].rearrange("r p -> p r")
                dst1 = sqsh[1][rowi:rowi + 1, :].rearrange("r p -> p r")
                kk.emit("gpsimd", g_branch_write(dst0, dst1, sq_p.ap),
                        reads=[sq_p, j_s], writes=[sqsh_b])
            kk.emit("gpsimd", g_flag(2 * nch + nbig + C), reads=[one16, j_s, sqsh_b])
            kk.emit("gpsimd", g_poll(2 * nch + nbig + C), reads=[j_s], writes=[poll_tok, flag_s])
            kk.barrier()

            for ib in range(WOBLK):
                kk.dma("sync", wout_sb[ib], wout_dr.ap[ib], rbufs=[wout_dr])
            for t128 in range(tb // P):
                trow = tb0 + t128 * P
                rowi = trow // P
                pm0 = psb[2 + (t128 % 2)]
                pm1 = psb[4 + (t128 % 2)]
                ysl = y_slab[t128 % 2]
                for k2 in range(2):
                    kk.dma("sync",
                           ysl.ap.rearrange("p (b t) -> p b t", b=32)[:, k2 * 16:(k2 + 1) * 16, :],
                           ysh[k2][:, trow:trow + P].rearrange("(b p) t -> p b t", p=P),
                           rbufs=[poll_tok], wbufs=[ysl])
                for ib in range(WOBLK):
                    yl_ap = ysl.ap[:, ib * 128:(ib + 1) * 128]
                    kk.emit("tensor", lambda e, yl_ap=yl_ap, pm0=pm0, ib=ib: e.matmul(
                        pm0.ap, yl_ap, wout_sb[ib].ap[:, 0:512],
                        start=(ib == 0), stop=(ib == WOBLK - 1)),
                        reads=[ysl, wout_sb[ib]], writes=[pm0])
                    kk.emit("tensor", lambda e, yl_ap=yl_ap, pm1=pm1, ib=ib: e.matmul(
                        pm1.ap, yl_ap, wout_sb[ib].ap[:, 512:1024],
                        start=(ib == 0), stop=(ib == WOBLK - 1)),
                        reads=[ysl, wout_sb[ib]], writes=[pm1])
                sv = s_ev[t128 % 2]
                kk.emit("scalar", lambda e, pm0=pm0, sv=sv: e.copy(
                    out=sv.ap[:, 0:512], in_=pm0.ap), reads=[pm0], writes=[sv])
                kk.emit("scalar", lambda e, pm1=pm1, sv=sv: e.copy(
                    out=sv.ap[:, 512:1024], in_=pm1.ap), reads=[pm1], writes=[sv])
                if "s" in tap_out:
                    kk.dma("gpsimd", tap_out["s"][trow:trow + P, :], sv)
                kk.dma("sync", sq0, sqsh[0][rowi:rowi + 1, :].rearrange("r p -> p r"),
                       rbufs=[poll_tok])
                kk.dma("sync", sq1, sqsh[1][rowi:rowi + 1, :].rearrange("r p -> p r"),
                       rbufs=[poll_tok])
                kk.emit("vector", lambda e: e.tensor_tensor(out=sq0.ap, in0=sq0.ap,
                        in1=sq1.ap, op=ALU.add), reads=[sq0, sq1], writes=[sq0])
                kk.emit("scalar", lambda e: e.activation(out=rstd3.ap, in_=sq0.ap,
                        func=AF.Ln, scale=1.0 / DM, bias=RMS_EPS), reads=[sq0, constv], writes=[rstd3])
                kk.emit("scalar", lambda e: e.activation(out=rstd3.ap, in_=rstd3.ap,
                        func=AF.Exp, scale=-0.5), reads=[rstd3], writes=[rstd3])
                kk.dma("sync", yfr, yffn_dr.ap[trow:trow + P, :], rbufs=[yffn_dr])
                kk.emit("vector", lambda e: e.tensor_scalar(out=yfr.ap, in0=yfr.ap,
                        scalar1=rstd3.ap, scalar2=None, op0=ALU.mult),
                        reads=[yfr, rstd3], writes=[yfr])
                kk.emit("scalar", lambda e: e.activation(out=g3e.ap, in_=yfr.ap,
                        func=AF.Exp, scale=-1.0), reads=[yfr], writes=[g3e])
                kk.emit("vector", lambda e: e.tensor_scalar(out=g3r.ap, in0=g3e.ap,
                        scalar1=1.0, scalar2=None, op0=ALU.add), reads=[g3e], writes=[g3r])
                kk.emit("vector", lambda e: e.reciprocal(out=g3r.ap, in_=g3r.ap),
                        reads=[g3r], writes=[g3r])
                kk.emit("vector", lambda e: e.tensor_tensor(out=yfr.ap, in0=yfr.ap,
                        in1=g3r.ap, op=ALU.mult), reads=[yfr, g3r], writes=[yfr])
                ov = o_t[t128 % 2]
                kk.dma("sync", xh_t, x_halfi[trow:trow + P, :])
                kk.emit("vector", lambda e, sv=sv, ov=ov: e.tensor_tensor(
                    out=ov.ap, in0=sv.ap, in1=yfr.ap, op=ALU.mult),
                    reads=[sv, yfr], writes=[ov])
                kk.emit("vector", lambda e, ov=ov: e.tensor_tensor(
                    out=ov.ap, in0=ov.ap, in1=xh_t.ap, op=ALU.add),
                    reads=[ov, xh_t], writes=[ov])
                kk.dma("sync", out_half[trow:trow + P, :], ov)
            kk.barrier()

        if "yffn" in tap_out:
            kk.dma("gpsimd", tap_out["yffn"], yffn_dr)
        if "hh" in tap_out:
            kk.dma("gpsimd", tap_out["hh"][0], hsh[0], rbufs=[hsh_b])
            kk.dma("gpsimd", tap_out["hh"][1], hsh[1], rbufs=[hsh_b])
        if "wo" in tap_out:
            kk.dma("gpsimd", tap_out["wo"], wout_dr)
        if "yy" in tap_out:
            kk.dma("gpsimd", tap_out["yy"][0], ysh[0], rbufs=[ysh_b])
            kk.dma("gpsimd", tap_out["yy"][1], ysh[1], rbufs=[ysh_b])

        kk.barrier()
        for en in ("sync", "gpsimd"):
            for s in range(K.NSLOT):
                if kk.slot_cnt[en][s]:
                    kk._wait(en, (en, "d", s), kk.slot_cnt[en][s])

        kk.flush(block)

    nc._kk = kk
    return nc



_IDENT = np.eye(128, dtype=ml_dtypes.bfloat16)


def _core_inputs(core, inp):
    b, j = core // 2, core & 1
    x = np.ascontiguousarray(inp["L0_attn_hidden_states"][b])
    wi = inp["L0_attn_in_proj_w"]
    di0, di1 = j * DIH, (j + 1) * DIH
    return {
        "x_full": x,
        "x_halfi": np.ascontiguousarray(x[:, j * DMH:(j + 1) * DMH]),
        "w_in_x_T": np.ascontiguousarray(wi[di0:di1, :].T),
        "w_in_z_T": np.ascontiguousarray(wi[4096 + di0:4096 + di1, :].T),
        "conv_w": np.ascontiguousarray(inp["L0_attn_conv_w"][di0:di1]),
        "conv_b": np.ascontiguousarray(inp["L0_attn_conv_b"][di0:di1, None]),
        "w_xp_T": np.ascontiguousarray(inp["L0_attn_x_proj_w"][:, di0:di1].T),
        "w_dt_T": np.ascontiguousarray(inp["L0_attn_dt_proj_w"][di0:di1, :].T),
        "dt_b": np.ascontiguousarray(inp["L0_attn_dt_proj_b"][di0:di1, None]),
        "a_log": np.ascontiguousarray(inp["L0_attn_A_log"][di0:di1]),
        "d_vec": np.ascontiguousarray(inp["L0_attn_D"][di0:di1, None]),
        "w_out_T": np.ascontiguousarray(
            inp["L0_attn_out_proj_w"].T[:, j * DMH:(j + 1) * DMH]),
        "gu_packed": np.ascontiguousarray(np.concatenate([
            inp["L0_ffn_gate_up_packed"][j * IH:(j + 1) * IH],
            inp["L0_ffn_gate_up_packed"][5632 + j * IH:5632 + (j + 1) * IH]], axis=0)),
        "gu_scales_T": np.ascontiguousarray(np.concatenate([
            inp["L0_ffn_gate_up_scales"][:, j * IH:(j + 1) * IH],
            inp["L0_ffn_gate_up_scales"][:, 5632 + j * IH:5632 + (j + 1) * IH]],
            axis=1).T.copy()).view(np.uint16),
        "dn_packed": np.ascontiguousarray(
            inp["L0_ffn_down_packed"][j * DMH:(j + 1) * DMH]),
        "dn_scales_T": np.ascontiguousarray(
            inp["L0_ffn_down_scales"][:, j * DMH:(j + 1) * DMH].T).view(np.uint16),
        "j_in": np.array([[j]], dtype=np.uint32),
        "ident": _IDENT,
        "constv": np.broadcast_to(
            np.array([LN_EPS, RMS_EPS, 1.0], np.float32), (128, 3)).copy(),
    }


_prog_cache = {}


def kernel(**inputs):
    t_tokens = inputs["L0_attn_hidden_states"].shape[1]
    if t_tokens not in _prog_cache:
        _prog_cache[t_tokens] = build_program(t_tokens=t_tokens)
    nc = _prog_cache[t_tokens]
    nonce = np.random.randint(1, 2**31 - 1, size=(1, 4)).astype(np.uint32)
    in_maps = []
    for core in range(8):
        m = _core_inputs(core, inputs)
        m["nonce_in"] = nonce
        in_maps.append(m)
    res = run_bass_kernel_spmd(nc, in_maps, core_ids=list(range(8)))
    out = np.empty((4, t_tokens, DM), dtype=np.float32)
    for core in range(8):
        b, j = core // 2, core & 1
        out[b][:, j * DMH:(j + 1) * DMH] = res.results[core]["out_half"]
    return out
